# version 07
# baseline (speedup 1.0000x reference)
import math
import os
import numpy as np
import concourse.bass as bass
import concourse.mybir as mybir
from concourse.bass_utils import run_bass_kernel_spmd

F32 = mybir.dt.float32
BF16 = mybir.dt.bfloat16
I32 = mybir.dt.int32
AF = mybir.ActivationFunctionType
ALU = mybir.AluOpType
AX = mybir.AxisListType

SEQ, D, NB, IF = 8192, 1024, 4, 2816
NCORE = 8
TOK, NCH, QW = 4096, 16, 256
NIC, NDC, NT512 = IF // 128, D // 128, 4096 // 512

INV2PI = float(np.float64(1.0) / (2.0 * np.pi))
TWOPI = float(2.0 * np.pi)
DDEC = 0.03
RFF_SCALE = float(np.sqrt(2.0 / 64.0) * np.sqrt(np.exp(-0.5)))
CMAG = [float(np.sqrt(abs(((-1.0) ** m) / math.factorial(m)) + 1e-10) * RFF_SCALE)
        for m in range(5)]
LN_EPS = 1e-5
BIGPOS = 3.0e38
BIGNEG = -1.0e30



class Q:
    def __init__(self, name):
        self.name = name
        self.cmds = []
        self.hwm = {}

    def replay(self, eng):
        for c in self.cmds:
            if c[0] == "w":
                eng.wait_ge(c[1], c[2])
            else:
                c[1](eng).then_inc(c[2], c[3])


class Ctr:
    def __init__(self, sem, step):
        self.sem, self.step = sem, step
        self.count = 0


class Env:

    CHAINED = ("v", "s", "g")

    def __init__(self, nc, qnames, ctr_spec, sem):
        self.nc = nc
        self.q = {k: Q(k) for k in qnames}
        self.c = {k: Ctr(sem("c_" + k), st) for k, st in ctr_spec}

    def op(self, qname, cname, emit, deps=()):
        ctr = self.c[cname]
        q = self.q[qname]
        alldeps = list(deps)
        if cname in self.CHAINED and ctr.count > 0:
            alldeps.append((ctr.sem, ctr.count))
        for semh, val in alldeps:
            if semh is None or val <= 0:
                continue
            key = id(semh)
            if q.hwm.get(key, -1) >= val:
                continue
            q.cmds.append(("w", semh, val))
            q.hwm[key] = val
        ctr.count += ctr.step
        q.cmds.append(("i", emit, ctr.sem, ctr.step))
        return (ctr.sem, ctr.count)

    def v(self, emit, deps=()):
        return self.op("v", "v", emit, deps)

    def s(self, emit, deps=()):
        return self.op("s", "s", emit, deps)

    def t(self, emit, deps=()):
        return self.op("t", "t", emit, deps)

    def g(self, emit, deps=()):
        return self.op("g", "g", emit, deps)

    def tok(self, cname):
        ctr = self.c[cname]
        return (ctr.sem, ctr.count)


ENG_CTRS = [("v", 1), ("s", 1), ("t", 1), ("g", 1)]


def finish(env, blk, extra_waits=()):
    q = env.q

    @blk.sync
    def _(eng):
        q["sy"].replay(eng)
        for semh, val in extra_waits:
            if val > 0:
                eng.wait_ge(semh, val)

    @blk.vector
    def _(eng):
        q["v"].replay(eng)

    @blk.scalar
    def _(eng):
        q["s"].replay(eng)

    @blk.tensor
    def _(eng):
        q["t"].replay(eng)

    @blk.gpsimd
    def _(eng):
        q["g"].replay(eng)


class War:

    def __init__(self):
        self.d = {}

    def get(self, k):
        return self.d.get(k, [])

    def set(self, k, toks):
        self.d[k] = list(toks)

    def add(self, k, tok):
        self.d.setdefault(k, []).append(tok)



def build_l1():
    nc = bass.Bass("TRN2", target_bir_lowering=False)
    din = lambda n, s, d=F32: nc.dram_tensor(n, s, d, kind="ExternalInput").ap()
    dout = lambda n, s, d=F32: nc.dram_tensor(n, s, d, kind="ExternalOutput").ap()

    x_own = din("x_own", [TOK, D])
    xT_own = din("xT_own", [128, NDC * TOK])
    wv_t = din("wv_t", [128, NDC * D])
    wout_t = din("wout_t", [128, NDC * D])
    wsh_t = din("wsh_t", [128, NDC * 64])
    wch_t = din("wch_t", [128, NDC])
    bsh_row = din("bsh_row", [1, 64])
    bch_col = din("bch_col", [128, 1])
    bv_cols = din("bv_cols", [128, NDC])
    omega_aug = din("omega_aug", [65, 32])
    wc0 = din("wc0", [128, 64])
    wc1 = din("wc1", [128, 64])
    wc2 = din("wc2", [64, 64])
    ident = din("ident", [128, 128])
    ones_r = din("ones_r", [1, 128])
    ones_c = din("ones_c", [64, 1])

    xnT_sp = dout("xnT_sp", [128, NDC * TOK])
    v_own = dout("v_own", [TOK, D], BF16)
    phiT_o = dout("phiT", [64, TOK], BF16)
    mx_o = dout("mx", [1, 2])

    from contextlib import ExitStack
    with ExitStack() as ctx:
        sb = lambda n, s, d=F32: ctx.enter_context(nc.sbuf_tensor(n, s, d))
        psa = lambda n, s: ctx.enter_context(nc.psum_tensor(n, s, F32))
        sem = lambda n: ctx.enter_context(nc.semaphore(n))

        wv_sb = sb("wv_sb", [128, NDC * D], BF16)
        wo_sb = sb("wo_sb", [128, NDC * D], BF16)
        wsh_sb = sb("wsh_sb", [128, NDC * 64], BF16)
        wch_sb = sb("wch_sb", [128, NDC], BF16)
        bsh_sb = sb("bsh_sb", [1, 64])
        bch_sb = sb("bch_sb", [128, 1])
        bv_sb = sb("bv_sb", [128, NDC])
        om_sb = sb("om_sb", [65, 32])
        wcs = [sb("wc0_sb", [128, 64]), sb("wc1_sb", [128, 64]), sb("wc2_sb", [64, 64])]
        id_sb = sb("id_sb", [128, 128])
        on_sb = sb("on_sb", [1, 128])
        onc_sb = sb("onc_sb", [64, 1])

        xbuf = [sb(f"xbuf{p}", [128, D]) for p in range(2)]
        xtb = [sb(f"xtb{p}", [128, NDC * 512]) for p in range(2)]
        xnT = [sb(f"xnT{p}", [128, NDC * 512]) for p in range(2)]
        xnTb = [sb(f"xnTb{p}", [128, NDC * 512], BF16) for p in range(2)]
        vT = [sb(f"vT{p}", [128, NDC * 512], BF16) for p in range(2)]
        vpb = [sb(f"vpb{p}", [128, D], BF16) for p in range(2)]

        sq_scr = sb("sq_scr", [128, D])
        stats = sb("stats", [128, 8])
        mr = sb("mr", [128, 2])
        mean_row = sb("mean_row", [1, 512])
        rstd_row = sb("rstd_row", [1, 512])
        m_b = sb("m_b", [128, 512])
        r_b = sb("r_b", [128, 512])
        gmax_col = sb("gmax_col", [128, 1])
        q_col = sb("q_col", [128, 1])
        qpow = sb("qpow", [128, 1])
        phic = sb("phic", [128, 5])
        shell = sb("shell", [128, 64])
        shn = sb("shn", [128, 65])
        shnT = sb("shnT", [65, 128])
        prt = sb("prt", [128, 64])
        prk = sb("prk", [128, 64], I32)
        prf = sb("prf", [128, 64])
        phis = sb("phis", [128, 64])
        tpb = sb("tpb", [128, 320])
        tpT = [sb("tpT0", [128, 128]), sb("tpT1", [128, 128]), sb("tpT2", [64, 128])]
        phiT_buf = sb("phiT_buf", [64, TOK], BF16)
        phisq = sb("phisq", [64, 128])
        nn_acc = sb("nn_acc", [1, 1])
        fin_row = sb("fin_row", [1, 128])
        mx_sb = sb("mx_sb", [1, 2])

        ps_mr = psa("ps_mr", [1, 256])
        ps_m = psa("ps_m", [128, 512])
        ps_r = psa("ps_r", [128, 512])
        ps_v = psa("ps_v", [128, 512])
        ps_vp = psa("ps_vp", [128, 1024])
        PSF = psa("PSF", [128, 512])
        PSG = psa("PSG", [128, 512])
        ps_q = PSF[:, 0:1]
        ps_sh = PSF[:, 32:96]
        ps_st = PSF[0:65, 128:256]
        ps_pj = PSF[:, 288:320]
        ps_nn = PSF[0:1, 384:512]
        ps_tp = [PSG[:, 0:128], PSG[:, 128:256], PSG[0:64, 256:384]]
        ps_phi = PSG[0:64, 384:512]

        E = Env(nc, ("sy", "v", "s", "t", "g"),
                ENG_CTRS + [("cl", 16), ("clg", 16), ("x0", 16), ("x1", 16), ("xt0", 16), ("xt1", 16),
                            ("sp0", 16), ("sp1", 16), ("vo0", 16), ("vo1", 16), ("go", 16)],
                sem)
        W = War()

        def cload(dst, src):
            return E.op("sy", "cl", lambda e, d=dst, s_=src: e.dma_start(out=d, in_=s_))

        def gload(dst, src):
            return E.op("g", "clg", lambda e, d=dst, s_=src: e.dma_start(out=d, in_=s_))

        for dst, src in [(bsh_sb[:], bsh_row), (bch_sb[:], bch_col),
                         (bv_sb[:], bv_cols), (om_sb[:], omega_aug), (wcs[0][:], wc0),
                         (wcs[1][:], wc1), (wcs[2][:], wc2), (id_sb[:], ident),
                         (on_sb[:], ones_r), (onc_sb[:], ones_c)]:
            cload(dst, src)
        CL = E.tok("cl")
        for dst, src in [(wv_sb[:], wv_t), (wo_sb[:], wout_t),
                         (wsh_sb[:], wsh_t), (wch_sb[:], wch_t)]:
            gload(dst, src)
        CLG = E.tok("clg")
        t_wv = t_wo = t_wsh = t_wch = CLG
        E.v(lambda e: e.memset(gmax_col[:], 0.0))
        E.v(lambda e: e.memset(nn_acc[:], 0.0))
        E.v(lambda e: e.memset(shn[:, 64:65], 1.0))
        E.v(lambda e: e.memset(phic[:, 0:1], CMAG[0]))

        x_free = [[], []]
        xt_free = [[], []]
        sp_prev = [None, None]
        vo_prev = [None, None]

        for tc in range(NT512):
            p = tc % 2
            xt_ld = E.op("sy", f"xt{p}", lambda e, tc=tc, p=p: e.dma_start(
                out=xtb[p][:].rearrange("q (c n) -> q c n", n=512),
                in_=xT_own.rearrange("q (c n) -> q c n", n=TOK)[:, :, 512 * tc:512 * (tc + 1)]),
                deps=xt_free[p])
            xt_free[p] = []

            mr_toks = []
            for tt in range(4):
                ti = tc * 4 + tt
                pp = ti % 2
                x_ld = E.op("sy", f"x{pp}", lambda e, ti=ti, pp=pp: e.dma_start(
                    out=xbuf[pp][:], in_=x_own[128 * ti:128 * (ti + 1), :]),
                    deps=x_free[pp])
                t1 = E.v(lambda e, pp=pp: e.reduce_sum(stats[:, 0:1], xbuf[pp][:], axis=AX.X),
                         deps=[x_ld])
                t2 = E.s(lambda e, pp=pp: e.activation(sq_scr[:], xbuf[pp][:], AF.Square,
                                                       accum_out=stats[:, 1:2]),
                         deps=[x_ld, CL] + W.get("sq_scr") + W.get("stats1"))
                x_free[pp] = [t1, t2]
                E.v(lambda e: e.tensor_scalar_mul(mr[:, 0:1], stats[:, 0:1], 1.0 / D),
                    deps=W.get("mr"))
                E.v(lambda e: e.tensor_mul(stats[:, 2:3], mr[:, 0:1], mr[:, 0:1]))
                E.v(lambda e: e.scalar_tensor_tensor(stats[:, 3:4], stats[:, 1:2], 1.0 / D,
                                                     stats[:, 2:3], ALU.mult, ALU.subtract),
                    deps=[t2])
                W.set("stats1", [E.tok("v")])
                E.v(lambda e: e.tensor_scalar_add(stats[:, 3:4], stats[:, 3:4], LN_EPS))
                t7 = E.v(lambda e: e.reciprocal(stats[:, 2:3], stats[:, 3:4]))
                t8 = E.s(lambda e: e.activation(mr[:, 1:2], stats[:, 2:3], AF.Sqrt), deps=[t7])
                t9a = E.t(lambda e: e.transpose(ps_mr[:, 0:128], mr[:, 0:1], id_sb[:]),
                          deps=[t8, E.tok("v"), CL] + W.get("ps_mr"))
                t9 = E.t(lambda e: e.transpose(ps_mr[:, 128:256], mr[:, 1:2], id_sb[:]))
                W.set("mr", [t9])
                t10a = E.v(lambda e, tt=tt: e.tensor_copy(mean_row[0:1, 128 * tt:128 * (tt + 1)], ps_mr[0:1, 0:128]),
                           deps=[t9a, t9] + W.get("mr_rows"))
                t10 = E.v(lambda e, tt=tt: e.tensor_copy(rstd_row[0:1, 128 * tt:128 * (tt + 1)], ps_mr[0:1, 128:256]))
                W.set("ps_mr", [t10])
                mr_toks.append(t10)

            tb0 = E.t(lambda e: e.matmul(ps_m[:], on_sb[:], mean_row[:], start=True, stop=True),
                      deps=mr_toks + [CL] + W.get("ps_m"))
            tb1 = E.t(lambda e: e.matmul(ps_r[:], on_sb[:], rstd_row[:], start=True, stop=True),
                      deps=W.get("ps_r"))
            W.set("mr_rows", [tb0, tb1])
            te0 = E.s(lambda e: e.copy(m_b[:], ps_m[:]), deps=[tb0] + W.get("m_b"))
            te1 = E.s(lambda e: e.copy(r_b[:], ps_r[:]), deps=[tb1] + W.get("r_b"))
            W.set("ps_m", [te0])
            W.set("ps_r", [te1])

            xn_toks = []
            xnb_toks = []
            for dc in range(NDC):
                sl = slice(512 * dc, 512 * (dc + 1))
                E.v(lambda e, p=p, sl=sl: e.tensor_sub(xnT[p][:, sl], xtb[p][:, sl], m_b[:]),
                    deps=[xt_ld, te0] + W.get(f"xnT{p}"))
                tb_ = E.v(lambda e, p=p, sl=sl: e.tensor_mul(xnT[p][:, sl], xnT[p][:, sl], r_b[:]),
                          deps=[te1])
                E.v(lambda e, p=p, sl=sl: e.reduce_max(stats[:, 4:5], xnT[p][:, sl], axis=AX.X,
                                                       apply_absolute_value=True))
                E.v(lambda e: e.tensor_max(gmax_col[:], gmax_col[:], stats[:, 4:5]))
                tc_ = E.s(lambda e, p=p, sl=sl: e.copy(xnTb[p][:, sl], xnT[p][:, sl]),
                          deps=[tb_] + W.get(f"xnTb{p}"))
                xn_toks.append(tb_)
                xnb_toks.append(tc_)
            W.set("m_b", [E.tok("v")])
            W.set("r_b", [E.tok("v")])
            xt_free[p] = [E.tok("v")]

            sp_tok = E.op("sy", f"sp{p}", lambda e, p=p, tc=tc: e.dma_start(
                out=xnT_sp[:, NDC * 512 * tc:NDC * 512 * (tc + 1)],
                in_=xnT[p][:]),
                deps=xn_toks + ([sp_prev[p]] if sp_prev[p] else []))
            sp_prev[p] = sp_tok
            W.set(f"xnT{p}", [sp_tok])

            vT_toks = []
            for co in range(NDC):
                mm = None
                for ci in range(NDC):
                    dep = ([t_wv] + xnb_toks + W.get("ps_v")) if ci == 0 else []
                    mm = E.t(lambda e, p=p, ci=ci, co=co: e.matmul(
                        ps_v[:], wv_sb[:, D * ci + 128 * co:D * ci + 128 * (co + 1)],
                        xnTb[p][:, 512 * ci:512 * (ci + 1)],
                        start=(ci == 0), stop=(ci == NDC - 1)), deps=dep)
                tv = E.s(lambda e, p=p, co=co: e.activation(
                    vT[p][:, 512 * co:512 * (co + 1)], ps_v[:], AF.Identity,
                    bias=bv_sb[:, co:co + 1]), deps=[mm] + W.get(f"vT{p}"))
                W.set("ps_v", [tv])
                vT_toks.append(tv)

            vp_last = []
            for tt in range(4):
                pp = tt % 2
                mm = None
                for ci in range(NDC):
                    for half in range(2):
                        dep = []
                        if ci == 0 and half == 0:
                            dep = [t_wo] + vT_toks + W.get("ps_vp")
                        mm = E.t(lambda e, p=p, ci=ci, tt=tt, half=half: e.matmul(
                            ps_vp[:, 512 * half:512 * (half + 1)],
                            vT[p][:, 512 * ci + 128 * tt:512 * ci + 128 * (tt + 1)],
                            wo_sb[:, D * ci + 512 * half:D * ci + 512 * (half + 1)],
                            start=(ci == 0), stop=(ci == NDC - 1)), deps=dep)
                tvp = E.v(lambda e, pp=pp: e.tensor_copy(vpb[pp][:], ps_vp[:]),
                          deps=[mm] + ([vo_prev[pp]] if vo_prev[pp] else []))
                W.set("ps_vp", [tvp])
                vo_prev[pp] = E.op("sy", f"vo{pp}", lambda e, pp=pp, tc=tc, tt=tt: e.dma_start(
                    out=v_own[128 * (4 * tc + tt):128 * (4 * tc + tt + 1), :], in_=vpb[pp][:]),
                    deps=[tvp])
                vp_last = [tvp]
            W.set(f"vT{p}", vp_last)

            for tt in range(4):
                ti = tc * 4 + tt
                mm = None
                for ci in range(NDC):
                    dep = ([t_wch] + xnb_toks + W.get("PSF")) if ci == 0 else []
                    mm = E.t(lambda e, p=p, ci=ci, tt=tt: e.matmul(
                        ps_q, xnTb[p][:, 512 * ci + 128 * tt:512 * ci + 128 * (tt + 1)],
                        wch_sb[:, ci:ci + 1],
                        start=(ci == 0), stop=(ci == NDC - 1)), deps=dep)
                tq = E.s(lambda e: e.activation(q_col[:], ps_q, AF.Tanh, bias=bch_sb[:, 0:1]),
                         deps=[mm] + W.get("q_col"))
                W.set("PSF", [tq])
                E.v(lambda e: e.tensor_mul(qpow[:], q_col[:], q_col[:]), deps=[tq] + W.get("phic"))
                E.v(lambda e: e.tensor_scalar_mul(phic[:, 1:2], q_col[:], CMAG[1]))
                E.v(lambda e: e.tensor_scalar_mul(phic[:, 2:3], qpow[:], CMAG[2]))
                E.v(lambda e: e.tensor_mul(qpow[:], qpow[:], q_col[:]))
                E.v(lambda e: e.tensor_scalar_mul(phic[:, 3:4], qpow[:], CMAG[3]))
                E.v(lambda e: e.tensor_mul(qpow[:], qpow[:], q_col[:]))
                tpc = E.v(lambda e: e.tensor_scalar_mul(phic[:, 4:5], qpow[:], CMAG[4]))
                W.set("q_col", [tpc])
                mm = None
                for ci in range(NDC):
                    dep = ([t_wsh] + xnb_toks + W.get("PSF")) if ci == 0 else []
                    mm = E.t(lambda e, p=p, ci=ci, tt=tt: e.matmul(
                        ps_sh, xnTb[p][:, 512 * ci + 128 * tt:512 * ci + 128 * (tt + 1)],
                        wsh_sb[:, 64 * ci:64 * (ci + 1)],
                        start=(ci == 0), stop=False), deps=dep)
                mm = E.t(lambda e: e.matmul(ps_sh, on_sb[:], bsh_sb[:], start=False, stop=True))
                tsh = E.v(lambda e: e.tensor_copy(shell[:], ps_sh), deps=[mm] + W.get("shell"))
                W.set("PSF", [tsh])
                tss = E.s(lambda e: e.activation(sq_scr[:, 0:64], shell[:], AF.Square,
                                                 accum_out=stats[:, 5:6]),
                          deps=[tsh] + W.get("sq_scr") + W.get("stats5"))
                tsq = E.s(lambda e: e.activation(stats[:, 6:7], stats[:, 5:6], AF.Sqrt))
                W.set("stats5", [tsq])
                W.set("sq_scr", [tss])
                E.v(lambda e: e.tensor_scalar_max(stats[:, 6:7], stats[:, 6:7], 1e-12), deps=[tsq])
                trn = E.v(lambda e: e.reciprocal(stats[:, 6:7], stats[:, 6:7]))
                tshn = E.v(lambda e: e.tensor_scalar_mul(shn[:, 0:64], shell[:], stats[:, 6:7]),
                           deps=W.get("shn"))
                W.set("shell", [tshn])
                tst = E.t(lambda e: e.transpose(ps_st, shn[:], id_sb[:]),
                          deps=[tshn] + W.get("PSF"))
                W.set("shn", [tst])
                tstc = E.v(lambda e: e.tensor_copy(shnT[:], ps_st), deps=[tst] + W.get("shnT"))
                W.set("PSF", [tstc])
                tpj = E.t(lambda e: e.matmul(ps_pj, shnT[:], om_sb[:], start=True, stop=True),
                          deps=[tstc, CL] + W.get("PSF"))
                W.set("shnT", [tpj])
                ta1 = E.v(lambda e: e.tensor_scalar(prt[:, 0:32], ps_pj, INV2PI, 64.75,
                                                    ALU.mult, ALU.add),
                          deps=[tpj] + W.get("prt"))
                ta2 = E.v(lambda e: e.tensor_scalar(prt[:, 32:64], ps_pj, INV2PI, 64.5,
                                                    ALU.mult, ALU.add))
                W.set("PSF", [ta2])
                E.v(lambda e: e.tensor_copy(prk[:], prt[:]))
                E.v(lambda e: e.scalar_tensor_tensor(prf[:], prt[:], -0.5, prk[:],
                                                     ALU.add, ALU.subtract))
                E.v(lambda e: e.tensor_single_scalar(prt[:], prf[:], -0.5, ALU.is_lt))
                tf = E.v(lambda e: e.tensor_add(prf[:], prf[:], prt[:]))
                W.set("prt", [tf])
                tphis = E.s(lambda e: e.activation(phis[:], prf[:], AF.Sin, scale=TWOPI),
                            deps=[tf] + W.get("phis"))
                tp_toks = []
                for m in range(5):
                    dep = [tphis, tpc] + W.get("tpb") if m == 0 else []
                    tp_toks.append(E.v(lambda e, m=m: e.tensor_scalar_mul(
                        tpb[:, 64 * m:64 * (m + 1)], phis[:], phic[:, m:m + 1]), deps=dep))
                W.set("phis", [tp_toks[-1]])
                W.set("phic", [tp_toks[-1]])
                ph_toks = []
                for k3 in range(3):
                    insl = tpb[:, 128 * k3:128 * (k3 + 1)] if k3 < 2 else tpb[:, 256:320]
                    ttr = E.t(lambda e, k3=k3, insl=insl: e.transpose(ps_tp[k3], insl, id_sb[:]),
                              deps=tp_toks + W.get("PSG"))
                    tcp = E.v(lambda e, k3=k3: e.tensor_copy(tpT[k3][:], ps_tp[k3]),
                              deps=[ttr] + W.get(f"tpT{k3}"))
                    W.set("PSG", [tcp])
                    ph_toks.append(tcp)
                W.set("tpb", [ph_toks[-1]])
                mm = None
                for k3 in range(3):
                    dep = (ph_toks + [CL] + W.get("PSG")) if k3 == 0 else []
                    mm = E.t(lambda e, k3=k3: e.matmul(ps_phi, wcs[k3][:], tpT[k3][:],
                                                       start=(k3 == 0), stop=(k3 == 2)), deps=dep)
                for k3 in range(3):
                    W.set(f"tpT{k3}", [mm])
                tphi = E.s(lambda e, ti=ti: e.copy(
                    phiT_buf[:, 128 * ti:128 * (ti + 1)], ps_phi), deps=[mm])
                tsq2 = E.s(lambda e: e.activation(phisq[:], ps_phi, AF.Square),
                           deps=W.get("phisq"))
                W.set("PSG", [tphi, tsq2])
                tnn = E.t(lambda e: e.matmul(ps_nn, onc_sb[:], phisq[:], start=True, stop=True),
                          deps=[tsq2, CL] + W.get("PSF"))
                W.set("phisq", [tnn])
                tnm = E.v(lambda e: e.reduce_max(fin_row[0:1, 0:1], ps_nn, axis=AX.X),
                          deps=[tnn] + W.get("fin_row"))
                W.set("PSF", [tnm])
                E.v(lambda e: e.tensor_max(nn_acc[:], nn_acc[:], fin_row[0:1, 0:1]))

        tgf = E.t(lambda e: e.transpose(ps_mr[0:1, 0:128], gmax_col[:], id_sb[:]),
                  deps=[E.tok("v"), CL] + W.get("ps_mr"))
        E.v(lambda e: e.tensor_copy(fin_row[:], ps_mr[0:1, 0:128]), deps=[tgf] + W.get("fin_row"))
        E.v(lambda e: e.reduce_max(mx_sb[0:1, 0:1], fin_row[:], axis=AX.X))
        tmxf = E.v(lambda e: e.tensor_copy(mx_sb[0:1, 1:2], nn_acc[:]))
        E.op("sy", "go", lambda e: e.dma_start(out=phiT_o, in_=phiT_buf[:]),
             deps=[E.tok("s")])
        E.op("sy", "go", lambda e: e.dma_start(out=mx_o, in_=mx_sb[:]),
             deps=[tmxf])

        blk = ctx.enter_context(nc.Block())
        waits = [E.tok("sp0"), E.tok("sp1"), E.tok("vo0"), E.tok("vo1"), E.tok("go")]
        finish(E, blk, extra_waits=waits)
    return nc



ROUND_SAFE = bool(os.environ.get("BASS_SIM_FP64"))


def emit_round_q(E, W, src_ap, inv_col, out_bf, rbuf, kbuf, fbuf, cbuf, deps, wkey):
    if not ROUND_SAFE:
        t1 = E.v(lambda e: e.tensor_scalar_mul(rbuf, src_ap, inv_col),
                 deps=list(deps) + W.get(wkey))
        E.v(lambda e: e.tensor_copy(kbuf, rbuf))
        t3 = E.v(lambda e: e.tensor_scalar(out_bf, kbuf, -128.0, 127.0, ALU.max, ALU.min))
        W.set(wkey, [t3])
        return t3
    t1 = E.v(lambda e: e.tensor_scalar(rbuf, src_ap, inv_col, 512.5, ALU.mult, ALU.add),
             deps=list(deps) + W.get(wkey))
    tk = E.s(lambda e: e.copy(kbuf, rbuf), deps=[t1] + W.get(wkey + "k"))
    t2 = E.v(lambda e: e.scalar_tensor_tensor(fbuf, rbuf, -0.5, kbuf, ALU.add, ALU.subtract),
             deps=[tk])
    E.v(lambda e: e.tensor_single_scalar(cbuf, fbuf, -0.5, ALU.is_lt))
    E.v(lambda e: e.tensor_add(fbuf, fbuf, cbuf))
    E.v(lambda e: e.scalar_tensor_tensor(cbuf, rbuf, -512.5, fbuf, ALU.add, ALU.subtract))
    t3 = E.v(lambda e: e.tensor_scalar(out_bf, cbuf, -128.0, 127.0, ALU.max, ALU.min))
    W.set(wkey, [t3])
    W.set(wkey + "k", [t2])
    return t3



def build_l2():
    nc = bass.Bass("TRN2", target_bir_lowering=False, dynamic_dma_scratch_size=2048)
    din = lambda n, s, d=F32: nc.dram_tensor(n, s, d, kind="ExternalInput").ap()
    dout = lambda n, s, d=F32: nc.dram_tensor(n, s, d, kind="ExternalOutput").ap()

    xnT_sp = din("xnT_sp", [128, NDC * TOK])
    vg = din("vg", [SEQ, D], BF16)
    phi_g = din("phi_g", [64, SEQ], BF16)
    phi_q = din("phi_q", [64, TOK], BF16)
    gx8 = din("gx8", [1, 8])
    nn8 = din("nn8", [1, 8])
    wg = din("wg", [IF, D])
    wu = din("wu", [IF, D])
    wgt = din("wgt", [128, NIC * D])
    wut = din("wut", [128, NIC * D])
    clip_x = din("clip_x", [1, 1])
    log_temp = din("log_temp", [1, 1])
    iota512 = din("iota512", [1, 512])
    qb16 = din("qb16", [1, NCH])
    maskS = din("maskS", [128, 4 * QW], BF16)
    b_out_bc = din("b_out_bc", [128, D])
    ident = din("ident", [128, 128])
    identb = din("identb", [128, 128], BF16)
    ones_r = din("ones_r", [1, 128])
    ones_kc = din("ones_kc", [128, 1], BF16)

    h_sp = dout("h_sp", [128, NIC * TOK])
    hx_o = dout("hx", [1, 1])
    oat_o = dout("oat", [TOK, D])

    NVC = 4

    from contextlib import ExitStack
    with ExitStack() as ctx:
        sb = lambda n, s, d=F32: ctx.enter_context(nc.sbuf_tensor(n, s, d))
        psa = lambda n, s: ctx.enter_context(nc.psum_tensor(n, s, F32))
        sem = lambda n: ctx.enter_context(nc.semaphore(n))

        gx_sb = sb("gx_sb", [1, 8])
        nn_sb = sb("nn_sb", [1, 8])
        cx_sb = sb("cx_sb", [1, 1])
        lt_sb = sb("lt_sb", [1, 1])
        io512 = sb("io512", [1, 512])
        qb_sb = sb("qb_sb", [1, NCH])
        mk_sb = sb("mk_sb", [128, 4 * QW], BF16)
        bo_sb = sb("bo_sb", [128, D])
        id_sb = sb("id_sb", [128, 128])
        on_sb = sb("on_sb", [1, 128])
        onk_sb = sb("onk_sb", [128, 1], BF16)
        vgc = sb("vgc", [128, NVC * D], BF16)

        srow = sb("srow", [1, 8])
        invx_col = sb("invx_col", [128, 1])
        sx_col = sb("sx_col", [128, 1])
        negc_col = sb("negc_col", [128, 1])
        invt = sb("invt", [1, 1])
        arow = sb("arow", [1, 256])
        abc = sb("abc", [64, 256])
        brow = sb("brow", [1, 512])
        segb = sb("segb", [1, 1])

        nx = sb("nx", [128, NDC * TOK], BF16)
        xns = sb("xns0", [128, NDC * 512])
        wob = [sb(f"wob{p}", [128, D]) for p in range(2)]
        swcol = sb("swcol", [128, 8])
        swcol2 = sb("swcol2", [128, 1])
        sgx = sb("sgx", [128, 2 * NIC])
        wtb = [sb(f"wtb{p}", [128, D]) for p in range(6)]
        invrow = [sb(f"invrow{p}", [1, D]) for p in range(2)]
        invbc = [sb(f"invbc{p}", [128, D]) for p in range(2)]
        ngt = [sb(f"ngt{p}", [128, D], BF16) for p in range(2)]
        nut = [sb(f"nut{p}", [128, D], BF16) for p in range(2)]
        hg = sb("hg", [128, 512])
        hu = sb("hu", [128, 512])
        hb = [sb(f"hb{p}", [128, 512]) for p in range(2)]
        hmax_col = sb("hmax_col", [128, 1])
        hrow = sb("hrow", [1, 128])
        hx_sb = sb("hx_sb", [1, 1])
        rb4 = sb("rb4", [128, D])
        kb4 = sb("kb4", [128, D], I32)
        fb4 = sb("fb4", [128, D if ROUND_SAFE else 1])
        cb4 = sb("cb4", [128, D if ROUND_SAFE else 1])

        ktil = [sb(f"ktil{p}", [64, SEQ], BF16) for p in range(2)]
        qt = [sb(f"qt{p}", [64, 256], BF16) for p in range(2)]
        phgs = sb("phgs", [64, 512], BF16)
        phqs = sb("phqs", [64, 256], BF16)
        vgb = [sb(f"vgb{p}", [128, D], BF16) for p in range(2)]
        atile = [sb(f"at{p}", [128, 256], BF16) for p in range(2)]
        rd2 = sb("rd2", [128, 4])
        rd = sb("rd", [128, 2])
        oatb = [sb(f"oatb{p}", [128, D]) for p in range(2)]

        PB1 = psa("PB1", [128, 512])
        PB2 = psa("PB2", [128, 512])
        PS = [psa("PSA", [128, 512]), psa("PSB", [128, 512])]
        PO1 = psa("PO1", [128, 1024])
        PO2 = psa("PO2", [128, 1024])

        E = Env(nc, ("sy", "v", "s", "t", "g"),
                ENG_CTRS + [("cl", 16), ("xs0", 16),
                            ("wo0", 16), ("wo1", 16),
                            ("wt0", 16), ("wt1", 16), ("wt2", 16),
                            ("wt3", 16), ("wt4", 16), ("wt5", 16),
                            ("hs0", 16), ("hs1", 16), ("ph0", 16), ("ph1", 16),
                            ("vg0", 16), ("vg1", 16), ("oa0", 16), ("oa1", 16),
                            ("go", 16)],
                sem)
        W = War()

        def cload(dst, src):
            return E.op("sy", "cl", lambda e, d=dst, s_=src: e.dma_start(out=d, in_=s_))

        for dst, src in [(gx_sb[:], gx8), (nn_sb[:], nn8), (cx_sb[:], clip_x),
                         (lt_sb[:], log_temp), (io512[:], iota512),
                         (qb_sb[:], qb16), (mk_sb[:], maskS), (bo_sb[:], b_out_bc),
                         (id_sb[:], ident), (on_sb[:], ones_r), (onk_sb[:], ones_kc)]:
            cload(dst, src)
        for kt in range(NVC):
            cload(vgc[:, D * kt:D * (kt + 1)], vg[128 * kt:128 * (kt + 1), :])
        CL = E.tok("cl")

        t1 = E.v(lambda e: e.reduce_max(srow[0:1, 0:1], gx_sb[:], axis=AX.X), deps=[CL])
        E.v(lambda e: e.reduce_max(srow[0:1, 1:2], nn_sb[:], axis=AX.X))
        ts1 = E.s(lambda e: e.activation(srow[0:1, 2:3], cx_sb[:], AF.Sigmoid), deps=[CL])
        E.v(lambda e: e.tensor_mul(srow[0:1, 3:4], srow[0:1, 2:3], srow[0:1, 0:1]), deps=[ts1])
        E.v(lambda e: e.reciprocal(srow[0:1, 4:5], srow[0:1, 3:4]))
        tinvx = E.v(lambda e: e.tensor_scalar_mul(srow[0:1, 4:5], srow[0:1, 4:5], 127.0))
        tsx = E.v(lambda e: e.tensor_scalar_mul(srow[0:1, 3:4], srow[0:1, 3:4], 1.0 / 127.0))
        tsp0 = E.s(lambda e: e.activation(srow[0:1, 5:6], lt_sb[:], AF.Exp), deps=[CL])
        tsp1 = E.v(lambda e: e.tensor_scalar_add(srow[0:1, 5:6], srow[0:1, 5:6], 1.0), deps=[tsp0])
        tsp = E.s(lambda e: e.activation(srow[0:1, 5:6], srow[0:1, 5:6], AF.Ln), deps=[tsp1])
        E.v(lambda e: e.tensor_scalar_add(srow[0:1, 5:6], srow[0:1, 5:6], 0.1), deps=[tsp])
        tit = E.v(lambda e: e.reciprocal(invt[:], srow[0:1, 5:6]))
        E.v(lambda e: e.tensor_mul(srow[0:1, 6:7], srow[0:1, 1:2], invt[:]))
        tnegc = E.v(lambda e: e.tensor_scalar(srow[0:1, 6:7], srow[0:1, 6:7], 1.02, -1.0,
                                              ALU.mult, ALU.mult))
        def bcast_col(dst, src_ap, dep):
            tm = E.t(lambda e, d=dst, s_=src_ap: e.matmul(PB1[:, 0:1], on_sb[:], s_,
                                                          start=True, stop=True),
                     deps=[dep, CL] + W.get("PB1"))
            tc_ = E.v(lambda e, d=dst: e.tensor_copy(d, PB1[:, 0:1]), deps=[tm])
            W.set("PB1", [tc_])
            return tc_
        t_invx = bcast_col(invx_col[:], srow[0:1, 4:5], tinvx)
        t_sx = bcast_col(sx_col[:], srow[0:1, 3:4], tsx)
        t_negc = bcast_col(negc_col[:], srow[0:1, 6:7], tnegc)
        tar = E.s(lambda e: e.activation(arow[:], io512[0:1, 0:256], AF.Exp, scale=-DDEC), deps=[CL])
        tar2 = E.v(lambda e: e.tensor_scalar_mul(arow[:], arow[:], invt[0:1, 0:1]),
                   deps=[tar, tit])
        tba = E.t(lambda e: e.matmul(PB1[0:64, 0:256], on_sb[0:1, 0:64], arow[:],
                                     start=True, stop=True),
                  deps=[tar2, CL] + W.get("PB1"))
        t_abc = E.v(lambda e: e.tensor_copy(abc[:], PB1[0:64, 0:256]), deps=[tba])
        W.set("PB1", [t_abc])
        E.v(lambda e: e.memset(hmax_col[:], 0.0))

        qs_v = {}
        qs_p = {}
        qr = {}

        def vscale(ic, w):
            p2, p3 = ic % 2, ic % 3
            worig, wtr = (wg, wgt) if w == 0 else (wu, wut)
            scol = ic if w == 0 else NIC + ic
            sc = 4 * w + 2 * (ic % 2)
            ld = E.op("sy", f"wo{p2}", lambda e, p2=p2, worig=worig, ic=ic: e.dma_start(
                out=wob[p2][:], in_=worig[128 * ic:128 * (ic + 1), :]),
                deps=W.get(f"wob{p2}"))
            b6 = 3 * w + p3
            ldt = E.op("sy", f"wt{b6}", lambda e, b6=b6, wtr=wtr, ic=ic: e.dma_start(
                out=wtb[b6][:], in_=wtr[:, D * ic:D * (ic + 1)]),
                deps=W.get(f"wtb{b6}"))
            tr1 = E.v(lambda e, p2=p2, sc=sc: e.reduce_max(
                swcol[:, sc:sc + 1], wob[p2][:], axis=AX.X, apply_absolute_value=True),
                deps=[ld] + W.get(f"swc{w}{ic % 2}"))
            W.set(f"wob{p2}", [tr1])
            E.v(lambda e, scol=scol, sc=sc: e.tensor_scalar_mul(
                sgx[:, scol:scol + 1], swcol[:, sc:sc + 1], 1.0 / 127.0))
            E.v(lambda e, sc=sc: e.reciprocal(swcol[:, sc + 1:sc + 2], swcol[:, sc:sc + 1]))
            tiv = E.v(lambda e, sc=sc: e.tensor_scalar_mul(
                swcol[:, sc + 1:sc + 2], swcol[:, sc + 1:sc + 2], 127.0))
            tmuls = E.v(lambda e, scol=scol: e.tensor_mul(
                sgx[:, scol:scol + 1], sgx[:, scol:scol + 1], sx_col[:]), deps=[t_sx])
            qs_v[(ic, w)] = (ldt, tiv, tmuls)

        def pescale_T(ic, w):
            PQ, pqk = (PB1, "PB1") if w == 0 else (PB2, "PB2")
            sc = 4 * w + 2 * (ic % 2)
            ldt, tiv, tmuls = qs_v[(ic, w)]
            ttr = E.t(lambda e, PQ=PQ, sc=sc: e.transpose(
                PQ[0:1, 0:128], swcol[:, sc + 1:sc + 2], id_sb[:]),
                deps=[tiv, CL] + W.get(pqk))
            W.set(f"swc{w}{ic % 2}", [ttr])
            tcp = E.v(lambda e, w=w, PQ=PQ: e.tensor_copy(
                invrow[w][0:1, 0:128], PQ[0:1, 0:128]), deps=[ttr] + W.get(f"irow{w}"))
            for r8 in range(1, NDC):
                E.v(lambda e, w=w, r8=r8: e.tensor_copy(
                    invrow[w][0:1, 128 * r8:128 * (r8 + 1)], invrow[w][0:1, 0:128]))
            W.set(pqk, [tcp])
            W.set(f"irow{w}", [])
            qs_p[(ic, w)] = []
            return E.tok("v")

        def pescale_B(ic, w, half, prow):
            PQ, pqk = (PB1, "PB1") if w == 0 else (PB2, "PB2")
            tb = E.t(lambda e, PQ=PQ, w=w, half=half: e.matmul(
                PQ[:, 0:512], on_sb[:], invrow[w][0:1, 512 * half:512 * (half + 1)],
                start=True, stop=True),
                deps=[prow, CL] + W.get(pqk))
            te = E.v(lambda e, PQ=PQ, w=w, half=half: e.tensor_copy(
                invbc[w][:, 512 * half:512 * (half + 1)], PQ[:, 0:512]),
                deps=[tb] + (W.get(f"ibw{w}") if half == 0 else []))
            W.set(pqk, [te])
            W.add(f"irow{w}", tb)
            qs_p[(ic, w)].append(te)

        def pescale_full(ic):
            for w in range(2):
                prow = pescale_T(ic, w)
                pescale_B(ic, w, 0, prow)
                pescale_B(ic, w, 1, prow)

        def qround(ic, w):
            p2, p3 = ic % 2, ic % 3
            dst, dname = (ngt, "ngt") if w == 0 else (nut, "nut")
            ldt, tiv, tmuls = qs_v[(ic, w)]
            b6 = 3 * w + p3
            tq1 = E.v(lambda e, b6=b6, w=w: e.tensor_mul(rb4[:], wtb[b6][:], invbc[w][:]),
                      deps=[ldt] + qs_p[(ic, w)] + W.get("rq"))
            W.set(f"wtb{b6}", [tq1])
            W.set(f"ibw{w}", [tq1])
            if ROUND_SAFE:
                tk = E.s(lambda e: e.copy(kb4[:], rb4[:]), deps=[tq1] + W.get("rqk"))
                t2 = E.v(lambda e: e.scalar_tensor_tensor(fb4[:], rb4[:], -0.5, kb4[:],
                                                          ALU.add, ALU.subtract), deps=[tk])
                E.v(lambda e: e.tensor_single_scalar(cb4[:], fb4[:], -0.5, ALU.is_lt))
                E.v(lambda e: e.tensor_add(fb4[:], fb4[:], cb4[:]))
                E.v(lambda e: e.tensor_sub(rb4[:], rb4[:], fb4[:]))
                tq3 = E.v(lambda e, p2=p2, dst=dst: e.tensor_scalar(
                    dst[p2][:], rb4[:], -128.0, 127.0, ALU.max, ALU.min),
                    deps=W.get(f"{dname}{p2}"))
                W.set("rqk", [t2])
            else:
                E.v(lambda e: e.tensor_copy(kb4[:], rb4[:]))
                tq3 = E.v(lambda e, p2=p2, dst=dst: e.tensor_scalar(
                    dst[p2][:], kb4[:], -128.0, 127.0, ALU.max, ALU.min),
                    deps=W.get(f"{dname}{p2}"))
            W.set("rq", [tq3])
            qr[(ic, w)] = (tq3, tmuls)

        NXR_TC = {}
        xs_state = {"free": []}

        def emit_nx(tc):
            ld = E.op("sy", "xs0", lambda e, tc=tc: e.dma_start(
                out=xns[:], in_=xnT_sp[:, NDC * 512 * tc:NDC * 512 * (tc + 1)]),
                deps=xs_state["free"])
            for dd in range(NDC // 2):
                src = xns[:, 1024 * dd:1024 * (dd + 1)]
                dstp = nx[:, 4096 * tc + 1024 * dd:4096 * tc + 1024 * (dd + 1)]
                emit_round_q(E, W, src, invx_col[:, 0:1], dstp,
                             rb4[:], kb4[:], fb4[:], cb4[:], [ld, t_invx], "rq")
            xs_state["free"] = [E.tok("v")]
            NXR_TC[tc] = E.tok("v")

        hs_prev = [None, None]

        def emit_fmms(k, pe):
            p = k % 2
            tng, tmulsg = qr[(k, 0)]
            tnu, tmulsu = qr[(k, 1)]
            prow_g = prow_u = None
            if pe is not None:
                prow_g = pescale_T(pe, 0)
                prow_u = pescale_T(pe, 1)
            mg = mu = None
            for tc in range(NT512):
                pb = tc % 2
                for dc in range(NDC):
                    depg = ([tng, NXR_TC[tc]] + W.get("PB1")) if dc == 0 else []
                    mg = E.t(lambda e, p=p, dc=dc, tc=tc: e.matmul(
                        PB1[:], ngt[p][:, 128 * dc:128 * (dc + 1)],
                        nx[:, 4096 * tc + 512 * dc:4096 * tc + 512 * (dc + 1)],
                        start=(dc == 0), stop=(dc == NDC - 1)), deps=depg)
                for dc in range(NDC):
                    depu = ([tnu, NXR_TC[tc]] + W.get("PB2")) if dc == 0 else []
                    mu = E.t(lambda e, p=p, dc=dc, tc=tc: e.matmul(
                        PB2[:], nut[p][:, 128 * dc:128 * (dc + 1)],
                        nx[:, 4096 * tc + 512 * dc:4096 * tc + 512 * (dc + 1)],
                        start=(dc == 0), stop=(dc == NDC - 1)), deps=depu)
                tg_ = E.s(lambda e, k=k: e.activation(hg[:], PB1[:], AF.Silu,
                                                      scale=sgx[:, k:k + 1]),
                          deps=[mg, tmulsg] + W.get("hg"))
                W.set("PB1", [tg_])
                tu_ = E.v(lambda e, k=k: e.tensor_scalar_mul(hu[:], PB2[:],
                                                             sgx[:, NIC + k:NIC + k + 1]),
                          deps=[mu, tmulsu] + W.get("hu"))
                W.set("PB2", [tu_])
                th_ = E.v(lambda e, pb=pb: e.tensor_mul(hb[pb][:], hg[:], hu[:]),
                          deps=[tg_, tu_] + ([hs_prev[pb]] if hs_prev[pb] else []))
                W.set("hg", [th_])
                W.set("hu", [th_])
                E.v(lambda e, pb=pb: e.reduce_max(swcol2[:, 0:1], hb[pb][:], axis=AX.X,
                                                  apply_absolute_value=True))
                E.v(lambda e: e.tensor_max(hmax_col[:], hmax_col[:], swcol2[:, 0:1]))
                hs_prev[pb] = E.op("sy", f"hs{pb}", lambda e, pb=pb, k=k, tc=tc: e.dma_start(
                    out=h_sp[:, NIC * 512 * tc + 512 * k:NIC * 512 * tc + 512 * (k + 1)],
                    in_=hb[pb][:]),
                    deps=[th_])
                if pe is not None and tc < 4:
                    pescale_B(pe, tc // 2, tc % 2, prow_g if tc < 2 else prow_u)
            W.set(f"ngt{p}", [mg])
            W.set(f"nut{p}", [mu])

        def emit_item(it):
            kind, a = it
            if kind == "nx":
                emit_nx(a)
            else:
                k = a
                if k + 1 < NIC:
                    qround(k + 1, 0)
                    qround(k + 1, 1)
                if k + 3 < NIC:
                    vscale(k + 3, 0)
                    vscale(k + 3, 1)
                emit_fmms(k, (k + 2) if k + 2 < NIC else None)

        def emit_setup(j):
            bf = j % 2
            ob_ = 1 - bf
            ldq = E.op("sy", "ph1", lambda e, j=j: e.dma_start(
                out=phqs[:], in_=phi_q[:, 256 * j:256 * (j + 1)]), deps=W.get("phqs"))
            tqt = E.v(lambda e, bf=bf: e.tensor_mul(qt[bf][:], phqs[:], abc[:]),
                      deps=[t_abc, ldq] + W.get(f"qtr{bf}"))
            W.set("phqs", [tqt])
            W.set(f"qtr{bf}", [])
            if j > 0:
                E.v(lambda e, j=j, bf=bf, ob_=ob_: e.tensor_scalar_mul(
                    ktil[bf][:, 0:512 * j], ktil[ob_][:, 0:512 * j], 2.1175823681357508e-07),
                    deps=W.get(f"ktr{bf}"))
            ldp = E.op("sy", "ph0", lambda e, j=j: e.dma_start(
                out=phgs[:], in_=phi_g[:, 512 * j:512 * (j + 1)]), deps=W.get("phgs"))
            tsg = E.v(lambda e, j=j: e.tensor_scalar_mul(
                segb[:], qb_sb[0:1, j:j + 1], -DDEC), deps=[CL] + W.get("segb"))
            tbr = E.s(lambda e, j=j: e.activation(brow[:], io512[:], AF.Exp,
                                                  scale=DDEC, bias=segb[0:1, 0:1]),
                      deps=[tsg, CL] + W.get("brow"))
            W.set("segb", [tbr])
            tbk = E.t(lambda e: e.matmul(PB1[0:64, 0:512], on_sb[0:1, 0:64], brow[:],
                                         start=True, stop=True),
                      deps=[tbr, CL] + W.get("PB1"))
            W.set("brow", [tbk])
            tkt = E.v(lambda e, j=j, bf=bf: e.tensor_mul(
                ktil[bf][:, 512 * j:512 * (j + 1)], phgs[:], PB1[0:64, 0:512]),
                deps=[tbk, ldp] + W.get(f"ktr{bf}"))
            W.set("PB1", [tkt])
            W.set("phgs", [tkt])
            W.set(f"ktr{bf}", [])
            W.set(f"rdy{bf}", [E.tok("v")])

        def emit_chunk(j):
            bf = j % 2
            nkt = 4 * (j + 1)
            t_rdy = W.get(f"rdy{bf}")

            def emit_S(kt, bf=bf, t_rdy=t_rdy):
                h = kt % 2
                tsm = E.t(lambda e, kt=kt, bf=bf, h=h: e.matmul(
                    PS[h][:, 0:256], ktil[bf][:, 128 * kt:128 * (kt + 1)], qt[bf][:],
                    start=True, stop=True),
                    deps=t_rdy + W.get(f"PS{h}"))
                W.add(f"ktr{bf}", tsm)
                W.add(f"qtr{bf}", tsm)
                return tsm

            def emit_vld(kt):
                if kt < NVC:
                    return CL
                pv = kt % 2
                return E.op("sy", f"vg{pv}", lambda e, kt=kt, pv=pv: e.dma_start(
                    out=vgb[pv][:], in_=vg[128 * kt:128 * (kt + 1), :]),
                    deps=W.get(f"vgb{pv}"))

            s_toks = {0: emit_S(0)}
            v_toks = {0: emit_vld(0)}
            av_done = []
            for kt in range(nkt):
                pa = kt % 2
                pv = kt % 2
                h = kt % 2
                if kt + 1 < nkt:
                    v_toks[kt + 1] = emit_vld(kt + 1)
                    s_toks[kt + 1] = emit_S(kt + 1)
                if kt == 0 and j + 1 < NCH:
                    emit_setup(j + 1)
                tsm = s_toks[kt]
                vld = v_toks[kt]
                m = kt - (nkt - 4)
                if m >= 0:
                    tsm = E.v(lambda e, h=h, m=m: e.tensor_tensor(
                        PS[h][:, 0:256], PS[h][:, 0:256], mk_sb[:, 256 * m:256 * (m + 1)],
                        ALU.min), deps=[tsm, CL])
                tex = E.s(lambda e, pa=pa, h=h: e.activation(
                    atile[pa][:], PS[h][:, 0:256], AF.Exp, bias=negc_col[:, 0:1]),
                    deps=[tsm, t_negc] + W.get(f"at{pa}"))
                W.set(f"PS{h}", [tex])
                first = (kt == 0)
                last = (kt == nkt - 1)
                rhs_v = vgc if kt < NVC else vgb[pv]
                rbase = D * kt if kt < NVC else 0
                mm_toks = []
                av_mms = []
                for sl in range(2):
                    po = PO1 if sl == 0 else PO2
                    for half in range(2):
                        dep = [tex, vld]
                        if first and half == 0:
                            dep = dep + W.get(f"PO{sl}")
                        mm = E.t(lambda e, sl=sl, half=half, po=po, pa=pa,
                                 rhs_v=rhs_v, rbase=rbase,
                                 first=first, last=last: e.matmul(
                            po[:, 512 * half:512 * (half + 1)],
                            atile[pa][:, 128 * sl:128 * (sl + 1)],
                            rhs_v[:, rbase + 512 * half:rbase + 512 * (half + 1)],
                            start=first, stop=last), deps=dep)
                        mm_toks.append(mm)
                        av_mms.append(mm)
                    col = (256 if sl == 0 else 320) + kt // 2
                    dmm = E.t(lambda e, sl=sl, pa=pa, h=h, col=col: e.matmul(
                        PS[h][:, col:col + 1], atile[pa][:, 128 * sl:128 * (sl + 1)],
                        onk_sb[:], start=True, stop=True),
                        deps=(W.get(f"PSD{h}") if kt < 2 else []))
                    mm_toks.append(dmm)
                if kt >= NVC:
                    W.set(f"vgb{pv}", list(av_mms))
                W.set(f"at{pa}", mm_toks)
                if last:
                    av_done = mm_toks
            ne = nkt // 2
            for h2 in range(2):
                t0 = E.v(lambda e, h2=h2, ne=ne: e.reduce_sum(
                    rd2[:, h2:h2 + 1], PS[h2][:, 256:256 + ne], axis=AX.X), deps=av_done)
                t1 = E.v(lambda e, h2=h2, ne=ne: e.reduce_sum(
                    rd2[:, 2 + h2:3 + h2], PS[h2][:, 320:320 + ne], axis=AX.X))
                W.set(f"PSD{h2}", [t0, t1])
                W.add(f"PS{h2}", t0)
                W.add(f"PS{h2}", t1)
            E.v(lambda e: e.tensor_add(rd2[:, 0:1], rd2[:, 0:1], rd2[:, 1:2]),
                deps=W.get("rd"))
            E.v(lambda e: e.tensor_add(rd2[:, 2:3], rd2[:, 2:3], rd2[:, 3:4]))
            E.v(lambda e: e.reciprocal(rd[:, 0:1], rd2[:, 0:1]))
            trd = E.v(lambda e: e.reciprocal(rd[:, 1:2], rd2[:, 2:3]))
            oa_toks = []
            for sl in range(2):
                po = PO1 if sl == 0 else PO2
                pb = sl
                tout = E.v(lambda e, sl=sl, po=po, pb=pb: e.scalar_tensor_tensor(
                    oatb[pb][:], po[:], rd[:, sl:sl + 1], bo_sb[:], ALU.mult, ALU.add),
                    deps=[trd, CL] + ([oa_prev[pb]] if oa_prev[pb] else []))
                W.set(f"PO{sl}", [tout])
                oa_prev[pb] = E.op("sy", f"oa{pb}", lambda e, pb=pb, j=j, sl=sl: e.dma_start(
                    out=oat_o[256 * j + 128 * sl:256 * j + 128 * (sl + 1), :], in_=oatb[pb][:]),
                    deps=[tout])
                oa_toks.append(tout)
            W.set("rd", oa_toks)

        oa_prev = [None, None]

        emit_setup(0)
        for ic in range(2):
            vscale(ic, 0)
            vscale(ic, 1)
        pescale_full(0)
        qround(0, 0)
        qround(0, 1)
        pescale_full(1)
        vscale(2, 0)
        vscale(2, 1)

        items = [("nx", t) for t in range(NT512)] + [("f", kk) for kk in range(NIC)]
        ndone = 0
        for j in range(NCH):
            target = min(len(items), int(round(len(items) * j / 14.0)))
            while ndone < target:
                emit_item(items[ndone])
                ndone += 1
            emit_chunk(j)
        while ndone < len(items):
            emit_item(items[ndone])
            ndone += 1

        thf = E.t(lambda e: e.transpose(PB1[0:1, 0:128], hmax_col[:], id_sb[:]),
                  deps=[E.tok("v"), CL] + W.get("PB1"))
        E.v(lambda e: e.tensor_copy(hrow[:], PB1[0:1, 0:128]), deps=[thf])
        W.set("PB1", [E.tok("v")])
        thx = E.v(lambda e: e.reduce_max(hx_sb[:], hrow[:], axis=AX.X))
        E.op("sy", "go", lambda e: e.dma_start(out=hx_o, in_=hx_sb[:]), deps=[thx])

        blk = ctx.enter_context(nc.Block())
        waits = [E.tok("hs0"), E.tok("hs1"), E.tok("oa0"), E.tok("oa1"), E.tok("go")]
        finish(E, blk, extra_waits=waits)
    return nc



def build_l3():
    nc = bass.Bass("TRN2", target_bir_lowering=False, dynamic_dma_scratch_size=2048)
    din = lambda n, s, d=F32: nc.dram_tensor(n, s, d, kind="ExternalInput").ap()
    dout = lambda n, s, d=F32: nc.dram_tensor(n, s, d, kind="ExternalOutput").ap()

    h_sp = din("h_sp", [128, NIC * TOK])
    hx8 = din("hx8", [1, 8])
    x_own = din("x_own", [TOK, D])
    oat = din("oat", [TOK, D])
    wd = din("wd", [D, IF])
    clip_d = din("clip_d", [1, 1])
    ident = din("ident", [128, 128])
    identb = din("identb", [128, 128], BF16)
    ones_r = din("ones_r", [1, 128])

    y_o = dout("y_out", [TOK, D])

    from contextlib import ExitStack
    with ExitStack() as ctx:
        sb = lambda n, s, d=F32: ctx.enter_context(nc.sbuf_tensor(n, s, d))
        psa = lambda n, s: ctx.enter_context(nc.psum_tensor(n, s, F32))
        sem = lambda n: ctx.enter_context(nc.semaphore(n))

        hx_sb = sb("hx_sb", [1, 8])
        cd_sb = sb("cd_sb", [1, 1])
        id_sb = sb("id_sb", [128, 128])
        idb_sb = sb("idb_sb", [128, 128], BF16)
        on_sb = sb("on_sb", [1, 128])
        srow = sb("srow", [1, 8])
        invh_col = sb("invh_col", [128, 1])
        sdrow = sb("sdrow", [1, D])
        sd_cols = sb("sd_cols", [128, NDC])
        sd_rows = sb("sd_rows", [NDC, 128])
        sdown_b = sb("sdown_b", [128, D])
        wdb = [sb("wdb0", [128, IF])]
        wdb = [wdb[0], wdb[0]]
        swc = sb("swc", [128, 1])
        ndT2 = sb("ndT2", [128, NIC * D], BF16)
        hsball = sb("hsball", [128, NIC * 512])
        nh = [sb(f"nh{p}", [128, NIC * 512], BF16) for p in range(2)]
        rb = sb("rb", [128, IF])
        kb = sb("kb", [128, IF], I32)
        fb = sb("fb", [128, IF if ROUND_SAFE else 1])
        cb = sb("cb", [128, IF if ROUND_SAFE else 1])
        xb = [sb(f"xb{p}", [128, D]) for p in range(2)]
        ob = [sb(f"ob{p}", [128, D]) for p in range(2)]
        yb = [sb(f"yb{p}", [128, D]) for p in range(2)]

        PY = [psa("PY0", [128, 1024]), psa("PY1", [128, 1024])]
        PT = psa("PT", [128, 512])

        E = Env(nc, ("sy", "v", "s", "t", "g"),
                ENG_CTRS + [("cl", 16), ("wd0", 16), ("hs0", 16),
                            ("xb0", 16), ("xb1", 16), ("ob0", 16), ("ob1", 16),
                            ("yo0", 16), ("yo1", 16), ("rg", 16)],
                sem)
        W = War()

        def cload(dst, src):
            return E.op("sy", "cl", lambda e, d=dst, s_=src: e.dma_start(out=d, in_=s_))

        for dst, src in [(hx_sb[:], hx8), (cd_sb[:], clip_d), (id_sb[:], ident),
                         (idb_sb[:], identb), (on_sb[:], ones_r)]:
            cload(dst, src)
        CL = E.tok("cl")

        E.v(lambda e: e.reduce_max(srow[0:1, 0:1], hx_sb[:], axis=AX.X), deps=[CL])
        ts1 = E.s(lambda e: e.activation(srow[0:1, 1:2], cd_sb[:], AF.Sigmoid), deps=[CL])
        E.v(lambda e: e.tensor_mul(srow[0:1, 2:3], srow[0:1, 1:2], srow[0:1, 0:1]), deps=[ts1])
        E.v(lambda e: e.reciprocal(srow[0:1, 3:4], srow[0:1, 2:3]))
        tih = E.v(lambda e: e.tensor_scalar_mul(srow[0:1, 3:4], srow[0:1, 3:4], 127.0))
        tsh = E.v(lambda e: e.tensor_scalar_mul(srow[0:1, 2:3], srow[0:1, 2:3], 1.0 / 127.0))
        tm = E.t(lambda e: e.matmul(PT[:, 0:1], on_sb[:], srow[0:1, 3:4], start=True, stop=True),
                 deps=[tih, CL])
        tivh = E.v(lambda e: e.tensor_copy(invh_col[:], PT[:, 0:1]), deps=[tm])
        W.set("PT", [tivh])

        for hc in range(NDC):
            p = hc % 2
            ld = E.op("sy", "wd0", lambda e, hc=hc, p=p: e.dma_start(
                out=wdb[p][:], in_=wd[128 * hc:128 * (hc + 1), :]), deps=W.get("wdb0"))
            E.v(lambda e, p=p: e.reduce_max(swc[:], wdb[p][:], axis=AX.X,
                                            apply_absolute_value=True), deps=[ld])
            E.v(lambda e, hc=hc: e.tensor_scalar_mul(sd_cols[:, hc:hc + 1], swc[:], 1.0 / 127.0))
            E.v(lambda e: e.reciprocal(swc[:], swc[:]))
            tiv = E.v(lambda e: e.tensor_scalar_mul(swc[:], swc[:], 127.0))
            tnq = emit_round_q(E, W, wdb[p][:], swc[:], rb[:],
                               rb[:], kb[:], fb[:], cb[:], [tiv], "rq")
            W.add("wdb0", tnq)
            for ic in range(NIC):
                ttr = E.t(lambda e, ic=ic: e.transpose(PT[:, 128:256], rb[:, 128 * ic:128 * (ic + 1)],
                                                       id_sb[:]),
                          deps=[tnq, CL] + W.get("PT"))
                tev = E.s(lambda e, ic=ic, hc=hc: e.copy(
                    ndT2[:, D * ic + 128 * hc:D * ic + 128 * (hc + 1)], PT[:, 128:256]),
                    deps=[ttr] + W.get("ndT2"))
                W.set("PT", [tev])
            W.add("rq", E.tok("s"))
            W.add("rq", E.tok("t"))

        tpk = E.t(lambda e: e.transpose(PT[0:NDC, 128:256], sd_cols[:], id_sb[:]),
                  deps=[E.tok("v"), CL] + W.get("PT"))
        tcp = E.v(lambda e: e.tensor_copy(sd_rows[:], PT[0:NDC, 128:256]), deps=[tpk])
        W.set("PT", [tcp])
        trg = E.op("g", "rg", lambda e: e.dma_start(out=sdrow[:], in_=sd_rows[:]), deps=[tcp])
        tsc = E.v(lambda e: e.tensor_scalar_mul(sdrow[:], sdrow[:], srow[0:1, 2:3]),
                  deps=[trg, tsh])
        bt = []
        for half in range(2):
            tbm = E.t(lambda e, half=half: e.matmul(
                PT[:], on_sb[:], sdrow[0:1, 512 * half:512 * (half + 1)],
                start=True, stop=True), deps=[tsc, CL] + W.get("PT"))
            tbe = E.v(lambda e, half=half: e.tensor_copy(
                sdown_b[:, 512 * half:512 * (half + 1)], PT[:]), deps=[tbm])
            W.set("PT", [tbe])
            bt.append(tbe)

        HW = NIC * 512

        def emit_hload(tc, hh):
            return E.op("sy", "hs0", lambda e, tc=tc, hh=hh: e.dma_start(
                out=hsball[:, 5632 * hh:5632 * (hh + 1)],
                in_=h_sp[:, HW * tc + 5632 * hh:HW * tc + 5632 * (hh + 1)]),
                deps=W.get(f"hsball{hh}"))

        def emit_rounds(tc, pn, lds):
            toks = []
            for g4 in range(4):
                hh = g4 // 2
                sl4 = slice(IF * g4, IF * (g4 + 1))
                tq = emit_round_q(E, W, hsball[:, sl4], invh_col[:, 0:1],
                                  nh[pn][:, sl4],
                                  rb[:], kb[:], fb[:], cb[:],
                                  [lds[hh], tivh] + W.get(f"nhw{pn}"), "rq")
                toks.append(tq)
            W.set("hsball0", [toks[1]])
            W.set("hsball1", [toks[3]])
            return toks

        yo_prev = [None, None]
        lds = [emit_hload(0, 0), emit_hload(0, 1)]
        nh_toks = {0: emit_rounds(0, 0, lds)}
        for tc in range(NT512):
            pn = tc % 2
            if tc + 1 < NT512:
                lds = [emit_hload(tc + 1, 0), emit_hload(tc + 1, 1)]
                nh_toks[tc + 1] = emit_rounds(tc + 1, (tc + 1) % 2, lds)
            for tt in range(4):
                pp = tt % 2
                mm = None
                for ic in range(NIC):
                    for half in range(2):
                        dep = []
                        if ic == 0 and half == 0:
                            dep = nh_toks[tc] + W.get(f"PY{pp}")
                        mm = E.t(lambda e, ic=ic, tt=tt, half=half, pp=pp, pn=pn: e.matmul(
                            PY[pp][:, 512 * half:512 * (half + 1)],
                            nh[pn][:, 512 * ic + 128 * tt:512 * ic + 128 * (tt + 1)],
                            ndT2[:, D * ic + 512 * half:D * ic + 512 * (half + 1)],
                            start=(ic == 0), stop=(ic == NIC - 1)),
                            deps=dep)
                W.set(f"nh_mm{pp}", [mm])
                ti = tc * 4 + tt
                xld = E.op("sy", f"xb{pp}", lambda e, ti=ti, pp=pp: e.dma_start(
                    out=xb[pp][:], in_=x_own[128 * ti:128 * (ti + 1), :]), deps=W.get(f"xb{pp}"))
                old = E.op("sy", f"ob{pp}", lambda e, ti=ti, pp=pp: e.dma_start(
                    out=ob[pp][:], in_=oat[128 * ti:128 * (ti + 1), :]), deps=W.get(f"ob{pp}"))
                tyy = E.v(lambda e, pp=pp: e.tensor_mul(yb[pp][:], PY[pp][:], sdown_b[:]),
                          deps=[mm] + bt + ([yo_prev[pp]] if yo_prev[pp] else []))
                W.set(f"PY{pp}", [tyy])
                E.v(lambda e, pp=pp: e.tensor_add(yb[pp][:], yb[pp][:], xb[pp][:]),
                    deps=[xld])
                tfin = E.v(lambda e, pp=pp: e.tensor_add(yb[pp][:], yb[pp][:], ob[pp][:]),
                           deps=[old])
                W.set(f"xb{pp}", [tfin])
                W.set(f"ob{pp}", [tfin])
                yo_prev[pp] = E.op("sy", f"yo{pp}", lambda e, ti=ti, pp=pp: e.dma_start(
                    out=y_o[128 * ti:128 * (ti + 1), :], in_=yb[pp][:]), deps=[tfin])
            W.set(f"nhw{pn}", W.get(f"nh_mm0") + W.get(f"nh_mm1"))

        blk = ctx.enter_context(nc.Block())
        waits = [E.tok("yo0"), E.tok("yo1")]
        finish(E, blk, extra_waits=waits)
    return nc



def _f32(a):
    return np.ascontiguousarray(a, dtype=np.float32)


def own_token_index(role):
    idx = []
    for j in range(NCH):
        g = (2 * j + role) * QW
        idx.extend(range(g, g + QW))
    return np.asarray(idx)


def blocked_T(a):
    d, n = a.shape
    return _f32(np.concatenate([a[128 * c:128 * (c + 1), :] for c in range(d // 128)], axis=1))


def mask_tiles(role):
    out = np.empty((128, 4 * QW), dtype=np.float32)
    for m in range(4):
        delta = 128 * m - role * QW
        k = np.arange(128)[:, None]
        qq = np.arange(QW)[None, :]
        out[:, m * QW:(m + 1) * QW] = np.where(qq < (k + delta), BIGNEG, BIGPOS)
    return out


def stage_l1_inputs(inputs):
    x = np.asarray(inputs["L0_attn_x"])
    wv = np.asarray(inputs["L0_attn_W_v"])
    wout = np.asarray(inputs["L0_attn_W_out"])
    wsh = np.asarray(inputs["L0_attn_W_shell"])
    wch = np.asarray(inputs["L0_attn_W_charge"])
    om = np.asarray(inputs["L0_attn_omega"])
    rffb = np.asarray(inputs["L0_attn_rff_bias"])
    wcomb = np.asarray(inputs["L0_attn_W_combine"])
    bsh = np.asarray(inputs["L0_attn_b_shell"])
    bch = np.asarray(inputs["L0_attn_b_charge"])
    bv = np.asarray(inputs["L0_attn_b_v"])

    shared = {
        "wv_t": blocked_T(_f32(wv).T),
        "wout_t": blocked_T(_f32(wout).T),
        "wsh_t": blocked_T(_f32(wsh).T),
        "wch_t": blocked_T(_f32(wch).T),
        "bsh_row": _f32(bsh.reshape(1, 64)),
        "bch_col": _f32(np.tile(np.asarray(bch).reshape(1, 1), (128, 1))),
        "bv_cols": _f32(bv.reshape(NDC, 128).T),
        "omega_aug": _f32(np.concatenate([_f32(om).T, rffb.reshape(1, 32)], axis=0)),
        "wc0": _f32(_f32(wcomb).T[0:128, :]),
        "wc1": _f32(_f32(wcomb).T[128:256, :]),
        "wc2": _f32(_f32(wcomb).T[256:320, :]),
        "ident": np.eye(128, dtype=np.float32),
        "ones_r": np.ones((1, 128), dtype=np.float32),
        "ones_c": np.ones((64, 1), dtype=np.float32),
    }
    maps = []
    for c in range(NCORE):
        s, r = c >> 1, c & 1
        rows = _f32(x[s][own_token_index(r)])
        m = dict(shared)
        m["x_own"] = rows
        m["xT_own"] = blocked_T(_f32(rows.T))
        maps.append(m)
    return maps


def stage_l2_inputs(inputs, r1):
    import ml_dtypes
    bf = ml_dtypes.bfloat16
    wg = _f32(np.asarray(inputs["L0_ffn_w_gate"]))
    wu = _f32(np.asarray(inputs["L0_ffn_w_up"]))
    clip_x = _f32(np.asarray(inputs["L0_ffn_clip_x"]).reshape(1, 1))
    log_temp = _f32(np.asarray(inputs["L0_attn_log_temp"]).reshape(1, 1))
    bout = np.asarray(inputs["L0_attn_b_out"])

    gx8 = np.concatenate([r1[c]["mx"][0:1, 0:1] for c in range(NCORE)], axis=1)
    nn8 = np.concatenate([r1[c]["mx"][0:1, 1:2] for c in range(NCORE)], axis=1)

    def icmajor_T(w):
        wt = _f32(w).T
        out = np.empty((128, NIC * D), dtype=np.float32)
        for ic in range(NIC):
            for dc in range(NDC):
                out[:, D * ic + 128 * dc:D * ic + 128 * (dc + 1)] = \
                    wt[128 * dc:128 * (dc + 1), 128 * ic:128 * (ic + 1)]
        return out

    shared = {
        "gx8": _f32(gx8), "nn8": _f32(nn8), "wg": wg, "wu": wu,
        "wgt": icmajor_T(wg), "wut": icmajor_T(wu),
        "clip_x": clip_x, "log_temp": log_temp,
        "iota512": np.arange(512, dtype=np.float32).reshape(1, 512),
        "b_out_bc": _f32(np.tile(bout.reshape(1, D), (128, 1))),
        "ident": np.eye(128, dtype=np.float32),
        "identb": np.eye(128, dtype=bf),
        "ones_r": np.ones((1, 128), dtype=np.float32),
        "ones_kc": np.ones((128, 1), dtype=bf),
    }
    maps = []
    for c in range(NCORE):
        s, r = c >> 1, c & 1
        va = r1[2 * s]["v_own"]
        vb = r1[2 * s + 1]["v_own"]
        vg = np.empty((SEQ, D), dtype=va.dtype)
        ia = own_token_index(0)
        ib = own_token_index(1)
        vg[ia] = va
        vg[ib] = vb
        pa = r1[2 * s]["phiT"]
        pb = r1[2 * s + 1]["phiT"]
        pg = np.empty((64, SEQ), dtype=pa.dtype)
        pg[:, ia] = pa
        pg[:, ib] = pb
        m = dict(shared)
        m["vg"] = np.ascontiguousarray(vg)
        m["phi_g"] = np.ascontiguousarray(pg)
        m["phi_q"] = np.ascontiguousarray(r1[c]["phiT"])
        m["xnT_sp"] = r1[c]["xnT_sp"]
        m["qb16"] = np.asarray([(2 * j + r) * QW for j in range(NCH)],
                               dtype=np.float32).reshape(1, NCH)
        m["maskS"] = mask_tiles(r).astype(ml_dtypes.bfloat16)
        maps.append(m)
    return maps


def stage_l3_inputs(inputs, maps1, r2):
    wd = _f32(np.asarray(inputs["L0_ffn_w_down"]))
    clip_d = _f32(np.asarray(inputs["L0_ffn_clip_down"]).reshape(1, 1))
    hx8 = np.concatenate([r2[c]["hx"][0:1, 0:1] for c in range(NCORE)], axis=1)
    import ml_dtypes
    shared = {
        "wd": wd, "clip_d": clip_d, "hx8": _f32(hx8),
        "ident": np.eye(128, dtype=np.float32),
        "identb": np.eye(128, dtype=ml_dtypes.bfloat16),
        "ones_r": np.ones((1, 128), dtype=np.float32),
    }
    maps = []
    for c in range(NCORE):
        m = dict(shared)
        m["h_sp"] = r2[c]["h_sp"]
        m["x_own"] = maps1[c]["x_own"]
        m["oat"] = r2[c]["oat"]
        maps.append(m)
    return maps


def kernel(**inputs):
    core_ids = list(range(NCORE))
    nc1 = build_l1()
    maps1 = stage_l1_inputs(inputs)
    r1 = run_bass_kernel_spmd(nc1, maps1, core_ids=core_ids).results
    nc2 = build_l2()
    maps2 = stage_l2_inputs(inputs, r1)
    r2 = run_bass_kernel_spmd(nc2, maps2, core_ids=core_ids).results
    nc3 = build_l3()
    maps3 = stage_l3_inputs(inputs, maps1, r2)
    r3 = run_bass_kernel_spmd(nc3, maps3, core_ids=core_ids).results
    x = np.asarray(inputs["L0_attn_x"])
    out = np.empty((NB, SEQ, D), dtype=np.float32)
    for c in range(NCORE):
        s, r = c >> 1, c & 1
        out[s][own_token_index(r)] = r3[c]["y_out"]
    return out


# version 09
# speedup vs baseline: 1.1186x; 1.1186x over previous
import math
import os
import numpy as np
import concourse.bass as bass
import concourse.mybir as mybir
from concourse.bass_utils import run_bass_kernel_spmd

F32 = mybir.dt.float32
BF16 = mybir.dt.bfloat16
I32 = mybir.dt.int32
AF = mybir.ActivationFunctionType
ALU = mybir.AluOpType
AX = mybir.AxisListType

SEQ, D, NB, IF = 8192, 1024, 4, 2816
NCORE = 8
TOK, NCH, QW = 4096, 16, 256
NIC, NDC, NT512 = IF // 128, D // 128, 4096 // 512

INV2PI = float(np.float64(1.0) / (2.0 * np.pi))
TWOPI = float(2.0 * np.pi)
DDEC = 0.03
RFF_SCALE = float(np.sqrt(2.0 / 64.0) * np.sqrt(np.exp(-0.5)))
CMAG = [float(np.sqrt(abs(((-1.0) ** m) / math.factorial(m)) + 1e-10) * RFF_SCALE)
        for m in range(5)]
LN_EPS = 1e-5
BIGPOS = 3.0e38
BIGNEG = -1.0e30



class Q:
    def __init__(self, name):
        self.name = name
        self.cmds = []
        self.hwm = {}

    def replay(self, eng):
        for c in self.cmds:
            if c[0] == "w":
                eng.wait_ge(c[1], c[2])
            else:
                c[1](eng).then_inc(c[2], c[3])


class Ctr:
    def __init__(self, sem, step):
        self.sem, self.step = sem, step
        self.count = 0


class Env:

    CHAINED = ("v", "s", "g")

    def __init__(self, nc, qnames, ctr_spec, sem):
        self.nc = nc
        self.q = {k: Q(k) for k in qnames}
        self.c = {k: Ctr(sem("c_" + k), st) for k, st in ctr_spec}

    def op(self, qname, cname, emit, deps=()):
        ctr = self.c[cname]
        q = self.q[qname]
        alldeps = list(deps)
        if cname in self.CHAINED and ctr.count > 0:
            alldeps.append((ctr.sem, ctr.count))
        for semh, val in alldeps:
            if semh is None or val <= 0:
                continue
            key = id(semh)
            if q.hwm.get(key, -1) >= val:
                continue
            q.cmds.append(("w", semh, val))
            q.hwm[key] = val
        ctr.count += ctr.step
        q.cmds.append(("i", emit, ctr.sem, ctr.step))
        return (ctr.sem, ctr.count)

    def v(self, emit, deps=()):
        return self.op("v", "v", emit, deps)

    def s(self, emit, deps=()):
        return self.op("s", "s", emit, deps)

    def t(self, emit, deps=()):
        return self.op("t", "t", emit, deps)

    def g(self, emit, deps=()):
        return self.op("g", "g", emit, deps)

    def tok(self, cname):
        ctr = self.c[cname]
        return (ctr.sem, ctr.count)


ENG_CTRS = [("v", 1), ("s", 1), ("t", 1), ("g", 1)]


def finish(env, blk, extra_waits=()):
    q = env.q

    @blk.sync
    def _(eng):
        q["sy"].replay(eng)
        for semh, val in extra_waits:
            if val > 0:
                eng.wait_ge(semh, val)

    @blk.vector
    def _(eng):
        q["v"].replay(eng)

    @blk.scalar
    def _(eng):
        q["s"].replay(eng)

    @blk.tensor
    def _(eng):
        q["t"].replay(eng)

    @blk.gpsimd
    def _(eng):
        q["g"].replay(eng)


class War:

    def __init__(self):
        self.d = {}

    def get(self, k):
        return self.d.get(k, [])

    def set(self, k, toks):
        self.d[k] = list(toks)

    def add(self, k, tok):
        self.d.setdefault(k, []).append(tok)



def build_l1():
    nc = bass.Bass("TRN2", target_bir_lowering=False)
    din = lambda n, s, d=F32: nc.dram_tensor(n, s, d, kind="ExternalInput").ap()
    dout = lambda n, s, d=F32: nc.dram_tensor(n, s, d, kind="ExternalOutput").ap()

    x_own = din("x_own", [TOK, D])
    xT_own = din("xT_own", [128, NDC * TOK])
    wv_t = din("wv_t", [128, NDC * D])
    wout_t = din("wout_t", [128, NDC * D])
    wsh_t = din("wsh_t", [128, NDC * 64])
    wch_t = din("wch_t", [128, NDC])
    bsh_row = din("bsh_row", [1, 64])
    bch_col = din("bch_col", [128, 1])
    bv_cols = din("bv_cols", [128, NDC])
    omega_aug = din("omega_aug", [65, 32])
    wc0 = din("wc0", [128, 64])
    wc1 = din("wc1", [128, 64])
    wc2 = din("wc2", [64, 64])
    ident = din("ident", [128, 128])
    ones_r = din("ones_r", [1, 128])
    ones_c = din("ones_c", [64, 1])

    xnT_sp = dout("xnT_sp", [128, NDC * TOK])
    v_own = dout("v_own", [TOK, D], BF16)
    phiT_o = dout("phiT", [64, TOK], BF16)
    mx_o = dout("mx", [1, 2])

    from contextlib import ExitStack
    with ExitStack() as ctx:
        sb = lambda n, s, d=F32: ctx.enter_context(nc.sbuf_tensor(n, s, d))
        psa = lambda n, s: ctx.enter_context(nc.psum_tensor(n, s, F32))
        sem = lambda n: ctx.enter_context(nc.semaphore(n))

        wv_sb = sb("wv_sb", [128, NDC * D], BF16)
        wo_sb = sb("wo_sb", [128, NDC * D], BF16)
        wsh_sb = sb("wsh_sb", [128, NDC * 64], BF16)
        wch_sb = sb("wch_sb", [128, NDC], BF16)
        bsh_sb = sb("bsh_sb", [1, 64])
        bch_sb = sb("bch_sb", [128, 1])
        bv_sb = sb("bv_sb", [128, NDC])
        om_sb = sb("om_sb", [65, 32])
        wcs = [sb("wc0_sb", [128, 64]), sb("wc1_sb", [128, 64]), sb("wc2_sb", [64, 64])]
        id_sb = sb("id_sb", [128, 128])
        on_sb = sb("on_sb", [1, 128])
        onc_sb = sb("onc_sb", [64, 1])

        xbuf = [sb(f"xbuf{p}", [128, D]) for p in range(2)]
        xtb = [sb(f"xtb{p}", [128, NDC * 512]) for p in range(2)]
        xnT = [sb(f"xnT{p}", [128, NDC * 512]) for p in range(2)]
        xnTb = [sb(f"xnTb{p}", [128, NDC * 512], BF16) for p in range(2)]
        vT = [sb(f"vT{p}", [128, NDC * 512], BF16) for p in range(2)]
        vpb = [sb(f"vpb{p}", [128, D], BF16) for p in range(2)]

        sq_scr = sb("sq_scr", [128, D])
        stats = sb("stats", [128, 8])
        mr = sb("mr", [128, 2])
        mean_row = sb("mean_row", [1, 512])
        rstd_row = sb("rstd_row", [1, 512])
        m_b = sb("m_b", [128, 512])
        r_b = sb("r_b", [128, 512])
        gmax_col = sb("gmax_col", [128, 1])
        q_col = sb("q_col", [128, 1])
        qpow = sb("qpow", [128, 1])
        phic = sb("phic", [128, 5])
        shell = sb("shell", [128, 64])
        shn = sb("shn", [128, 65])
        shnT = sb("shnT", [65, 128])
        prt = sb("prt", [128, 64])
        prk = sb("prk", [128, 64], I32)
        prf = sb("prf", [128, 64])
        phis = sb("phis", [128, 64])
        tpb = sb("tpb", [128, 320])
        tpT = [sb("tpT0", [128, 128]), sb("tpT1", [128, 128]), sb("tpT2", [64, 128])]
        phiT_buf = sb("phiT_buf", [64, TOK], BF16)
        phisq = sb("phisq", [64, 128])
        nn_acc = sb("nn_acc", [1, 1])
        fin_row = sb("fin_row", [1, 128])
        mx_sb = sb("mx_sb", [1, 2])

        ps_mr = psa("ps_mr", [1, 256])
        ps_m = psa("ps_m", [128, 512])
        ps_r = psa("ps_r", [128, 512])
        ps_v = psa("ps_v", [128, 512])
        ps_vp = psa("ps_vp", [128, 1024])
        PSF = psa("PSF", [128, 512])
        PSG = psa("PSG", [128, 512])
        ps_q = PSF[:, 0:1]
        ps_sh = PSF[:, 32:96]
        ps_st = PSF[0:65, 128:256]
        ps_pj = PSF[:, 288:320]
        ps_nn = PSF[0:1, 384:512]
        ps_tp = [PSG[:, 0:128], PSG[:, 128:256], PSG[0:64, 256:384]]
        ps_phi = PSG[0:64, 384:512]

        E = Env(nc, ("sy", "v", "s", "t", "g"),
                ENG_CTRS + [("cl", 16), ("clg", 16), ("x0", 16), ("x1", 16), ("xt0", 16), ("xt1", 16),
                            ("sp0", 16), ("sp1", 16), ("vo0", 16), ("vo1", 16), ("go", 16)],
                sem)
        W = War()

        def cload(dst, src):
            return E.op("sy", "cl", lambda e, d=dst, s_=src: e.dma_start(out=d, in_=s_))

        def gload(dst, src):
            return E.op("g", "clg", lambda e, d=dst, s_=src: e.dma_start(out=d, in_=s_))

        for dst, src in [(bsh_sb[:], bsh_row), (bch_sb[:], bch_col),
                         (bv_sb[:], bv_cols), (om_sb[:], omega_aug), (wcs[0][:], wc0),
                         (wcs[1][:], wc1), (wcs[2][:], wc2), (id_sb[:], ident),
                         (on_sb[:], ones_r), (onc_sb[:], ones_c)]:
            cload(dst, src)
        CL = E.tok("cl")
        for dst, src in [(wv_sb[:], wv_t), (wo_sb[:], wout_t),
                         (wsh_sb[:], wsh_t), (wch_sb[:], wch_t)]:
            gload(dst, src)
        CLG = E.tok("clg")
        t_wv = t_wo = t_wsh = t_wch = CLG
        E.v(lambda e: e.memset(gmax_col[:], 0.0))
        E.v(lambda e: e.memset(nn_acc[:], 0.0))
        E.v(lambda e: e.memset(shn[:, 64:65], 1.0))
        E.v(lambda e: e.memset(phic[:, 0:1], CMAG[0]))

        x_free = [[], []]
        xt_free = [[], []]
        sp_prev = [None, None]
        vo_prev = [None, None]

        for tc in range(NT512):
            p = tc % 2
            xt_ld = E.op("sy", f"xt{p}", lambda e, tc=tc, p=p: e.dma_start(
                out=xtb[p][:].rearrange("q (c n) -> q c n", n=512),
                in_=xT_own.rearrange("q (c n) -> q c n", n=TOK)[:, :, 512 * tc:512 * (tc + 1)]),
                deps=xt_free[p])
            xt_free[p] = []

            mr_toks = []
            for tt in range(4):
                ti = tc * 4 + tt
                pp = ti % 2
                x_ld = E.op("sy", f"x{pp}", lambda e, ti=ti, pp=pp: e.dma_start(
                    out=xbuf[pp][:], in_=x_own[128 * ti:128 * (ti + 1), :]),
                    deps=x_free[pp])
                t1 = E.v(lambda e, pp=pp: e.reduce_sum(stats[:, 0:1], xbuf[pp][:], axis=AX.X),
                         deps=[x_ld])
                t2 = E.s(lambda e, pp=pp: e.activation(sq_scr[:], xbuf[pp][:], AF.Square,
                                                       accum_out=stats[:, 1:2]),
                         deps=[x_ld, CL] + W.get("sq_scr") + W.get("stats1"))
                x_free[pp] = [t1, t2]
                E.v(lambda e: e.tensor_scalar_mul(mr[:, 0:1], stats[:, 0:1], 1.0 / D),
                    deps=W.get("mr"))
                E.v(lambda e: e.tensor_mul(stats[:, 2:3], mr[:, 0:1], mr[:, 0:1]))
                E.v(lambda e: e.scalar_tensor_tensor(stats[:, 3:4], stats[:, 1:2], 1.0 / D,
                                                     stats[:, 2:3], ALU.mult, ALU.subtract),
                    deps=[t2])
                W.set("stats1", [E.tok("v")])
                E.v(lambda e: e.tensor_scalar_add(stats[:, 3:4], stats[:, 3:4], LN_EPS))
                t7 = E.v(lambda e: e.reciprocal(stats[:, 2:3], stats[:, 3:4]))
                t8 = E.s(lambda e: e.activation(mr[:, 1:2], stats[:, 2:3], AF.Sqrt), deps=[t7])
                t9a = E.t(lambda e: e.transpose(ps_mr[:, 0:128], mr[:, 0:1], id_sb[:]),
                          deps=[t8, E.tok("v"), CL] + W.get("ps_mr"))
                t9 = E.t(lambda e: e.transpose(ps_mr[:, 128:256], mr[:, 1:2], id_sb[:]))
                W.set("mr", [t9])
                t10a = E.v(lambda e, tt=tt: e.tensor_copy(mean_row[0:1, 128 * tt:128 * (tt + 1)], ps_mr[0:1, 0:128]),
                           deps=[t9a, t9] + W.get("mr_rows"))
                t10 = E.v(lambda e, tt=tt: e.tensor_copy(rstd_row[0:1, 128 * tt:128 * (tt + 1)], ps_mr[0:1, 128:256]))
                W.set("ps_mr", [t10])
                mr_toks.append(t10)

            tb0 = E.t(lambda e: e.matmul(ps_m[:], on_sb[:], mean_row[:], start=True, stop=True),
                      deps=mr_toks + [CL] + W.get("ps_m"))
            tb1 = E.t(lambda e: e.matmul(ps_r[:], on_sb[:], rstd_row[:], start=True, stop=True),
                      deps=W.get("ps_r"))
            W.set("mr_rows", [tb0, tb1])
            te0 = E.s(lambda e: e.copy(m_b[:], ps_m[:]), deps=[tb0] + W.get("m_b"))
            te1 = E.s(lambda e: e.copy(r_b[:], ps_r[:]), deps=[tb1] + W.get("r_b"))
            W.set("ps_m", [te0])
            W.set("ps_r", [te1])

            xn_toks = []
            xnb_toks = []
            for dc in range(NDC):
                sl = slice(512 * dc, 512 * (dc + 1))
                E.v(lambda e, p=p, sl=sl: e.tensor_sub(xnT[p][:, sl], xtb[p][:, sl], m_b[:]),
                    deps=[xt_ld, te0] + W.get(f"xnT{p}"))
                tb_ = E.v(lambda e, p=p, sl=sl: e.tensor_mul(xnT[p][:, sl], xnT[p][:, sl], r_b[:]),
                          deps=[te1])
                E.v(lambda e, p=p, sl=sl: e.reduce_max(stats[:, 4:5], xnT[p][:, sl], axis=AX.X,
                                                       apply_absolute_value=True))
                E.v(lambda e: e.tensor_max(gmax_col[:], gmax_col[:], stats[:, 4:5]))
                tc_ = E.s(lambda e, p=p, sl=sl: e.copy(xnTb[p][:, sl], xnT[p][:, sl]),
                          deps=[tb_] + W.get(f"xnTb{p}"))
                xn_toks.append(tb_)
                xnb_toks.append(tc_)
            W.set("m_b", [E.tok("v")])
            W.set("r_b", [E.tok("v")])
            xt_free[p] = [E.tok("v")]

            sp_tok = E.op("sy", f"sp{p}", lambda e, p=p, tc=tc: e.dma_start(
                out=xnT_sp[:, NDC * 512 * tc:NDC * 512 * (tc + 1)],
                in_=xnT[p][:]),
                deps=xn_toks + ([sp_prev[p]] if sp_prev[p] else []))
            sp_prev[p] = sp_tok
            W.set(f"xnT{p}", [sp_tok])

            vT_toks = []
            for co in range(NDC):
                mm = None
                for ci in range(NDC):
                    dep = ([t_wv] + xnb_toks + W.get("ps_v")) if ci == 0 else []
                    mm = E.t(lambda e, p=p, ci=ci, co=co: e.matmul(
                        ps_v[:], wv_sb[:, D * ci + 128 * co:D * ci + 128 * (co + 1)],
                        xnTb[p][:, 512 * ci:512 * (ci + 1)],
                        start=(ci == 0), stop=(ci == NDC - 1)), deps=dep)
                tv = E.s(lambda e, p=p, co=co: e.activation(
                    vT[p][:, 512 * co:512 * (co + 1)], ps_v[:], AF.Identity,
                    bias=bv_sb[:, co:co + 1]), deps=[mm] + W.get(f"vT{p}"))
                W.set("ps_v", [tv])
                vT_toks.append(tv)

            vp_last = []
            for tt in range(4):
                pp = tt % 2
                mm = None
                for ci in range(NDC):
                    for half in range(2):
                        dep = []
                        if ci == 0 and half == 0:
                            dep = [t_wo] + vT_toks + W.get("ps_vp")
                        mm = E.t(lambda e, p=p, ci=ci, tt=tt, half=half: e.matmul(
                            ps_vp[:, 512 * half:512 * (half + 1)],
                            vT[p][:, 512 * ci + 128 * tt:512 * ci + 128 * (tt + 1)],
                            wo_sb[:, D * ci + 512 * half:D * ci + 512 * (half + 1)],
                            start=(ci == 0), stop=(ci == NDC - 1)), deps=dep)
                tvp = E.v(lambda e, pp=pp: e.tensor_copy(vpb[pp][:], ps_vp[:]),
                          deps=[mm] + ([vo_prev[pp]] if vo_prev[pp] else []))
                W.set("ps_vp", [tvp])
                vo_prev[pp] = E.op("sy", f"vo{pp}", lambda e, pp=pp, tc=tc, tt=tt: e.dma_start(
                    out=v_own[128 * (4 * tc + tt):128 * (4 * tc + tt + 1), :], in_=vpb[pp][:]),
                    deps=[tvp])
                vp_last = [tvp]
            W.set(f"vT{p}", vp_last)

            for tt in range(4):
                ti = tc * 4 + tt
                mm = None
                for ci in range(NDC):
                    dep = ([t_wch] + xnb_toks + W.get("PSF")) if ci == 0 else []
                    mm = E.t(lambda e, p=p, ci=ci, tt=tt: e.matmul(
                        ps_q, xnTb[p][:, 512 * ci + 128 * tt:512 * ci + 128 * (tt + 1)],
                        wch_sb[:, ci:ci + 1],
                        start=(ci == 0), stop=(ci == NDC - 1)), deps=dep)
                tq = E.s(lambda e: e.activation(q_col[:], ps_q, AF.Tanh, bias=bch_sb[:, 0:1]),
                         deps=[mm] + W.get("q_col"))
                W.set("PSF", [tq])
                E.v(lambda e: e.tensor_mul(qpow[:], q_col[:], q_col[:]), deps=[tq] + W.get("phic"))
                E.v(lambda e: e.tensor_scalar_mul(phic[:, 1:2], q_col[:], CMAG[1]))
                E.v(lambda e: e.tensor_scalar_mul(phic[:, 2:3], qpow[:], CMAG[2]))
                E.v(lambda e: e.tensor_mul(qpow[:], qpow[:], q_col[:]))
                E.v(lambda e: e.tensor_scalar_mul(phic[:, 3:4], qpow[:], CMAG[3]))
                E.v(lambda e: e.tensor_mul(qpow[:], qpow[:], q_col[:]))
                tpc = E.v(lambda e: e.tensor_scalar_mul(phic[:, 4:5], qpow[:], CMAG[4]))
                W.set("q_col", [tpc])
                mm = None
                for ci in range(NDC):
                    dep = ([t_wsh] + xnb_toks + W.get("PSF")) if ci == 0 else []
                    mm = E.t(lambda e, p=p, ci=ci, tt=tt: e.matmul(
                        ps_sh, xnTb[p][:, 512 * ci + 128 * tt:512 * ci + 128 * (tt + 1)],
                        wsh_sb[:, 64 * ci:64 * (ci + 1)],
                        start=(ci == 0), stop=False), deps=dep)
                mm = E.t(lambda e: e.matmul(ps_sh, on_sb[:], bsh_sb[:], start=False, stop=True))
                tsh = E.v(lambda e: e.tensor_copy(shell[:], ps_sh), deps=[mm] + W.get("shell"))
                W.set("PSF", [tsh])
                tss = E.s(lambda e: e.activation(sq_scr[:, 0:64], shell[:], AF.Square,
                                                 accum_out=stats[:, 5:6]),
                          deps=[tsh] + W.get("sq_scr") + W.get("stats5"))
                tsq = E.s(lambda e: e.activation(stats[:, 6:7], stats[:, 5:6], AF.Sqrt))
                W.set("stats5", [tsq])
                W.set("sq_scr", [tss])
                E.v(lambda e: e.tensor_scalar_max(stats[:, 6:7], stats[:, 6:7], 1e-12), deps=[tsq])
                trn = E.v(lambda e: e.reciprocal(stats[:, 6:7], stats[:, 6:7]))
                tshn = E.v(lambda e: e.tensor_scalar_mul(shn[:, 0:64], shell[:], stats[:, 6:7]),
                           deps=W.get("shn"))
                W.set("shell", [tshn])
                tst = E.t(lambda e: e.transpose(ps_st, shn[:], id_sb[:]),
                          deps=[tshn] + W.get("PSF"))
                W.set("shn", [tst])
                tstc = E.v(lambda e: e.tensor_copy(shnT[:], ps_st), deps=[tst] + W.get("shnT"))
                W.set("PSF", [tstc])
                tpj = E.t(lambda e: e.matmul(ps_pj, shnT[:], om_sb[:], start=True, stop=True),
                          deps=[tstc, CL] + W.get("PSF"))
                W.set("shnT", [tpj])
                ta1 = E.v(lambda e: e.tensor_scalar(prt[:, 0:32], ps_pj, INV2PI, 64.75,
                                                    ALU.mult, ALU.add),
                          deps=[tpj] + W.get("prt"))
                ta2 = E.v(lambda e: e.tensor_scalar(prt[:, 32:64], ps_pj, INV2PI, 64.5,
                                                    ALU.mult, ALU.add))
                W.set("PSF", [ta2])
                E.v(lambda e: e.tensor_copy(prk[:], prt[:]))
                E.v(lambda e: e.scalar_tensor_tensor(prf[:], prt[:], -0.5, prk[:],
                                                     ALU.add, ALU.subtract))
                E.v(lambda e: e.tensor_single_scalar(prt[:], prf[:], -0.5, ALU.is_lt))
                tf = E.v(lambda e: e.tensor_add(prf[:], prf[:], prt[:]))
                W.set("prt", [tf])
                tphis = E.s(lambda e: e.activation(phis[:], prf[:], AF.Sin, scale=TWOPI),
                            deps=[tf] + W.get("phis"))
                tp_toks = []
                for m in range(5):
                    dep = [tphis, tpc] + W.get("tpb") if m == 0 else []
                    tp_toks.append(E.v(lambda e, m=m: e.tensor_scalar_mul(
                        tpb[:, 64 * m:64 * (m + 1)], phis[:], phic[:, m:m + 1]), deps=dep))
                W.set("phis", [tp_toks[-1]])
                W.set("phic", [tp_toks[-1]])
                ph_toks = []
                for k3 in range(3):
                    insl = tpb[:, 128 * k3:128 * (k3 + 1)] if k3 < 2 else tpb[:, 256:320]
                    ttr = E.t(lambda e, k3=k3, insl=insl: e.transpose(ps_tp[k3], insl, id_sb[:]),
                              deps=tp_toks + W.get("PSG"))
                    tcp = E.v(lambda e, k3=k3: e.tensor_copy(tpT[k3][:], ps_tp[k3]),
                              deps=[ttr] + W.get(f"tpT{k3}"))
                    W.set("PSG", [tcp])
                    ph_toks.append(tcp)
                W.set("tpb", [ph_toks[-1]])
                mm = None
                for k3 in range(3):
                    dep = (ph_toks + [CL] + W.get("PSG")) if k3 == 0 else []
                    mm = E.t(lambda e, k3=k3: e.matmul(ps_phi, wcs[k3][:], tpT[k3][:],
                                                       start=(k3 == 0), stop=(k3 == 2)), deps=dep)
                for k3 in range(3):
                    W.set(f"tpT{k3}", [mm])
                tphi = E.s(lambda e, ti=ti: e.copy(
                    phiT_buf[:, 128 * ti:128 * (ti + 1)], ps_phi), deps=[mm])
                tsq2 = E.s(lambda e: e.activation(phisq[:], ps_phi, AF.Square),
                           deps=W.get("phisq"))
                W.set("PSG", [tphi, tsq2])
                tnn = E.t(lambda e: e.matmul(ps_nn, onc_sb[:], phisq[:], start=True, stop=True),
                          deps=[tsq2, CL] + W.get("PSF"))
                W.set("phisq", [tnn])
                tnm = E.v(lambda e: e.reduce_max(fin_row[0:1, 0:1], ps_nn, axis=AX.X),
                          deps=[tnn] + W.get("fin_row"))
                W.set("PSF", [tnm])
                E.v(lambda e: e.tensor_max(nn_acc[:], nn_acc[:], fin_row[0:1, 0:1]))

        tgf = E.t(lambda e: e.transpose(ps_mr[0:1, 0:128], gmax_col[:], id_sb[:]),
                  deps=[E.tok("v"), CL] + W.get("ps_mr"))
        E.v(lambda e: e.tensor_copy(fin_row[:], ps_mr[0:1, 0:128]), deps=[tgf] + W.get("fin_row"))
        E.v(lambda e: e.reduce_max(mx_sb[0:1, 0:1], fin_row[:], axis=AX.X))
        tmxf = E.v(lambda e: e.tensor_copy(mx_sb[0:1, 1:2], nn_acc[:]))
        E.op("sy", "go", lambda e: e.dma_start(out=phiT_o, in_=phiT_buf[:]),
             deps=[E.tok("s")])
        E.op("sy", "go", lambda e: e.dma_start(out=mx_o, in_=mx_sb[:]),
             deps=[tmxf])

        blk = ctx.enter_context(nc.Block())
        waits = [E.tok("sp0"), E.tok("sp1"), E.tok("vo0"), E.tok("vo1"), E.tok("go")]
        finish(E, blk, extra_waits=waits)
    return nc



ROUND_SAFE = bool(os.environ.get("BASS_SIM_FP64"))


def emit_round_q(E, W, src_ap, inv_col, out_bf, rbuf, kbuf, fbuf, cbuf, deps, wkey):
    if not ROUND_SAFE:
        t1 = E.v(lambda e: e.tensor_scalar_mul(rbuf, src_ap, inv_col),
                 deps=list(deps) + W.get(wkey))
        E.v(lambda e: e.tensor_copy(kbuf, rbuf))
        t3 = E.v(lambda e: e.tensor_scalar(out_bf, kbuf, -128.0, 127.0, ALU.max, ALU.min))
        W.set(wkey, [t3])
        return t3
    t1 = E.v(lambda e: e.tensor_scalar(rbuf, src_ap, inv_col, 512.5, ALU.mult, ALU.add),
             deps=list(deps) + W.get(wkey))
    tk = E.s(lambda e: e.copy(kbuf, rbuf), deps=[t1] + W.get(wkey + "k"))
    t2 = E.v(lambda e: e.scalar_tensor_tensor(fbuf, rbuf, -0.5, kbuf, ALU.add, ALU.subtract),
             deps=[tk])
    E.v(lambda e: e.tensor_single_scalar(cbuf, fbuf, -0.5, ALU.is_lt))
    E.v(lambda e: e.tensor_add(fbuf, fbuf, cbuf))
    E.v(lambda e: e.scalar_tensor_tensor(cbuf, rbuf, -512.5, fbuf, ALU.add, ALU.subtract))
    t3 = E.v(lambda e: e.tensor_scalar(out_bf, cbuf, -128.0, 127.0, ALU.max, ALU.min))
    W.set(wkey, [t3])
    W.set(wkey + "k", [t2])
    return t3



def build_l2():
    nc = bass.Bass("TRN2", target_bir_lowering=False, dynamic_dma_scratch_size=2048)
    din = lambda n, s, d=F32: nc.dram_tensor(n, s, d, kind="ExternalInput").ap()
    dout = lambda n, s, d=F32: nc.dram_tensor(n, s, d, kind="ExternalOutput").ap()

    xnT_sp = din("xnT_sp", [128, NDC * TOK])
    vg = din("vg", [SEQ, D], BF16)
    phi_g = din("phi_g", [64, SEQ], BF16)
    phi_q = din("phi_q", [64, TOK], BF16)
    gx8 = din("gx8", [1, 8])
    nn8 = din("nn8", [1, 8])
    wg = din("wg", [IF, D])
    wu = din("wu", [IF, D])
    wgt = din("wgt", [128, NIC * D])
    wut = din("wut", [128, NIC * D])
    clip_x = din("clip_x", [1, 1])
    log_temp = din("log_temp", [1, 1])
    iota512 = din("iota512", [1, 512])
    qb16 = din("qb16", [1, NCH])
    maskS = din("maskS", [128, 4 * QW], BF16)
    b_out_bc = din("b_out_bc", [128, D])
    ident = din("ident", [128, 128])
    identb = din("identb", [128, 128], BF16)
    ones_r = din("ones_r", [1, 128])
    ones_kc = din("ones_kc", [128, 1], BF16)

    h_sp = dout("h_sp", [128, NIC * TOK])
    hx_o = dout("hx", [1, 1])
    oat_o = dout("oat", [TOK, D])

    NVC = 3

    from contextlib import ExitStack
    with ExitStack() as ctx:
        sb = lambda n, s, d=F32: ctx.enter_context(nc.sbuf_tensor(n, s, d))
        psa = lambda n, s: ctx.enter_context(nc.psum_tensor(n, s, F32))
        sem = lambda n: ctx.enter_context(nc.semaphore(n))

        gx_sb = sb("gx_sb", [1, 8])
        nn_sb = sb("nn_sb", [1, 8])
        cx_sb = sb("cx_sb", [1, 1])
        lt_sb = sb("lt_sb", [1, 1])
        io512 = sb("io512", [1, 512])
        qb_sb = sb("qb_sb", [1, NCH])
        mk_sb = sb("mk_sb", [128, 4 * QW], BF16)
        bo_sb = sb("bo_sb", [128, D])
        id_sb = sb("id_sb", [128, 128])
        on_sb = sb("on_sb", [1, 128])
        onk_sb = sb("onk_sb", [128, 1], BF16)
        vgc = sb("vgc", [128, NVC * D], BF16)

        srow = sb("srow", [1, 8])
        invx_col = sb("invx_col", [128, 1])
        sx_col = sb("sx_col", [128, 1])
        negc_col = sb("negc_col", [128, 1])
        invt = sb("invt", [1, 1])
        arow = sb("arow", [1, 256])
        abc = sb("abc", [64, 256])
        brow = sb("brow", [1, 512])
        segb = sb("segb", [1, 1])

        nx = sb("nx", [128, NDC * TOK], BF16)
        xns = sb("xns0", [128, NDC * 512])
        wob = [sb(f"wob{p}", [128, D]) for p in range(2)]
        swcol = sb("swcol", [128, 8])
        swcol2 = sb("swcol2", [128, 1])
        sgx = sb("sgx", [128, 2 * NIC])
        wtb = [sb(f"wtb{p}", [128, D]) for p in range(6)]
        invrow = [sb(f"invrow{p}", [1, D]) for p in range(2)]
        invbc = [sb(f"invbc{p}", [128, D]) for p in range(2)]
        ngt = [sb(f"ngt{p}", [128, D], BF16) for p in range(2)]
        nut = [sb(f"nut{p}", [128, D], BF16) for p in range(2)]
        hg = sb("hg", [128, 512])
        hu = sb("hu", [128, 512])
        hb = [sb(f"hb{p}", [128, 512]) for p in range(2)]
        hmax_col = sb("hmax_col", [128, 1])
        hrow = sb("hrow", [1, 128])
        hx_sb = sb("hx_sb", [1, 1])
        rb4 = sb("rb4", [128, D])
        kb4 = sb("kb4", [128, D], I32)
        fb4 = sb("fb4", [128, D if ROUND_SAFE else 1])
        cb4 = sb("cb4", [128, D if ROUND_SAFE else 1])

        ktil = [sb(f"ktil{p}", [64, SEQ], BF16) for p in range(2)]
        qt = [sb(f"qt{p}", [64, 256], BF16) for p in range(2)]
        phgs = sb("phgs", [64, 512], BF16)
        phqs = sb("phqs", [64, 256], BF16)
        vgb = [sb(f"vgb{p}", [128, D], BF16) for p in range(3)]
        atile = [sb(f"at{p}", [128, 256], BF16) for p in range(2)]
        rd2 = sb("rd2", [128, 4])
        rd = sb("rd", [128, 2])
        oatb = [sb(f"oatb{p}", [128, D]) for p in range(2)]

        PB1 = psa("PB1", [128, 512])
        PB2 = psa("PB2", [128, 512])
        PS = [psa("PSA", [128, 512]), psa("PSB", [128, 512])]
        PO1 = psa("PO1", [128, 1024])
        PO2 = psa("PO2", [128, 1024])

        E = Env(nc, ("sy", "v", "s", "t", "g"),
                ENG_CTRS + [("cl", 16), ("xs0", 16),
                            ("wo0", 16), ("wo1", 16),
                            ("wt0", 16), ("wt1", 16), ("wt2", 16),
                            ("wt3", 16), ("wt4", 16), ("wt5", 16),
                            ("hs0", 16), ("hs1", 16), ("ph0", 16), ("ph1", 16),
                            ("vg0", 16), ("vg1", 16), ("vg2", 16), ("oa0", 16), ("oa1", 16),
                            ("go", 16)],
                sem)
        W = War()

        def cload(dst, src):
            return E.op("sy", "cl", lambda e, d=dst, s_=src: e.dma_start(out=d, in_=s_))

        for dst, src in [(gx_sb[:], gx8), (nn_sb[:], nn8), (cx_sb[:], clip_x),
                         (lt_sb[:], log_temp), (io512[:], iota512),
                         (qb_sb[:], qb16), (mk_sb[:], maskS), (bo_sb[:], b_out_bc),
                         (id_sb[:], ident), (on_sb[:], ones_r), (onk_sb[:], ones_kc)]:
            cload(dst, src)
        for kt in range(NVC):
            cload(vgc[:, D * kt:D * (kt + 1)], vg[128 * kt:128 * (kt + 1), :])
        CL = E.tok("cl")

        t1 = E.v(lambda e: e.reduce_max(srow[0:1, 0:1], gx_sb[:], axis=AX.X), deps=[CL])
        E.v(lambda e: e.reduce_max(srow[0:1, 1:2], nn_sb[:], axis=AX.X))
        ts1 = E.s(lambda e: e.activation(srow[0:1, 2:3], cx_sb[:], AF.Sigmoid), deps=[CL])
        E.v(lambda e: e.tensor_mul(srow[0:1, 3:4], srow[0:1, 2:3], srow[0:1, 0:1]), deps=[ts1])
        E.v(lambda e: e.reciprocal(srow[0:1, 4:5], srow[0:1, 3:4]))
        tinvx = E.v(lambda e: e.tensor_scalar_mul(srow[0:1, 4:5], srow[0:1, 4:5], 127.0))
        tsx = E.v(lambda e: e.tensor_scalar_mul(srow[0:1, 3:4], srow[0:1, 3:4], 1.0 / 127.0))
        tsp0 = E.s(lambda e: e.activation(srow[0:1, 5:6], lt_sb[:], AF.Exp), deps=[CL])
        tsp1 = E.v(lambda e: e.tensor_scalar_add(srow[0:1, 5:6], srow[0:1, 5:6], 1.0), deps=[tsp0])
        tsp = E.s(lambda e: e.activation(srow[0:1, 5:6], srow[0:1, 5:6], AF.Ln), deps=[tsp1])
        E.v(lambda e: e.tensor_scalar_add(srow[0:1, 5:6], srow[0:1, 5:6], 0.1), deps=[tsp])
        tit = E.v(lambda e: e.reciprocal(invt[:], srow[0:1, 5:6]))
        E.v(lambda e: e.tensor_mul(srow[0:1, 6:7], srow[0:1, 1:2], invt[:]))
        tnegc = E.v(lambda e: e.tensor_scalar(srow[0:1, 6:7], srow[0:1, 6:7], 1.02, -1.0,
                                              ALU.mult, ALU.mult))
        def bcast_col(dst, src_ap, dep):
            tm = E.t(lambda e, d=dst, s_=src_ap: e.matmul(PB1[:, 0:1], on_sb[:], s_,
                                                          start=True, stop=True),
                     deps=[dep, CL] + W.get("PB1"))
            tc_ = E.v(lambda e, d=dst: e.tensor_copy(d, PB1[:, 0:1]), deps=[tm])
            W.set("PB1", [tc_])
            return tc_
        t_invx = bcast_col(invx_col[:], srow[0:1, 4:5], tinvx)
        t_sx = bcast_col(sx_col[:], srow[0:1, 3:4], tsx)
        t_negc = bcast_col(negc_col[:], srow[0:1, 6:7], tnegc)
        tar = E.s(lambda e: e.activation(arow[:], io512[0:1, 0:256], AF.Exp, scale=-DDEC), deps=[CL])
        tar2 = E.v(lambda e: e.tensor_scalar_mul(arow[:], arow[:], invt[0:1, 0:1]),
                   deps=[tar, tit])
        tba = E.t(lambda e: e.matmul(PB1[0:64, 0:256], on_sb[0:1, 0:64], arow[:],
                                     start=True, stop=True),
                  deps=[tar2, CL] + W.get("PB1"))
        t_abc = E.v(lambda e: e.tensor_copy(abc[:], PB1[0:64, 0:256]), deps=[tba])
        W.set("PB1", [t_abc])
        E.v(lambda e: e.memset(hmax_col[:], 0.0))

        qs_v = {}
        qs_p = {}
        qr = {}

        def vscale(ic, w):
            p2, p3 = ic % 2, ic % 3
            worig, wtr = (wg, wgt) if w == 0 else (wu, wut)
            scol = ic if w == 0 else NIC + ic
            sc = 4 * w + 2 * (ic % 2)
            ld = E.op("sy", f"wo{p2}", lambda e, p2=p2, worig=worig, ic=ic: e.dma_start(
                out=wob[p2][:], in_=worig[128 * ic:128 * (ic + 1), :]),
                deps=W.get(f"wob{p2}"))
            b6 = 3 * w + p3
            ldt = E.op("sy", f"wt{b6}", lambda e, b6=b6, wtr=wtr, ic=ic: e.dma_start(
                out=wtb[b6][:], in_=wtr[:, D * ic:D * (ic + 1)]),
                deps=W.get(f"wtb{b6}"))
            tr1 = E.v(lambda e, p2=p2, sc=sc: e.reduce_max(
                swcol[:, sc:sc + 1], wob[p2][:], axis=AX.X, apply_absolute_value=True),
                deps=[ld] + W.get(f"swc{w}{ic % 2}"))
            W.set(f"wob{p2}", [tr1])
            E.v(lambda e, scol=scol, sc=sc: e.tensor_scalar_mul(
                sgx[:, scol:scol + 1], swcol[:, sc:sc + 1], 1.0 / 127.0))
            E.v(lambda e, sc=sc: e.reciprocal(swcol[:, sc + 1:sc + 2], swcol[:, sc:sc + 1]))
            tiv = E.v(lambda e, sc=sc: e.tensor_scalar_mul(
                swcol[:, sc + 1:sc + 2], swcol[:, sc + 1:sc + 2], 127.0))
            tmuls = E.v(lambda e, scol=scol: e.tensor_mul(
                sgx[:, scol:scol + 1], sgx[:, scol:scol + 1], sx_col[:]), deps=[t_sx])
            qs_v[(ic, w)] = (ldt, tiv, tmuls)

        def pescale_T(ic, w):
            PQ, pqk = (PB1, "PB1") if w == 0 else (PB2, "PB2")
            sc = 4 * w + 2 * (ic % 2)
            ldt, tiv, tmuls = qs_v[(ic, w)]
            ttr = E.t(lambda e, PQ=PQ, sc=sc: e.transpose(
                PQ[0:1, 0:128], swcol[:, sc + 1:sc + 2], id_sb[:]),
                deps=[tiv, CL] + W.get(pqk))
            W.set(f"swc{w}{ic % 2}", [ttr])
            tcp = E.v(lambda e, w=w, PQ=PQ: e.tensor_copy(
                invrow[w][0:1, 0:128], PQ[0:1, 0:128]), deps=[ttr] + W.get(f"irow{w}"))
            for r8 in range(1, NDC):
                E.v(lambda e, w=w, r8=r8: e.tensor_copy(
                    invrow[w][0:1, 128 * r8:128 * (r8 + 1)], invrow[w][0:1, 0:128]))
            W.set(pqk, [tcp])
            W.set(f"irow{w}", [])
            qs_p[(ic, w)] = []
            return E.tok("v")

        def pescale_B(ic, w, half, prow):
            PQ, pqk = (PB1, "PB1") if w == 0 else (PB2, "PB2")
            tb = E.t(lambda e, PQ=PQ, w=w, half=half: e.matmul(
                PQ[:, 0:512], on_sb[:], invrow[w][0:1, 512 * half:512 * (half + 1)],
                start=True, stop=True),
                deps=[prow, CL] + W.get(pqk))
            te = E.v(lambda e, PQ=PQ, w=w, half=half: e.tensor_copy(
                invbc[w][:, 512 * half:512 * (half + 1)], PQ[:, 0:512]),
                deps=[tb] + (W.get(f"ibw{w}") if half == 0 else []))
            W.set(pqk, [te])
            W.add(f"irow{w}", tb)
            qs_p[(ic, w)].append(te)

        def pescale_full(ic):
            for w in range(2):
                prow = pescale_T(ic, w)
                pescale_B(ic, w, 0, prow)
                pescale_B(ic, w, 1, prow)

        def qround(ic, w):
            p2, p3 = ic % 2, ic % 3
            dst, dname = (ngt, "ngt") if w == 0 else (nut, "nut")
            ldt, tiv, tmuls = qs_v[(ic, w)]
            b6 = 3 * w + p3
            tq1 = E.v(lambda e, b6=b6, w=w: e.tensor_mul(rb4[:], wtb[b6][:], invbc[w][:]),
                      deps=[ldt] + qs_p[(ic, w)] + W.get("rq"))
            W.set(f"wtb{b6}", [tq1])
            W.set(f"ibw{w}", [tq1])
            if ROUND_SAFE:
                tk = E.s(lambda e: e.copy(kb4[:], rb4[:]), deps=[tq1] + W.get("rqk"))
                t2 = E.v(lambda e: e.scalar_tensor_tensor(fb4[:], rb4[:], -0.5, kb4[:],
                                                          ALU.add, ALU.subtract), deps=[tk])
                E.v(lambda e: e.tensor_single_scalar(cb4[:], fb4[:], -0.5, ALU.is_lt))
                E.v(lambda e: e.tensor_add(fb4[:], fb4[:], cb4[:]))
                E.v(lambda e: e.tensor_sub(rb4[:], rb4[:], fb4[:]))
                tq3 = E.v(lambda e, p2=p2, dst=dst: e.tensor_scalar(
                    dst[p2][:], rb4[:], -128.0, 127.0, ALU.max, ALU.min),
                    deps=W.get(f"{dname}{p2}"))
                W.set("rqk", [t2])
            else:
                E.v(lambda e: e.tensor_copy(kb4[:], rb4[:]))
                tq3 = E.v(lambda e, p2=p2, dst=dst: e.tensor_scalar(
                    dst[p2][:], kb4[:], -128.0, 127.0, ALU.max, ALU.min),
                    deps=W.get(f"{dname}{p2}"))
            W.set("rq", [tq3])
            qr[(ic, w)] = (tq3, tmuls)

        NXR_TC = {}
        xs_state = {"free": []}

        def emit_nx(tc):
            ld = E.op("sy", "xs0", lambda e, tc=tc: e.dma_start(
                out=xns[:], in_=xnT_sp[:, NDC * 512 * tc:NDC * 512 * (tc + 1)]),
                deps=xs_state["free"])
            for dd in range(NDC // 2):
                src = xns[:, 1024 * dd:1024 * (dd + 1)]
                dstp = nx[:, 4096 * tc + 1024 * dd:4096 * tc + 1024 * (dd + 1)]
                emit_round_q(E, W, src, invx_col[:, 0:1], dstp,
                             rb4[:], kb4[:], fb4[:], cb4[:], [ld, t_invx], "rq")
            xs_state["free"] = [E.tok("v")]
            NXR_TC[tc] = E.tok("v")

        hs_prev = [None, None]

        def emit_fmms(k, pe):
            p = k % 2
            tng, tmulsg = qr[(k, 0)]
            tnu, tmulsu = qr[(k, 1)]
            prow_g = prow_u = None
            if pe is not None:
                prow_g = pescale_T(pe, 0)
                prow_u = pescale_T(pe, 1)
            mg = mu = None
            for tc in range(NT512):
                pb = tc % 2
                for dc in range(NDC):
                    depg = ([tng, NXR_TC[tc]] + W.get("PB1")) if dc == 0 else []
                    mg = E.t(lambda e, p=p, dc=dc, tc=tc: e.matmul(
                        PB1[:], ngt[p][:, 128 * dc:128 * (dc + 1)],
                        nx[:, 4096 * tc + 512 * dc:4096 * tc + 512 * (dc + 1)],
                        start=(dc == 0), stop=(dc == NDC - 1)), deps=depg)
                for dc in range(NDC):
                    depu = ([tnu, NXR_TC[tc]] + W.get("PB2")) if dc == 0 else []
                    mu = E.t(lambda e, p=p, dc=dc, tc=tc: e.matmul(
                        PB2[:], nut[p][:, 128 * dc:128 * (dc + 1)],
                        nx[:, 4096 * tc + 512 * dc:4096 * tc + 512 * (dc + 1)],
                        start=(dc == 0), stop=(dc == NDC - 1)), deps=depu)
                tg_ = E.s(lambda e, k=k: e.activation(hg[:], PB1[:], AF.Silu,
                                                      scale=sgx[:, k:k + 1]),
                          deps=[mg, tmulsg] + W.get("hg"))
                W.set("PB1", [tg_])
                tu_ = E.v(lambda e, k=k: e.tensor_scalar_mul(hu[:], PB2[:],
                                                             sgx[:, NIC + k:NIC + k + 1]),
                          deps=[mu, tmulsu] + W.get("hu"))
                W.set("PB2", [tu_])
                th_ = E.v(lambda e, pb=pb: e.tensor_mul(hb[pb][:], hg[:], hu[:]),
                          deps=[tg_, tu_] + ([hs_prev[pb]] if hs_prev[pb] else []))
                W.set("hg", [th_])
                W.set("hu", [th_])
                E.v(lambda e, pb=pb: e.reduce_max(swcol2[:, 0:1], hb[pb][:], axis=AX.X,
                                                  apply_absolute_value=True))
                E.v(lambda e: e.tensor_max(hmax_col[:], hmax_col[:], swcol2[:, 0:1]))
                hs_prev[pb] = E.op("sy", f"hs{pb}", lambda e, pb=pb, k=k, tc=tc: e.dma_start(
                    out=h_sp[:, NIC * 512 * tc + 512 * k:NIC * 512 * tc + 512 * (k + 1)],
                    in_=hb[pb][:]),
                    deps=[th_])
                if pe is not None and tc < 4:
                    pescale_B(pe, tc // 2, tc % 2, prow_g if tc < 2 else prow_u)
            W.set(f"ngt{p}", [mg])
            W.set(f"nut{p}", [mu])

        def emit_item(it):
            kind, a = it
            if kind == "nx":
                emit_nx(a)
            else:
                k = a
                if k + 1 < NIC:
                    qround(k + 1, 0)
                    qround(k + 1, 1)
                if k + 3 < NIC:
                    vscale(k + 3, 0)
                    vscale(k + 3, 1)
                emit_fmms(k, (k + 2) if k + 2 < NIC else None)

        def emit_setup(j):
            bf = j % 2
            ob_ = 1 - bf
            ldq = E.op("sy", "ph1", lambda e, j=j: e.dma_start(
                out=phqs[:], in_=phi_q[:, 256 * j:256 * (j + 1)]), deps=W.get("phqs"))
            tqt = E.v(lambda e, bf=bf: e.tensor_mul(qt[bf][:], phqs[:], abc[:]),
                      deps=[t_abc, ldq] + W.get(f"qtr{bf}"))
            W.set("phqs", [tqt])
            W.set(f"qtr{bf}", [])
            if j > 0:
                E.v(lambda e, j=j, bf=bf, ob_=ob_: e.tensor_scalar_mul(
                    ktil[bf][:, 0:512 * j], ktil[ob_][:, 0:512 * j], 2.1175823681357508e-07),
                    deps=W.get(f"ktr{bf}"))
            ldp = E.op("sy", "ph0", lambda e, j=j: e.dma_start(
                out=phgs[:], in_=phi_g[:, 512 * j:512 * (j + 1)]), deps=W.get("phgs"))
            tsg = E.v(lambda e, j=j: e.tensor_scalar_mul(
                segb[:], qb_sb[0:1, j:j + 1], -DDEC), deps=[CL] + W.get("segb"))
            tbr = E.s(lambda e, j=j: e.activation(brow[:], io512[:], AF.Exp,
                                                  scale=DDEC, bias=segb[0:1, 0:1]),
                      deps=[tsg, CL] + W.get("brow"))
            W.set("segb", [tbr])
            tbk = E.t(lambda e: e.matmul(PB1[0:64, 0:512], on_sb[0:1, 0:64], brow[:],
                                         start=True, stop=True),
                      deps=[tbr, CL] + W.get("PB1"))
            W.set("brow", [tbk])
            tkt = E.v(lambda e, j=j, bf=bf: e.tensor_mul(
                ktil[bf][:, 512 * j:512 * (j + 1)], phgs[:], PB1[0:64, 0:512]),
                deps=[tbk, ldp] + W.get(f"ktr{bf}"))
            W.set("PB1", [tkt])
            W.set("phgs", [tkt])
            W.set(f"ktr{bf}", [])
            W.set(f"rdy{bf}", [E.tok("v")])

        def emit_chunk(j):
            bf = j % 2
            nkt = 4 * (j + 1)
            t_rdy = W.get(f"rdy{bf}")

            def emit_S(kt, bf=bf, t_rdy=t_rdy):
                h = kt % 2
                tsm = E.t(lambda e, kt=kt, bf=bf, h=h: e.matmul(
                    PS[h][:, 0:256], ktil[bf][:, 128 * kt:128 * (kt + 1)], qt[bf][:],
                    start=True, stop=True),
                    deps=t_rdy + W.get(f"PS{h}"))
                W.add(f"ktr{bf}", tsm)
                W.add(f"qtr{bf}", tsm)
                return tsm

            def emit_vld(kt):
                if kt < NVC:
                    return CL
                pv = kt % 3
                return E.op("s", f"vg{pv}", lambda e, kt=kt, pv=pv: e.dma_start(
                    out=vgb[pv][:], in_=vg[128 * kt:128 * (kt + 1), :]),
                    deps=W.get(f"vgb{pv}"))

            s_toks = {0: emit_S(0)}
            v_toks = {0: emit_vld(0)}
            av_done = []
            for kt in range(nkt):
                pa = kt % 2
                pv = kt % 3
                h = kt % 2
                if kt + 1 < nkt:
                    v_toks[kt + 1] = emit_vld(kt + 1)
                    s_toks[kt + 1] = emit_S(kt + 1)
                if kt == 0 and j + 1 < NCH:
                    emit_setup(j + 1)
                tsm = s_toks[kt]
                vld = v_toks[kt]
                m = kt - (nkt - 4)
                if m >= 0:
                    tsm = E.v(lambda e, h=h, m=m: e.tensor_tensor(
                        PS[h][:, 0:256], PS[h][:, 0:256], mk_sb[:, 256 * m:256 * (m + 1)],
                        ALU.min), deps=[tsm, CL])
                tex = E.s(lambda e, pa=pa, h=h: e.activation(
                    atile[pa][:], PS[h][:, 0:256], AF.Exp, bias=negc_col[:, 0:1]),
                    deps=[tsm, t_negc] + W.get(f"at{pa}"))
                W.set(f"PS{h}", [tex])
                first = (kt == 0)
                last = (kt == nkt - 1)
                rhs_v = vgc if kt < NVC else vgb[pv]
                rbase = D * kt if kt < NVC else 0
                mm_toks = []
                av_mms = []
                for sl in range(2):
                    po = PO1 if sl == 0 else PO2
                    for half in range(2):
                        dep = [tex, vld]
                        if first and half == 0:
                            dep = dep + W.get(f"PO{sl}")
                        mm = E.t(lambda e, sl=sl, half=half, po=po, pa=pa,
                                 rhs_v=rhs_v, rbase=rbase,
                                 first=first, last=last: e.matmul(
                            po[:, 512 * half:512 * (half + 1)],
                            atile[pa][:, 128 * sl:128 * (sl + 1)],
                            rhs_v[:, rbase + 512 * half:rbase + 512 * (half + 1)],
                            start=first, stop=last), deps=dep)
                        mm_toks.append(mm)
                        av_mms.append(mm)
                    col = (256 if sl == 0 else 320) + kt // 2
                    dmm = E.t(lambda e, sl=sl, pa=pa, h=h, col=col: e.matmul(
                        PS[h][:, col:col + 1], atile[pa][:, 128 * sl:128 * (sl + 1)],
                        onk_sb[:], start=True, stop=True),
                        deps=(W.get(f"PSD{h}") if kt < 2 else []))
                    mm_toks.append(dmm)
                if kt >= NVC:
                    W.set(f"vgb{pv}", list(av_mms))
                W.set(f"at{pa}", mm_toks)
                if last:
                    av_done = mm_toks
            ne = nkt // 2
            for h2 in range(2):
                t0 = E.v(lambda e, h2=h2, ne=ne: e.reduce_sum(
                    rd2[:, h2:h2 + 1], PS[h2][:, 256:256 + ne], axis=AX.X), deps=av_done)
                t1 = E.v(lambda e, h2=h2, ne=ne: e.reduce_sum(
                    rd2[:, 2 + h2:3 + h2], PS[h2][:, 320:320 + ne], axis=AX.X))
                W.set(f"PSD{h2}", [t0, t1])
                W.add(f"PS{h2}", t0)
                W.add(f"PS{h2}", t1)
            E.v(lambda e: e.tensor_add(rd2[:, 0:1], rd2[:, 0:1], rd2[:, 1:2]),
                deps=W.get("rd"))
            E.v(lambda e: e.tensor_add(rd2[:, 2:3], rd2[:, 2:3], rd2[:, 3:4]))
            E.v(lambda e: e.reciprocal(rd[:, 0:1], rd2[:, 0:1]))
            trd = E.v(lambda e: e.reciprocal(rd[:, 1:2], rd2[:, 2:3]))
            oa_toks = []
            for sl in range(2):
                po = PO1 if sl == 0 else PO2
                pb = sl
                tout = E.v(lambda e, sl=sl, po=po, pb=pb: e.scalar_tensor_tensor(
                    oatb[pb][:], po[:], rd[:, sl:sl + 1], bo_sb[:], ALU.mult, ALU.add),
                    deps=[trd, CL] + ([oa_prev[pb]] if oa_prev[pb] else []))
                W.set(f"PO{sl}", [tout])
                oa_prev[pb] = E.op("sy", f"oa{pb}", lambda e, pb=pb, j=j, sl=sl: e.dma_start(
                    out=oat_o[256 * j + 128 * sl:256 * j + 128 * (sl + 1), :], in_=oatb[pb][:]),
                    deps=[tout])
                oa_toks.append(tout)
            W.set("rd", oa_toks)

        oa_prev = [None, None]

        emit_setup(0)
        for ic in range(2):
            vscale(ic, 0)
            vscale(ic, 1)
        pescale_full(0)
        qround(0, 0)
        qround(0, 1)
        pescale_full(1)
        vscale(2, 0)
        vscale(2, 1)

        items = [("nx", t) for t in range(NT512)] + [("f", kk) for kk in range(NIC)]
        ndone = 0
        for j in range(NCH):
            target = min(len(items), int(round(len(items) * j / 14.0)))
            while ndone < target:
                emit_item(items[ndone])
                ndone += 1
            emit_chunk(j)
        while ndone < len(items):
            emit_item(items[ndone])
            ndone += 1

        thf = E.t(lambda e: e.transpose(PB1[0:1, 0:128], hmax_col[:], id_sb[:]),
                  deps=[E.tok("v"), CL] + W.get("PB1"))
        E.v(lambda e: e.tensor_copy(hrow[:], PB1[0:1, 0:128]), deps=[thf])
        W.set("PB1", [E.tok("v")])
        thx = E.v(lambda e: e.reduce_max(hx_sb[:], hrow[:], axis=AX.X))
        E.op("sy", "go", lambda e: e.dma_start(out=hx_o, in_=hx_sb[:]), deps=[thx])

        blk = ctx.enter_context(nc.Block())
        waits = [E.tok("hs0"), E.tok("hs1"), E.tok("oa0"), E.tok("oa1"), E.tok("go")]
        finish(E, blk, extra_waits=waits)
    return nc



def build_l3():
    nc = bass.Bass("TRN2", target_bir_lowering=False, dynamic_dma_scratch_size=2048)
    din = lambda n, s, d=F32: nc.dram_tensor(n, s, d, kind="ExternalInput").ap()
    dout = lambda n, s, d=F32: nc.dram_tensor(n, s, d, kind="ExternalOutput").ap()

    h_sp = din("h_sp", [128, NIC * TOK])
    hx8 = din("hx8", [1, 8])
    x_own = din("x_own", [TOK, D])
    oat = din("oat", [TOK, D])
    wd = din("wd", [D, IF])
    clip_d = din("clip_d", [1, 1])
    ident = din("ident", [128, 128])
    identb = din("identb", [128, 128], BF16)
    ones_r = din("ones_r", [1, 128])

    y_o = dout("y_out", [TOK, D])

    from contextlib import ExitStack
    with ExitStack() as ctx:
        sb = lambda n, s, d=F32: ctx.enter_context(nc.sbuf_tensor(n, s, d))
        psa = lambda n, s: ctx.enter_context(nc.psum_tensor(n, s, F32))
        sem = lambda n: ctx.enter_context(nc.semaphore(n))

        hx_sb = sb("hx_sb", [1, 8])
        cd_sb = sb("cd_sb", [1, 1])
        id_sb = sb("id_sb", [128, 128])
        idb_sb = sb("idb_sb", [128, 128], BF16)
        on_sb = sb("on_sb", [1, 128])
        srow = sb("srow", [1, 8])
        invh_col = sb("invh_col", [128, 1])
        sdrow = sb("sdrow", [1, D])
        sd_cols = sb("sd_cols", [128, NDC])
        sd_rows = sb("sd_rows", [NDC, 128])
        sdown_b = sb("sdown_b", [128, D])
        wdb = [sb("wdb0", [128, IF])]
        wdb = [wdb[0], wdb[0]]
        swc = sb("swc", [128, 1])
        ndT2 = sb("ndT2", [128, NIC * D], BF16)
        hsball = sb("hsball", [128, NIC * 512])
        nh = [sb(f"nh{p}", [128, NIC * 512], BF16) for p in range(2)]
        rb = sb("rb", [128, IF])
        kb = sb("kb", [128, IF], I32)
        fb = sb("fb", [128, IF if ROUND_SAFE else 1])
        cb = sb("cb", [128, IF if ROUND_SAFE else 1])
        xb = [sb(f"xb{p}", [128, D]) for p in range(2)]
        ob = [sb(f"ob{p}", [128, D]) for p in range(2)]
        yb = [sb(f"yb{p}", [128, D]) for p in range(2)]

        PY = [psa("PY0", [128, 1024]), psa("PY1", [128, 1024])]
        PT = psa("PT", [128, 512])

        E = Env(nc, ("sy", "v", "s", "t", "g"),
                ENG_CTRS + [("cl", 16), ("wd0", 16), ("hs0", 16),
                            ("xb0", 16), ("xb1", 16), ("ob0", 16), ("ob1", 16),
                            ("yo0", 16), ("yo1", 16), ("rg", 16)],
                sem)
        W = War()

        def cload(dst, src):
            return E.op("sy", "cl", lambda e, d=dst, s_=src: e.dma_start(out=d, in_=s_))

        for dst, src in [(hx_sb[:], hx8), (cd_sb[:], clip_d), (id_sb[:], ident),
                         (idb_sb[:], identb), (on_sb[:], ones_r)]:
            cload(dst, src)
        CL = E.tok("cl")

        E.v(lambda e: e.reduce_max(srow[0:1, 0:1], hx_sb[:], axis=AX.X), deps=[CL])
        ts1 = E.s(lambda e: e.activation(srow[0:1, 1:2], cd_sb[:], AF.Sigmoid), deps=[CL])
        E.v(lambda e: e.tensor_mul(srow[0:1, 2:3], srow[0:1, 1:2], srow[0:1, 0:1]), deps=[ts1])
        E.v(lambda e: e.reciprocal(srow[0:1, 3:4], srow[0:1, 2:3]))
        tih = E.v(lambda e: e.tensor_scalar_mul(srow[0:1, 3:4], srow[0:1, 3:4], 127.0))
        tsh = E.v(lambda e: e.tensor_scalar_mul(srow[0:1, 2:3], srow[0:1, 2:3], 1.0 / 127.0))
        tm = E.t(lambda e: e.matmul(PT[:, 0:1], on_sb[:], srow[0:1, 3:4], start=True, stop=True),
                 deps=[tih, CL])
        tivh = E.v(lambda e: e.tensor_copy(invh_col[:], PT[:, 0:1]), deps=[tm])
        W.set("PT", [tivh])

        for hc in range(NDC):
            p = hc % 2
            ld = E.op("sy", "wd0", lambda e, hc=hc, p=p: e.dma_start(
                out=wdb[p][:], in_=wd[128 * hc:128 * (hc + 1), :]), deps=W.get("wdb0"))
            E.v(lambda e, p=p: e.reduce_max(swc[:], wdb[p][:], axis=AX.X,
                                            apply_absolute_value=True), deps=[ld])
            E.v(lambda e, hc=hc: e.tensor_scalar_mul(sd_cols[:, hc:hc + 1], swc[:], 1.0 / 127.0))
            E.v(lambda e: e.reciprocal(swc[:], swc[:]))
            tiv = E.v(lambda e: e.tensor_scalar_mul(swc[:], swc[:], 127.0))
            tnq = emit_round_q(E, W, wdb[p][:], swc[:], rb[:],
                               rb[:], kb[:], fb[:], cb[:], [tiv], "rq")
            W.add("wdb0", tnq)
            for ic in range(NIC):
                ttr = E.t(lambda e, ic=ic: e.transpose(PT[:, 128:256], rb[:, 128 * ic:128 * (ic + 1)],
                                                       id_sb[:]),
                          deps=[tnq, CL] + W.get("PT"))
                tev = E.s(lambda e, ic=ic, hc=hc: e.copy(
                    ndT2[:, D * ic + 128 * hc:D * ic + 128 * (hc + 1)], PT[:, 128:256]),
                    deps=[ttr] + W.get("ndT2"))
                W.set("PT", [tev])
            W.add("rq", E.tok("s"))
            W.add("rq", E.tok("t"))

        tpk = E.t(lambda e: e.transpose(PT[0:NDC, 128:256], sd_cols[:], id_sb[:]),
                  deps=[E.tok("v"), CL] + W.get("PT"))
        tcp = E.v(lambda e: e.tensor_copy(sd_rows[:], PT[0:NDC, 128:256]), deps=[tpk])
        W.set("PT", [tcp])
        trg = E.op("g", "rg", lambda e: e.dma_start(out=sdrow[:], in_=sd_rows[:]), deps=[tcp])
        tsc = E.v(lambda e: e.tensor_scalar_mul(sdrow[:], sdrow[:], srow[0:1, 2:3]),
                  deps=[trg, tsh])
        bt = []
        for half in range(2):
            tbm = E.t(lambda e, half=half: e.matmul(
                PT[:], on_sb[:], sdrow[0:1, 512 * half:512 * (half + 1)],
                start=True, stop=True), deps=[tsc, CL] + W.get("PT"))
            tbe = E.v(lambda e, half=half: e.tensor_copy(
                sdown_b[:, 512 * half:512 * (half + 1)], PT[:]), deps=[tbm])
            W.set("PT", [tbe])
            bt.append(tbe)

        HW = NIC * 512

        def emit_hload(tc, hh):
            return E.op("sy", "hs0", lambda e, tc=tc, hh=hh: e.dma_start(
                out=hsball[:, 5632 * hh:5632 * (hh + 1)],
                in_=h_sp[:, HW * tc + 5632 * hh:HW * tc + 5632 * (hh + 1)]),
                deps=W.get(f"hsball{hh}"))

        def emit_rounds(tc, pn, lds):
            toks = []
            for g4 in range(4):
                hh = g4 // 2
                sl4 = slice(IF * g4, IF * (g4 + 1))
                tq = emit_round_q(E, W, hsball[:, sl4], invh_col[:, 0:1],
                                  nh[pn][:, sl4],
                                  rb[:], kb[:], fb[:], cb[:],
                                  [lds[hh], tivh] + W.get(f"nhw{pn}"), "rq")
                toks.append(tq)
            W.set("hsball0", [toks[1]])
            W.set("hsball1", [toks[3]])
            return toks

        yo_prev = [None, None]
        lds = [emit_hload(0, 0), emit_hload(0, 1)]
        nh_toks = {0: emit_rounds(0, 0, lds)}
        for tc in range(NT512):
            pn = tc % 2
            if tc + 1 < NT512:
                lds = [emit_hload(tc + 1, 0), emit_hload(tc + 1, 1)]
                nh_toks[tc + 1] = emit_rounds(tc + 1, (tc + 1) % 2, lds)
            for tt in range(4):
                pp = tt % 2
                mm = None
                for ic in range(NIC):
                    for half in range(2):
                        dep = []
                        if ic == 0 and half == 0:
                            dep = nh_toks[tc] + W.get(f"PY{pp}")
                        mm = E.t(lambda e, ic=ic, tt=tt, half=half, pp=pp, pn=pn: e.matmul(
                            PY[pp][:, 512 * half:512 * (half + 1)],
                            nh[pn][:, 512 * ic + 128 * tt:512 * ic + 128 * (tt + 1)],
                            ndT2[:, D * ic + 512 * half:D * ic + 512 * (half + 1)],
                            start=(ic == 0), stop=(ic == NIC - 1)),
                            deps=dep)
                W.set(f"nh_mm{pp}", [mm])
                ti = tc * 4 + tt
                xld = E.op("sy", f"xb{pp}", lambda e, ti=ti, pp=pp: e.dma_start(
                    out=xb[pp][:], in_=x_own[128 * ti:128 * (ti + 1), :]), deps=W.get(f"xb{pp}"))
                old = E.op("sy", f"ob{pp}", lambda e, ti=ti, pp=pp: e.dma_start(
                    out=ob[pp][:], in_=oat[128 * ti:128 * (ti + 1), :]), deps=W.get(f"ob{pp}"))
                tyy = E.v(lambda e, pp=pp: e.tensor_mul(yb[pp][:], PY[pp][:], sdown_b[:]),
                          deps=[mm] + bt + ([yo_prev[pp]] if yo_prev[pp] else []))
                W.set(f"PY{pp}", [tyy])
                E.v(lambda e, pp=pp: e.tensor_add(yb[pp][:], yb[pp][:], xb[pp][:]),
                    deps=[xld])
                tfin = E.v(lambda e, pp=pp: e.tensor_add(yb[pp][:], yb[pp][:], ob[pp][:]),
                           deps=[old])
                W.set(f"xb{pp}", [tfin])
                W.set(f"ob{pp}", [tfin])
                yo_prev[pp] = E.op("sy", f"yo{pp}", lambda e, ti=ti, pp=pp: e.dma_start(
                    out=y_o[128 * ti:128 * (ti + 1), :], in_=yb[pp][:]), deps=[tfin])
            W.set(f"nhw{pn}", W.get(f"nh_mm0") + W.get(f"nh_mm1"))

        blk = ctx.enter_context(nc.Block())
        waits = [E.tok("yo0"), E.tok("yo1")]
        finish(E, blk, extra_waits=waits)
    return nc



def _f32(a):
    return np.ascontiguousarray(a, dtype=np.float32)


def own_token_index(role):
    idx = []
    for j in range(NCH):
        g = (2 * j + role) * QW
        idx.extend(range(g, g + QW))
    return np.asarray(idx)


def blocked_T(a):
    d, n = a.shape
    return _f32(np.concatenate([a[128 * c:128 * (c + 1), :] for c in range(d // 128)], axis=1))


def mask_tiles(role):
    out = np.empty((128, 4 * QW), dtype=np.float32)
    for m in range(4):
        delta = 128 * m - role * QW
        k = np.arange(128)[:, None]
        qq = np.arange(QW)[None, :]
        out[:, m * QW:(m + 1) * QW] = np.where(qq < (k + delta), BIGNEG, BIGPOS)
    return out


def stage_l1_inputs(inputs):
    x = np.asarray(inputs["L0_attn_x"])
    wv = np.asarray(inputs["L0_attn_W_v"])
    wout = np.asarray(inputs["L0_attn_W_out"])
    wsh = np.asarray(inputs["L0_attn_W_shell"])
    wch = np.asarray(inputs["L0_attn_W_charge"])
    om = np.asarray(inputs["L0_attn_omega"])
    rffb = np.asarray(inputs["L0_attn_rff_bias"])
    wcomb = np.asarray(inputs["L0_attn_W_combine"])
    bsh = np.asarray(inputs["L0_attn_b_shell"])
    bch = np.asarray(inputs["L0_attn_b_charge"])
    bv = np.asarray(inputs["L0_attn_b_v"])

    shared = {
        "wv_t": blocked_T(_f32(wv).T),
        "wout_t": blocked_T(_f32(wout).T),
        "wsh_t": blocked_T(_f32(wsh).T),
        "wch_t": blocked_T(_f32(wch).T),
        "bsh_row": _f32(bsh.reshape(1, 64)),
        "bch_col": _f32(np.tile(np.asarray(bch).reshape(1, 1), (128, 1))),
        "bv_cols": _f32(bv.reshape(NDC, 128).T),
        "omega_aug": _f32(np.concatenate([_f32(om).T, rffb.reshape(1, 32)], axis=0)),
        "wc0": _f32(_f32(wcomb).T[0:128, :]),
        "wc1": _f32(_f32(wcomb).T[128:256, :]),
        "wc2": _f32(_f32(wcomb).T[256:320, :]),
        "ident": np.eye(128, dtype=np.float32),
        "ones_r": np.ones((1, 128), dtype=np.float32),
        "ones_c": np.ones((64, 1), dtype=np.float32),
    }
    maps = []
    for c in range(NCORE):
        s, r = c >> 1, c & 1
        rows = _f32(x[s][own_token_index(r)])
        m = dict(shared)
        m["x_own"] = rows
        m["xT_own"] = blocked_T(_f32(rows.T))
        maps.append(m)
    return maps


def stage_l2_inputs(inputs, r1):
    import ml_dtypes
    bf = ml_dtypes.bfloat16
    wg = _f32(np.asarray(inputs["L0_ffn_w_gate"]))
    wu = _f32(np.asarray(inputs["L0_ffn_w_up"]))
    clip_x = _f32(np.asarray(inputs["L0_ffn_clip_x"]).reshape(1, 1))
    log_temp = _f32(np.asarray(inputs["L0_attn_log_temp"]).reshape(1, 1))
    bout = np.asarray(inputs["L0_attn_b_out"])

    gx8 = np.concatenate([r1[c]["mx"][0:1, 0:1] for c in range(NCORE)], axis=1)
    nn8 = np.concatenate([r1[c]["mx"][0:1, 1:2] for c in range(NCORE)], axis=1)

    def icmajor_T(w):
        wt = _f32(w).T
        out = np.empty((128, NIC * D), dtype=np.float32)
        for ic in range(NIC):
            for dc in range(NDC):
                out[:, D * ic + 128 * dc:D * ic + 128 * (dc + 1)] = \
                    wt[128 * dc:128 * (dc + 1), 128 * ic:128 * (ic + 1)]
        return out

    shared = {
        "gx8": _f32(gx8), "nn8": _f32(nn8), "wg": wg, "wu": wu,
        "wgt": icmajor_T(wg), "wut": icmajor_T(wu),
        "clip_x": clip_x, "log_temp": log_temp,
        "iota512": np.arange(512, dtype=np.float32).reshape(1, 512),
        "b_out_bc": _f32(np.tile(bout.reshape(1, D), (128, 1))),
        "ident": np.eye(128, dtype=np.float32),
        "identb": np.eye(128, dtype=bf),
        "ones_r": np.ones((1, 128), dtype=np.float32),
        "ones_kc": np.ones((128, 1), dtype=bf),
    }
    maps = []
    for c in range(NCORE):
        s, r = c >> 1, c & 1
        va = r1[2 * s]["v_own"]
        vb = r1[2 * s + 1]["v_own"]
        vg = np.empty((SEQ, D), dtype=va.dtype)
        ia = own_token_index(0)
        ib = own_token_index(1)
        vg[ia] = va
        vg[ib] = vb
        pa = r1[2 * s]["phiT"]
        pb = r1[2 * s + 1]["phiT"]
        pg = np.empty((64, SEQ), dtype=pa.dtype)
        pg[:, ia] = pa
        pg[:, ib] = pb
        m = dict(shared)
        m["vg"] = np.ascontiguousarray(vg)
        m["phi_g"] = np.ascontiguousarray(pg)
        m["phi_q"] = np.ascontiguousarray(r1[c]["phiT"])
        m["xnT_sp"] = r1[c]["xnT_sp"]
        m["qb16"] = np.asarray([(2 * j + r) * QW for j in range(NCH)],
                               dtype=np.float32).reshape(1, NCH)
        m["maskS"] = mask_tiles(r).astype(ml_dtypes.bfloat16)
        maps.append(m)
    return maps


def stage_l3_inputs(inputs, maps1, r2):
    wd = _f32(np.asarray(inputs["L0_ffn_w_down"]))
    clip_d = _f32(np.asarray(inputs["L0_ffn_clip_down"]).reshape(1, 1))
    hx8 = np.concatenate([r2[c]["hx"][0:1, 0:1] for c in range(NCORE)], axis=1)
    import ml_dtypes
    shared = {
        "wd": wd, "clip_d": clip_d, "hx8": _f32(hx8),
        "ident": np.eye(128, dtype=np.float32),
        "identb": np.eye(128, dtype=ml_dtypes.bfloat16),
        "ones_r": np.ones((1, 128), dtype=np.float32),
    }
    maps = []
    for c in range(NCORE):
        m = dict(shared)
        m["h_sp"] = r2[c]["h_sp"]
        m["x_own"] = maps1[c]["x_own"]
        m["oat"] = r2[c]["oat"]
        maps.append(m)
    return maps


def kernel(**inputs):
    core_ids = list(range(NCORE))
    nc1 = build_l1()
    maps1 = stage_l1_inputs(inputs)
    r1 = run_bass_kernel_spmd(nc1, maps1, core_ids=core_ids).results
    nc2 = build_l2()
    maps2 = stage_l2_inputs(inputs, r1)
    r2 = run_bass_kernel_spmd(nc2, maps2, core_ids=core_ids).results
    nc3 = build_l3()
    maps3 = stage_l3_inputs(inputs, maps1, r2)
    r3 = run_bass_kernel_spmd(nc3, maps3, core_ids=core_ids).results
    x = np.asarray(inputs["L0_attn_x"])
    out = np.empty((NB, SEQ, D), dtype=np.float32)
    for c in range(NCORE):
        s, r = c >> 1, c & 1
        out[s][own_token_index(r)] = r3[c]["y_out"]
    return out


# version 10 (final)
# speedup vs baseline: 1.1427x; 1.0216x over previous
import math
import os
import numpy as np
import concourse.bass as bass
import concourse.mybir as mybir
from concourse.bass_utils import run_bass_kernel_spmd

F32 = mybir.dt.float32
BF16 = mybir.dt.bfloat16
I32 = mybir.dt.int32
AF = mybir.ActivationFunctionType
ALU = mybir.AluOpType
AX = mybir.AxisListType

SEQ, D, NB, IF = 8192, 1024, 4, 2816
NCORE = 8
TOK, NCH, QW = 4096, 16, 256
NIC, NDC, NT512 = IF // 128, D // 128, 4096 // 512

INV2PI = float(np.float64(1.0) / (2.0 * np.pi))
TWOPI = float(2.0 * np.pi)
DDEC = 0.03
RFF_SCALE = float(np.sqrt(2.0 / 64.0) * np.sqrt(np.exp(-0.5)))
CMAG = [float(np.sqrt(abs(((-1.0) ** m) / math.factorial(m)) + 1e-10) * RFF_SCALE)
        for m in range(5)]
LN_EPS = 1e-5
BIGPOS = 3.0e38
BIGNEG = -1.0e30



class Q:
    def __init__(self, name):
        self.name = name
        self.cmds = []
        self.hwm = {}

    def replay(self, eng):
        for c in self.cmds:
            if c[0] == "w":
                eng.wait_ge(c[1], c[2])
            else:
                c[1](eng).then_inc(c[2], c[3])


class Ctr:
    def __init__(self, sem, step):
        self.sem, self.step = sem, step
        self.count = 0


class Env:

    CHAINED = ("v", "s", "g")

    def __init__(self, nc, qnames, ctr_spec, sem):
        self.nc = nc
        self.q = {k: Q(k) for k in qnames}
        self.c = {k: Ctr(sem("c_" + k), st) for k, st in ctr_spec}

    def op(self, qname, cname, emit, deps=()):
        ctr = self.c[cname]
        q = self.q[qname]
        alldeps = list(deps)
        if cname in self.CHAINED and ctr.count > 0:
            alldeps.append((ctr.sem, ctr.count))
        for semh, val in alldeps:
            if semh is None or val <= 0:
                continue
            key = id(semh)
            if q.hwm.get(key, -1) >= val:
                continue
            q.cmds.append(("w", semh, val))
            q.hwm[key] = val
        ctr.count += ctr.step
        q.cmds.append(("i", emit, ctr.sem, ctr.step))
        return (ctr.sem, ctr.count)

    def v(self, emit, deps=()):
        return self.op("v", "v", emit, deps)

    def s(self, emit, deps=()):
        return self.op("s", "s", emit, deps)

    def t(self, emit, deps=()):
        return self.op("t", "t", emit, deps)

    def g(self, emit, deps=()):
        return self.op("g", "g", emit, deps)

    def tok(self, cname):
        ctr = self.c[cname]
        return (ctr.sem, ctr.count)


ENG_CTRS = [("v", 1), ("s", 1), ("t", 1), ("g", 1)]


def finish(env, blk, extra_waits=()):
    q = env.q

    @blk.sync
    def _(eng):
        q["sy"].replay(eng)
        for semh, val in extra_waits:
            if val > 0:
                eng.wait_ge(semh, val)

    @blk.vector
    def _(eng):
        q["v"].replay(eng)

    @blk.scalar
    def _(eng):
        q["s"].replay(eng)

    @blk.tensor
    def _(eng):
        q["t"].replay(eng)

    @blk.gpsimd
    def _(eng):
        q["g"].replay(eng)


class War:

    def __init__(self):
        self.d = {}

    def get(self, k):
        return self.d.get(k, [])

    def set(self, k, toks):
        self.d[k] = list(toks)

    def add(self, k, tok):
        self.d.setdefault(k, []).append(tok)



def build_l1():
    nc = bass.Bass("TRN2", target_bir_lowering=False)
    din = lambda n, s, d=F32: nc.dram_tensor(n, s, d, kind="ExternalInput").ap()
    dout = lambda n, s, d=F32: nc.dram_tensor(n, s, d, kind="ExternalOutput").ap()

    x_own = din("x_own", [TOK, D])
    xT_own = din("xT_own", [128, NDC * TOK])
    wv_t = din("wv_t", [128, NDC * D])
    wout_t = din("wout_t", [128, NDC * D])
    wsh_t = din("wsh_t", [128, NDC * 64])
    wch_t = din("wch_t", [128, NDC])
    bsh_row = din("bsh_row", [1, 64])
    bch_col = din("bch_col", [128, 1])
    bv_cols = din("bv_cols", [128, NDC])
    omega_aug = din("omega_aug", [65, 32])
    wc0 = din("wc0", [128, 64])
    wc1 = din("wc1", [128, 64])
    wc2 = din("wc2", [64, 64])
    ident = din("ident", [128, 128])
    ones_r = din("ones_r", [1, 128])
    ones_c = din("ones_c", [64, 1])

    xnT_sp = dout("xnT_sp", [128, NDC * TOK])
    v_own = dout("v_own", [TOK, D], BF16)
    phiT_o = dout("phiT", [64, TOK], BF16)
    mx_o = dout("mx", [1, 2])

    from contextlib import ExitStack
    with ExitStack() as ctx:
        sb = lambda n, s, d=F32: ctx.enter_context(nc.sbuf_tensor(n, s, d))
        psa = lambda n, s: ctx.enter_context(nc.psum_tensor(n, s, F32))
        sem = lambda n: ctx.enter_context(nc.semaphore(n))

        wv_sb = sb("wv_sb", [128, NDC * D], BF16)
        wo_sb = sb("wo_sb", [128, NDC * D], BF16)
        wsh_sb = sb("wsh_sb", [128, NDC * 64], BF16)
        wch_sb = sb("wch_sb", [128, NDC], BF16)
        bsh_sb = sb("bsh_sb", [1, 64])
        bch_sb = sb("bch_sb", [128, 1])
        bv_sb = sb("bv_sb", [128, NDC])
        om_sb = sb("om_sb", [65, 32])
        wcs = [sb("wc0_sb", [128, 64]), sb("wc1_sb", [128, 64]), sb("wc2_sb", [64, 64])]
        id_sb = sb("id_sb", [128, 128])
        on_sb = sb("on_sb", [1, 128])
        onc_sb = sb("onc_sb", [64, 1])

        xbuf = [sb(f"xbuf{p}", [128, D]) for p in range(2)]
        xtb = [sb(f"xtb{p}", [128, NDC * 512]) for p in range(2)]
        xnT = [sb(f"xnT{p}", [128, NDC * 512]) for p in range(2)]
        xnTb = [sb(f"xnTb{p}", [128, NDC * 512], BF16) for p in range(2)]
        vT = [sb(f"vT{p}", [128, NDC * 512], BF16) for p in range(2)]
        vpb = [sb(f"vpb{p}", [128, D], BF16) for p in range(2)]

        sq_scr = sb("sq_scr", [128, D])
        stats = sb("stats", [128, 8])
        mr = sb("mr", [128, 2])
        mean_row = sb("mean_row", [1, 512])
        rstd_row = sb("rstd_row", [1, 512])
        m_b = sb("m_b", [128, 512])
        r_b = sb("r_b", [128, 512])
        gmax_col = sb("gmax_col", [128, 1])
        q_col = sb("q_col", [128, 1])
        qpow = sb("qpow", [128, 1])
        phic = sb("phic", [128, 5])
        shell = sb("shell", [128, 64])
        shn = sb("shn", [128, 65])
        shnT = sb("shnT", [65, 128])
        prt = sb("prt", [128, 64])
        prk = sb("prk", [128, 64], I32)
        prf = sb("prf", [128, 64])
        phis = sb("phis", [128, 64])
        tpb = sb("tpb", [128, 320])
        tpT = [sb("tpT0", [128, 128]), sb("tpT1", [128, 128]), sb("tpT2", [64, 128])]
        phiT_buf = sb("phiT_buf", [64, TOK], BF16)
        phisq = sb("phisq", [64, 128])
        nn_acc = sb("nn_acc", [1, 1])
        fin_row = sb("fin_row", [1, 128])
        mx_sb = sb("mx_sb", [1, 2])

        ps_mr = psa("ps_mr", [1, 256])
        ps_m = psa("ps_m", [128, 512])
        ps_r = psa("ps_r", [128, 512])
        ps_v = psa("ps_v", [128, 512])
        ps_vp = psa("ps_vp", [128, 1024])
        PSF = psa("PSF", [128, 512])
        PSG = psa("PSG", [128, 512])
        ps_q = PSF[:, 0:1]
        ps_sh = PSF[:, 32:96]
        ps_st = PSF[0:65, 128:256]
        ps_pj = PSF[:, 288:320]
        ps_nn = PSF[0:1, 384:512]
        ps_tp = [PSG[:, 0:128], PSG[:, 128:256], PSG[0:64, 256:384]]
        ps_phi = PSG[0:64, 384:512]

        E = Env(nc, ("sy", "v", "s", "t", "g"),
                ENG_CTRS + [("cl", 16), ("clg", 16), ("x0", 16), ("x1", 16), ("xt0", 16), ("xt1", 16),
                            ("sp0", 16), ("sp1", 16), ("vo0", 16), ("vo1", 16), ("go", 16)],
                sem)
        W = War()

        def cload(dst, src):
            return E.op("sy", "cl", lambda e, d=dst, s_=src: e.dma_start(out=d, in_=s_))

        def gload(dst, src):
            return E.op("g", "clg", lambda e, d=dst, s_=src: e.dma_start(out=d, in_=s_))

        for dst, src in [(bsh_sb[:], bsh_row), (bch_sb[:], bch_col),
                         (bv_sb[:], bv_cols), (om_sb[:], omega_aug), (wcs[0][:], wc0),
                         (wcs[1][:], wc1), (wcs[2][:], wc2), (id_sb[:], ident),
                         (on_sb[:], ones_r), (onc_sb[:], ones_c)]:
            cload(dst, src)
        CL = E.tok("cl")
        for dst, src in [(wv_sb[:], wv_t), (wo_sb[:], wout_t),
                         (wsh_sb[:], wsh_t), (wch_sb[:], wch_t)]:
            gload(dst, src)
        CLG = E.tok("clg")
        t_wv = t_wo = t_wsh = t_wch = CLG
        E.v(lambda e: e.memset(gmax_col[:], 0.0))
        E.v(lambda e: e.memset(nn_acc[:], 0.0))
        E.v(lambda e: e.memset(shn[:, 64:65], 1.0))
        E.v(lambda e: e.memset(phic[:, 0:1], CMAG[0]))

        x_free = [[], []]
        xt_free = [[], []]
        sp_prev = [None, None]
        vo_prev = [None, None]

        for tc in range(NT512):
            p = tc % 2
            xt_ld = E.op("sy", f"xt{p}", lambda e, tc=tc, p=p: e.dma_start(
                out=xtb[p][:].rearrange("q (c n) -> q c n", n=512),
                in_=xT_own.rearrange("q (c n) -> q c n", n=TOK)[:, :, 512 * tc:512 * (tc + 1)]),
                deps=xt_free[p])
            xt_free[p] = []

            mr_toks = []
            for tt in range(4):
                ti = tc * 4 + tt
                pp = ti % 2
                x_ld = E.op("sy", f"x{pp}", lambda e, ti=ti, pp=pp: e.dma_start(
                    out=xbuf[pp][:], in_=x_own[128 * ti:128 * (ti + 1), :]),
                    deps=x_free[pp])
                t1 = E.v(lambda e, pp=pp: e.reduce_sum(stats[:, 0:1], xbuf[pp][:], axis=AX.X),
                         deps=[x_ld])
                t2 = E.s(lambda e, pp=pp: e.activation(sq_scr[:], xbuf[pp][:], AF.Square,
                                                       accum_out=stats[:, 1:2]),
                         deps=[x_ld, CL] + W.get("sq_scr") + W.get("stats1"))
                x_free[pp] = [t1, t2]
                E.v(lambda e: e.tensor_scalar_mul(mr[:, 0:1], stats[:, 0:1], 1.0 / D),
                    deps=W.get("mr"))
                E.v(lambda e: e.tensor_mul(stats[:, 2:3], mr[:, 0:1], mr[:, 0:1]))
                E.v(lambda e: e.scalar_tensor_tensor(stats[:, 3:4], stats[:, 1:2], 1.0 / D,
                                                     stats[:, 2:3], ALU.mult, ALU.subtract),
                    deps=[t2])
                W.set("stats1", [E.tok("v")])
                E.v(lambda e: e.tensor_scalar_add(stats[:, 3:4], stats[:, 3:4], LN_EPS))
                t7 = E.v(lambda e: e.reciprocal(stats[:, 2:3], stats[:, 3:4]))
                t8 = E.s(lambda e: e.activation(mr[:, 1:2], stats[:, 2:3], AF.Sqrt), deps=[t7])
                t9a = E.t(lambda e: e.transpose(ps_mr[:, 0:128], mr[:, 0:1], id_sb[:]),
                          deps=[t8, E.tok("v"), CL] + W.get("ps_mr"))
                t9 = E.t(lambda e: e.transpose(ps_mr[:, 128:256], mr[:, 1:2], id_sb[:]))
                W.set("mr", [t9])
                t10a = E.v(lambda e, tt=tt: e.tensor_copy(mean_row[0:1, 128 * tt:128 * (tt + 1)], ps_mr[0:1, 0:128]),
                           deps=[t9a, t9] + W.get("mr_rows"))
                t10 = E.v(lambda e, tt=tt: e.tensor_copy(rstd_row[0:1, 128 * tt:128 * (tt + 1)], ps_mr[0:1, 128:256]))
                W.set("ps_mr", [t10])
                mr_toks.append(t10)

            tb0 = E.t(lambda e: e.matmul(ps_m[:], on_sb[:], mean_row[:], start=True, stop=True),
                      deps=mr_toks + [CL] + W.get("ps_m"))
            tb1 = E.t(lambda e: e.matmul(ps_r[:], on_sb[:], rstd_row[:], start=True, stop=True),
                      deps=W.get("ps_r"))
            W.set("mr_rows", [tb0, tb1])
            te0 = E.s(lambda e: e.copy(m_b[:], ps_m[:]), deps=[tb0] + W.get("m_b"))
            te1 = E.s(lambda e: e.copy(r_b[:], ps_r[:]), deps=[tb1] + W.get("r_b"))
            W.set("ps_m", [te0])
            W.set("ps_r", [te1])

            xn_toks = []
            xnb_toks = []
            for dc in range(NDC):
                sl = slice(512 * dc, 512 * (dc + 1))
                E.v(lambda e, p=p, sl=sl: e.tensor_sub(xnT[p][:, sl], xtb[p][:, sl], m_b[:]),
                    deps=[xt_ld, te0] + W.get(f"xnT{p}"))
                tb_ = E.v(lambda e, p=p, sl=sl: e.tensor_mul(xnT[p][:, sl], xnT[p][:, sl], r_b[:]),
                          deps=[te1])
                E.v(lambda e, p=p, sl=sl: e.reduce_max(stats[:, 4:5], xnT[p][:, sl], axis=AX.X,
                                                       apply_absolute_value=True))
                E.v(lambda e: e.tensor_max(gmax_col[:], gmax_col[:], stats[:, 4:5]))
                tc_ = E.s(lambda e, p=p, sl=sl: e.copy(xnTb[p][:, sl], xnT[p][:, sl]),
                          deps=[tb_] + W.get(f"xnTb{p}"))
                xn_toks.append(tb_)
                xnb_toks.append(tc_)
            W.set("m_b", [E.tok("v")])
            W.set("r_b", [E.tok("v")])
            xt_free[p] = [E.tok("v")]

            sp_tok = E.op("sy", f"sp{p}", lambda e, p=p, tc=tc: e.dma_start(
                out=xnT_sp[:, NDC * 512 * tc:NDC * 512 * (tc + 1)],
                in_=xnT[p][:]),
                deps=xn_toks + ([sp_prev[p]] if sp_prev[p] else []))
            sp_prev[p] = sp_tok
            W.set(f"xnT{p}", [sp_tok])

            vT_toks = []
            for co in range(NDC):
                mm = None
                for ci in range(NDC):
                    dep = ([t_wv] + xnb_toks + W.get("ps_v")) if ci == 0 else []
                    mm = E.t(lambda e, p=p, ci=ci, co=co: e.matmul(
                        ps_v[:], wv_sb[:, D * ci + 128 * co:D * ci + 128 * (co + 1)],
                        xnTb[p][:, 512 * ci:512 * (ci + 1)],
                        start=(ci == 0), stop=(ci == NDC - 1)), deps=dep)
                tv = E.s(lambda e, p=p, co=co: e.activation(
                    vT[p][:, 512 * co:512 * (co + 1)], ps_v[:], AF.Identity,
                    bias=bv_sb[:, co:co + 1]), deps=[mm] + W.get(f"vT{p}"))
                W.set("ps_v", [tv])
                vT_toks.append(tv)

            vp_last = []
            for tt in range(4):
                pp = tt % 2
                mm = None
                for ci in range(NDC):
                    for half in range(2):
                        dep = []
                        if ci == 0 and half == 0:
                            dep = [t_wo] + vT_toks + W.get("ps_vp")
                        mm = E.t(lambda e, p=p, ci=ci, tt=tt, half=half: e.matmul(
                            ps_vp[:, 512 * half:512 * (half + 1)],
                            vT[p][:, 512 * ci + 128 * tt:512 * ci + 128 * (tt + 1)],
                            wo_sb[:, D * ci + 512 * half:D * ci + 512 * (half + 1)],
                            start=(ci == 0), stop=(ci == NDC - 1)), deps=dep)
                tvp = E.v(lambda e, pp=pp: e.tensor_copy(vpb[pp][:], ps_vp[:]),
                          deps=[mm] + ([vo_prev[pp]] if vo_prev[pp] else []))
                W.set("ps_vp", [tvp])
                vo_prev[pp] = E.op("sy", f"vo{pp}", lambda e, pp=pp, tc=tc, tt=tt: e.dma_start(
                    out=v_own[128 * (4 * tc + tt):128 * (4 * tc + tt + 1), :], in_=vpb[pp][:]),
                    deps=[tvp])
                vp_last = [tvp]
            W.set(f"vT{p}", vp_last)

            for tt in range(4):
                ti = tc * 4 + tt
                mm = None
                for ci in range(NDC):
                    dep = ([t_wch] + xnb_toks + W.get("PSF")) if ci == 0 else []
                    mm = E.t(lambda e, p=p, ci=ci, tt=tt: e.matmul(
                        ps_q, xnTb[p][:, 512 * ci + 128 * tt:512 * ci + 128 * (tt + 1)],
                        wch_sb[:, ci:ci + 1],
                        start=(ci == 0), stop=(ci == NDC - 1)), deps=dep)
                tq = E.s(lambda e: e.activation(q_col[:], ps_q, AF.Tanh, bias=bch_sb[:, 0:1]),
                         deps=[mm] + W.get("q_col"))
                W.set("PSF", [tq])
                E.v(lambda e: e.tensor_mul(qpow[:], q_col[:], q_col[:]), deps=[tq] + W.get("phic"))
                E.v(lambda e: e.tensor_scalar_mul(phic[:, 1:2], q_col[:], CMAG[1]))
                E.v(lambda e: e.tensor_scalar_mul(phic[:, 2:3], qpow[:], CMAG[2]))
                E.v(lambda e: e.tensor_mul(qpow[:], qpow[:], q_col[:]))
                E.v(lambda e: e.tensor_scalar_mul(phic[:, 3:4], qpow[:], CMAG[3]))
                E.v(lambda e: e.tensor_mul(qpow[:], qpow[:], q_col[:]))
                tpc = E.v(lambda e: e.tensor_scalar_mul(phic[:, 4:5], qpow[:], CMAG[4]))
                W.set("q_col", [tpc])
                mm = None
                for ci in range(NDC):
                    dep = ([t_wsh] + xnb_toks + W.get("PSF")) if ci == 0 else []
                    mm = E.t(lambda e, p=p, ci=ci, tt=tt: e.matmul(
                        ps_sh, xnTb[p][:, 512 * ci + 128 * tt:512 * ci + 128 * (tt + 1)],
                        wsh_sb[:, 64 * ci:64 * (ci + 1)],
                        start=(ci == 0), stop=False), deps=dep)
                mm = E.t(lambda e: e.matmul(ps_sh, on_sb[:], bsh_sb[:], start=False, stop=True))
                tsh = E.v(lambda e: e.tensor_copy(shell[:], ps_sh), deps=[mm] + W.get("shell"))
                W.set("PSF", [tsh])
                tss = E.s(lambda e: e.activation(sq_scr[:, 0:64], shell[:], AF.Square,
                                                 accum_out=stats[:, 5:6]),
                          deps=[tsh] + W.get("sq_scr") + W.get("stats5"))
                tsq = E.s(lambda e: e.activation(stats[:, 6:7], stats[:, 5:6], AF.Sqrt))
                W.set("stats5", [tsq])
                W.set("sq_scr", [tss])
                E.v(lambda e: e.tensor_scalar_max(stats[:, 6:7], stats[:, 6:7], 1e-12), deps=[tsq])
                trn = E.v(lambda e: e.reciprocal(stats[:, 6:7], stats[:, 6:7]))
                tshn = E.v(lambda e: e.tensor_scalar_mul(shn[:, 0:64], shell[:], stats[:, 6:7]),
                           deps=W.get("shn"))
                W.set("shell", [tshn])
                tst = E.t(lambda e: e.transpose(ps_st, shn[:], id_sb[:]),
                          deps=[tshn] + W.get("PSF"))
                W.set("shn", [tst])
                tstc = E.v(lambda e: e.tensor_copy(shnT[:], ps_st), deps=[tst] + W.get("shnT"))
                W.set("PSF", [tstc])
                tpj = E.t(lambda e: e.matmul(ps_pj, shnT[:], om_sb[:], start=True, stop=True),
                          deps=[tstc, CL] + W.get("PSF"))
                W.set("shnT", [tpj])
                ta1 = E.v(lambda e: e.tensor_scalar(prt[:, 0:32], ps_pj, INV2PI, 64.75,
                                                    ALU.mult, ALU.add),
                          deps=[tpj] + W.get("prt"))
                ta2 = E.v(lambda e: e.tensor_scalar(prt[:, 32:64], ps_pj, INV2PI, 64.5,
                                                    ALU.mult, ALU.add))
                W.set("PSF", [ta2])
                E.v(lambda e: e.tensor_copy(prk[:], prt[:]))
                E.v(lambda e: e.scalar_tensor_tensor(prf[:], prt[:], -0.5, prk[:],
                                                     ALU.add, ALU.subtract))
                E.v(lambda e: e.tensor_single_scalar(prt[:], prf[:], -0.5, ALU.is_lt))
                tf = E.v(lambda e: e.tensor_add(prf[:], prf[:], prt[:]))
                W.set("prt", [tf])
                tphis = E.s(lambda e: e.activation(phis[:], prf[:], AF.Sin, scale=TWOPI),
                            deps=[tf] + W.get("phis"))
                tp_toks = []
                for m in range(5):
                    dep = [tphis, tpc] + W.get("tpb") if m == 0 else []
                    tp_toks.append(E.v(lambda e, m=m: e.tensor_scalar_mul(
                        tpb[:, 64 * m:64 * (m + 1)], phis[:], phic[:, m:m + 1]), deps=dep))
                W.set("phis", [tp_toks[-1]])
                W.set("phic", [tp_toks[-1]])
                ph_toks = []
                for k3 in range(3):
                    insl = tpb[:, 128 * k3:128 * (k3 + 1)] if k3 < 2 else tpb[:, 256:320]
                    ttr = E.t(lambda e, k3=k3, insl=insl: e.transpose(ps_tp[k3], insl, id_sb[:]),
                              deps=tp_toks + W.get("PSG"))
                    tcp = E.v(lambda e, k3=k3: e.tensor_copy(tpT[k3][:], ps_tp[k3]),
                              deps=[ttr] + W.get(f"tpT{k3}"))
                    W.set("PSG", [tcp])
                    ph_toks.append(tcp)
                W.set("tpb", [ph_toks[-1]])
                mm = None
                for k3 in range(3):
                    dep = (ph_toks + [CL] + W.get("PSG")) if k3 == 0 else []
                    mm = E.t(lambda e, k3=k3: e.matmul(ps_phi, wcs[k3][:], tpT[k3][:],
                                                       start=(k3 == 0), stop=(k3 == 2)), deps=dep)
                for k3 in range(3):
                    W.set(f"tpT{k3}", [mm])
                tphi = E.s(lambda e, ti=ti: e.copy(
                    phiT_buf[:, 128 * ti:128 * (ti + 1)], ps_phi), deps=[mm])
                tsq2 = E.s(lambda e: e.activation(phisq[:], ps_phi, AF.Square),
                           deps=W.get("phisq"))
                W.set("PSG", [tphi, tsq2])
                tnn = E.t(lambda e: e.matmul(ps_nn, onc_sb[:], phisq[:], start=True, stop=True),
                          deps=[tsq2, CL] + W.get("PSF"))
                W.set("phisq", [tnn])
                tnm = E.v(lambda e: e.reduce_max(fin_row[0:1, 0:1], ps_nn, axis=AX.X),
                          deps=[tnn] + W.get("fin_row"))
                W.set("PSF", [tnm])
                E.v(lambda e: e.tensor_max(nn_acc[:], nn_acc[:], fin_row[0:1, 0:1]))

        tgf = E.t(lambda e: e.transpose(ps_mr[0:1, 0:128], gmax_col[:], id_sb[:]),
                  deps=[E.tok("v"), CL] + W.get("ps_mr"))
        E.v(lambda e: e.tensor_copy(fin_row[:], ps_mr[0:1, 0:128]), deps=[tgf] + W.get("fin_row"))
        E.v(lambda e: e.reduce_max(mx_sb[0:1, 0:1], fin_row[:], axis=AX.X))
        tmxf = E.v(lambda e: e.tensor_copy(mx_sb[0:1, 1:2], nn_acc[:]))
        E.op("sy", "go", lambda e: e.dma_start(out=phiT_o, in_=phiT_buf[:]),
             deps=[E.tok("s")])
        E.op("sy", "go", lambda e: e.dma_start(out=mx_o, in_=mx_sb[:]),
             deps=[tmxf])

        blk = ctx.enter_context(nc.Block())
        waits = [E.tok("sp0"), E.tok("sp1"), E.tok("vo0"), E.tok("vo1"), E.tok("go")]
        finish(E, blk, extra_waits=waits)
    return nc



ROUND_SAFE = bool(os.environ.get("BASS_SIM_FP64"))


def emit_round_q(E, W, src_ap, inv_col, out_bf, rbuf, kbuf, fbuf, cbuf, deps, wkey):
    if not ROUND_SAFE:
        t1 = E.v(lambda e: e.tensor_scalar_mul(rbuf, src_ap, inv_col),
                 deps=list(deps) + W.get(wkey))
        E.v(lambda e: e.tensor_copy(kbuf, rbuf))
        t3 = E.v(lambda e: e.tensor_scalar(out_bf, kbuf, -128.0, 127.0, ALU.max, ALU.min))
        W.set(wkey, [t3])
        return t3
    t1 = E.v(lambda e: e.tensor_scalar(rbuf, src_ap, inv_col, 512.5, ALU.mult, ALU.add),
             deps=list(deps) + W.get(wkey))
    tk = E.s(lambda e: e.copy(kbuf, rbuf), deps=[t1] + W.get(wkey + "k"))
    t2 = E.v(lambda e: e.scalar_tensor_tensor(fbuf, rbuf, -0.5, kbuf, ALU.add, ALU.subtract),
             deps=[tk])
    E.v(lambda e: e.tensor_single_scalar(cbuf, fbuf, -0.5, ALU.is_lt))
    E.v(lambda e: e.tensor_add(fbuf, fbuf, cbuf))
    E.v(lambda e: e.scalar_tensor_tensor(cbuf, rbuf, -512.5, fbuf, ALU.add, ALU.subtract))
    t3 = E.v(lambda e: e.tensor_scalar(out_bf, cbuf, -128.0, 127.0, ALU.max, ALU.min))
    W.set(wkey, [t3])
    W.set(wkey + "k", [t2])
    return t3



def build_l2():
    nc = bass.Bass("TRN2", target_bir_lowering=False, dynamic_dma_scratch_size=2048)
    din = lambda n, s, d=F32: nc.dram_tensor(n, s, d, kind="ExternalInput").ap()
    dout = lambda n, s, d=F32: nc.dram_tensor(n, s, d, kind="ExternalOutput").ap()

    xnT_sp = din("xnT_sp", [128, NDC * TOK])
    vg = din("vg", [SEQ, D], BF16)
    phi_g = din("phi_g", [64, SEQ], BF16)
    phi_q = din("phi_q", [64, TOK], BF16)
    gx8 = din("gx8", [1, 8])
    nn8 = din("nn8", [1, 8])
    wg = din("wg", [IF, D])
    wu = din("wu", [IF, D])
    wgt = din("wgt", [128, NIC * D])
    wut = din("wut", [128, NIC * D])
    clip_x = din("clip_x", [1, 1])
    log_temp = din("log_temp", [1, 1])
    iota512 = din("iota512", [1, 512])
    qb16 = din("qb16", [1, NCH])
    maskS = din("maskS", [128, 4 * QW], BF16)
    b_out_bc = din("b_out_bc", [128, D])
    ident = din("ident", [128, 128])
    identb = din("identb", [128, 128], BF16)
    ones_r = din("ones_r", [1, 128])
    ones_kc = din("ones_kc", [128, 1], BF16)

    h_sp = dout("h_sp", [128, NIC * TOK])
    hx_o = dout("hx", [1, 1])
    oat_o = dout("oat", [TOK, D])

    NVC = 3

    from contextlib import ExitStack
    with ExitStack() as ctx:
        sb = lambda n, s, d=F32: ctx.enter_context(nc.sbuf_tensor(n, s, d))
        psa = lambda n, s: ctx.enter_context(nc.psum_tensor(n, s, F32))
        sem = lambda n: ctx.enter_context(nc.semaphore(n))

        gx_sb = sb("gx_sb", [1, 8])
        nn_sb = sb("nn_sb", [1, 8])
        cx_sb = sb("cx_sb", [1, 1])
        lt_sb = sb("lt_sb", [1, 1])
        io512 = sb("io512", [1, 512])
        qb_sb = sb("qb_sb", [1, NCH])
        mk_sb = sb("mk_sb", [128, 4 * QW], BF16)
        bo_sb = sb("bo_sb", [128, D])
        id_sb = sb("id_sb", [128, 128])
        on_sb = sb("on_sb", [1, 128])
        onk_sb = sb("onk_sb", [128, 1], BF16)
        vgc = sb("vgc", [128, NVC * D], BF16)

        srow = sb("srow", [1, 8])
        invx_col = sb("invx_col", [128, 1])
        sx_col = sb("sx_col", [128, 1])
        negc_col = sb("negc_col", [128, 1])
        invt = sb("invt", [1, 1])
        arow = sb("arow", [1, 256])
        abc = sb("abc", [64, 256])
        brow = sb("brow", [1, 512])
        segb = sb("segb", [1, 1])

        nx = sb("nx", [128, NDC * TOK], BF16)
        xns = sb("xns0", [128, NDC * 512])
        wob = [sb(f"wob{p}", [128, D]) for p in range(2)]
        swcol = sb("swcol", [128, 8])
        swcol2 = sb("swcol2", [128, 1])
        sgx = sb("sgx", [128, 2 * NIC])
        wtb = [sb(f"wtb{p}", [128, D]) for p in range(6)]
        invrow = [sb(f"invrow{p}", [1, D]) for p in range(2)]
        invbc = [sb(f"invbc{p}", [128, D]) for p in range(2)]
        ngt = [sb(f"ngt{p}", [128, D], BF16) for p in range(2)]
        nut = [sb(f"nut{p}", [128, D], BF16) for p in range(2)]
        hg = sb("hg", [128, 512])
        hu = sb("hu", [128, 512])
        hb = [sb(f"hb{p}", [128, 512]) for p in range(2)]
        hmax_col = sb("hmax_col", [128, 1])
        hrow = sb("hrow", [1, 128])
        hx_sb = sb("hx_sb", [1, 1])
        rb4 = sb("rb4", [128, D])
        kb4 = sb("kb4", [128, D], I32)
        fb4 = sb("fb4", [128, D if ROUND_SAFE else 1])
        cb4 = sb("cb4", [128, D if ROUND_SAFE else 1])

        ktil = [sb(f"ktil{p}", [64, SEQ], BF16) for p in range(2)]
        qt = [sb(f"qt{p}", [64, 256], BF16) for p in range(2)]
        phgs = sb("phgs", [64, 512], BF16)
        phqs = sb("phqs", [64, 256], BF16)
        vgb = [sb(f"vgb{p}", [128, D], BF16) for p in range(3)]
        atile = [sb(f"at{p}", [128, 256], BF16) for p in range(2)]
        rd2 = sb("rd2", [128, 4])
        rd = sb("rd", [128, 2])
        oatb = [sb(f"oatb{p}", [128, D]) for p in range(2)]

        PB1 = psa("PB1", [128, 512])
        PB2 = psa("PB2", [128, 512])
        PS = [psa("PSA", [128, 512]), psa("PSB", [128, 512])]
        PO1 = psa("PO1", [128, 1024])
        PO2 = psa("PO2", [128, 1024])

        E = Env(nc, ("sy", "v", "s", "t", "g"),
                ENG_CTRS + [("cl", 16), ("xs0", 16),
                            ("wo0", 16), ("wo1", 16),
                            ("wt0", 16), ("wt1", 16), ("wt2", 16),
                            ("wt3", 16), ("wt4", 16), ("wt5", 16),
                            ("hs0", 16), ("hs1", 16), ("ph0", 16), ("ph1", 16),
                            ("vg0", 16), ("vg1", 16), ("vg2", 16), ("oa0", 16), ("oa1", 16),
                            ("go", 16)],
                sem)
        W = War()

        def cload(dst, src):
            return E.op("sy", "cl", lambda e, d=dst, s_=src: e.dma_start(out=d, in_=s_))

        for dst, src in [(gx_sb[:], gx8), (nn_sb[:], nn8), (cx_sb[:], clip_x),
                         (lt_sb[:], log_temp), (io512[:], iota512),
                         (qb_sb[:], qb16), (mk_sb[:], maskS), (bo_sb[:], b_out_bc),
                         (id_sb[:], ident), (on_sb[:], ones_r), (onk_sb[:], ones_kc)]:
            cload(dst, src)
        for kt in range(NVC):
            cload(vgc[:, D * kt:D * (kt + 1)], vg[128 * kt:128 * (kt + 1), :])
        CL = E.tok("cl")

        t1 = E.v(lambda e: e.reduce_max(srow[0:1, 0:1], gx_sb[:], axis=AX.X), deps=[CL])
        E.v(lambda e: e.reduce_max(srow[0:1, 1:2], nn_sb[:], axis=AX.X))
        ts1 = E.s(lambda e: e.activation(srow[0:1, 2:3], cx_sb[:], AF.Sigmoid), deps=[CL])
        E.v(lambda e: e.tensor_mul(srow[0:1, 3:4], srow[0:1, 2:3], srow[0:1, 0:1]), deps=[ts1])
        E.v(lambda e: e.reciprocal(srow[0:1, 4:5], srow[0:1, 3:4]))
        tinvx = E.v(lambda e: e.tensor_scalar_mul(srow[0:1, 4:5], srow[0:1, 4:5], 127.0))
        tsx = E.v(lambda e: e.tensor_scalar_mul(srow[0:1, 3:4], srow[0:1, 3:4], 1.0 / 127.0))
        tsp0 = E.s(lambda e: e.activation(srow[0:1, 5:6], lt_sb[:], AF.Exp), deps=[CL])
        tsp1 = E.v(lambda e: e.tensor_scalar_add(srow[0:1, 5:6], srow[0:1, 5:6], 1.0), deps=[tsp0])
        tsp = E.s(lambda e: e.activation(srow[0:1, 5:6], srow[0:1, 5:6], AF.Ln), deps=[tsp1])
        E.v(lambda e: e.tensor_scalar_add(srow[0:1, 5:6], srow[0:1, 5:6], 0.1), deps=[tsp])
        tit = E.v(lambda e: e.reciprocal(invt[:], srow[0:1, 5:6]))
        E.v(lambda e: e.tensor_mul(srow[0:1, 6:7], srow[0:1, 1:2], invt[:]))
        tnegc = E.v(lambda e: e.tensor_scalar(srow[0:1, 6:7], srow[0:1, 6:7], 1.02, -1.0,
                                              ALU.mult, ALU.mult))
        def bcast_col(dst, src_ap, dep):
            tm = E.t(lambda e, d=dst, s_=src_ap: e.matmul(PB1[:, 0:1], on_sb[:], s_,
                                                          start=True, stop=True),
                     deps=[dep, CL] + W.get("PB1"))
            tc_ = E.v(lambda e, d=dst: e.tensor_copy(d, PB1[:, 0:1]), deps=[tm])
            W.set("PB1", [tc_])
            return tc_
        t_invx = bcast_col(invx_col[:], srow[0:1, 4:5], tinvx)
        t_sx = bcast_col(sx_col[:], srow[0:1, 3:4], tsx)
        t_negc = bcast_col(negc_col[:], srow[0:1, 6:7], tnegc)
        tar = E.s(lambda e: e.activation(arow[:], io512[0:1, 0:256], AF.Exp, scale=-DDEC), deps=[CL])
        tar2 = E.v(lambda e: e.tensor_scalar_mul(arow[:], arow[:], invt[0:1, 0:1]),
                   deps=[tar, tit])
        tba = E.t(lambda e: e.matmul(PB1[0:64, 0:256], on_sb[0:1, 0:64], arow[:],
                                     start=True, stop=True),
                  deps=[tar2, CL] + W.get("PB1"))
        t_abc = E.v(lambda e: e.tensor_copy(abc[:], PB1[0:64, 0:256]), deps=[tba])
        W.set("PB1", [t_abc])
        E.v(lambda e: e.memset(hmax_col[:], 0.0))

        qs_v = {}
        qs_p = {}
        qr = {}

        def vscale(ic, w):
            p2, p3 = ic % 2, ic % 3
            worig, wtr = (wg, wgt) if w == 0 else (wu, wut)
            scol = ic if w == 0 else NIC + ic
            sc = 4 * w + 2 * (ic % 2)
            ld = E.op("sy", f"wo{p2}", lambda e, p2=p2, worig=worig, ic=ic: e.dma_start(
                out=wob[p2][:], in_=worig[128 * ic:128 * (ic + 1), :]),
                deps=W.get(f"wob{p2}"))
            b6 = 3 * w + p3
            ldt = E.op("sy", f"wt{b6}", lambda e, b6=b6, wtr=wtr, ic=ic: e.dma_start(
                out=wtb[b6][:], in_=wtr[:, D * ic:D * (ic + 1)]),
                deps=W.get(f"wtb{b6}"))
            tr1 = E.v(lambda e, p2=p2, sc=sc: e.reduce_max(
                swcol[:, sc:sc + 1], wob[p2][:], axis=AX.X, apply_absolute_value=True),
                deps=[ld] + W.get(f"swc{w}{ic % 2}"))
            W.set(f"wob{p2}", [tr1])
            E.v(lambda e, scol=scol, sc=sc: e.tensor_scalar_mul(
                sgx[:, scol:scol + 1], swcol[:, sc:sc + 1], 1.0 / 127.0))
            E.v(lambda e, sc=sc: e.reciprocal(swcol[:, sc + 1:sc + 2], swcol[:, sc:sc + 1]))
            tiv = E.v(lambda e, sc=sc: e.tensor_scalar_mul(
                swcol[:, sc + 1:sc + 2], swcol[:, sc + 1:sc + 2], 127.0))
            tmuls = E.v(lambda e, scol=scol: e.tensor_mul(
                sgx[:, scol:scol + 1], sgx[:, scol:scol + 1], sx_col[:]), deps=[t_sx])
            qs_v[(ic, w)] = (ldt, tiv, tmuls)

        def pescale_T(ic, w):
            PQ, pqk = (PB1, "PB1") if w == 0 else (PB2, "PB2")
            sc = 4 * w + 2 * (ic % 2)
            ldt, tiv, tmuls = qs_v[(ic, w)]
            ttr = E.t(lambda e, PQ=PQ, sc=sc: e.transpose(
                PQ[0:1, 0:128], swcol[:, sc + 1:sc + 2], id_sb[:]),
                deps=[tiv, CL] + W.get(pqk))
            W.set(f"swc{w}{ic % 2}", [ttr])
            tcp = E.v(lambda e, w=w, PQ=PQ: e.tensor_copy(
                invrow[w][0:1, 0:128], PQ[0:1, 0:128]), deps=[ttr] + W.get(f"irow{w}"))
            for r8 in range(1, NDC):
                E.v(lambda e, w=w, r8=r8: e.tensor_copy(
                    invrow[w][0:1, 128 * r8:128 * (r8 + 1)], invrow[w][0:1, 0:128]))
            W.set(pqk, [tcp])
            W.set(f"irow{w}", [])
            qs_p[(ic, w)] = []
            return E.tok("v")

        def pescale_B(ic, w, half, prow):
            PQ, pqk = (PB1, "PB1") if w == 0 else (PB2, "PB2")
            tb = E.t(lambda e, PQ=PQ, w=w, half=half: e.matmul(
                PQ[:, 0:512], on_sb[:], invrow[w][0:1, 512 * half:512 * (half + 1)],
                start=True, stop=True),
                deps=[prow, CL] + W.get(pqk))
            te = E.v(lambda e, PQ=PQ, w=w, half=half: e.tensor_copy(
                invbc[w][:, 512 * half:512 * (half + 1)], PQ[:, 0:512]),
                deps=[tb] + (W.get(f"ibw{w}") if half == 0 else []))
            W.set(pqk, [te])
            W.add(f"irow{w}", tb)
            qs_p[(ic, w)].append(te)

        def pescale_full(ic):
            for w in range(2):
                prow = pescale_T(ic, w)
                pescale_B(ic, w, 0, prow)
                pescale_B(ic, w, 1, prow)

        def qround(ic, w):
            p2, p3 = ic % 2, ic % 3
            dst, dname = (ngt, "ngt") if w == 0 else (nut, "nut")
            ldt, tiv, tmuls = qs_v[(ic, w)]
            b6 = 3 * w + p3
            tq1 = E.v(lambda e, b6=b6, w=w: e.tensor_mul(rb4[:], wtb[b6][:], invbc[w][:]),
                      deps=[ldt] + qs_p[(ic, w)] + W.get("rq"))
            W.set(f"wtb{b6}", [tq1])
            W.set(f"ibw{w}", [tq1])
            if ROUND_SAFE:
                tk = E.s(lambda e: e.copy(kb4[:], rb4[:]), deps=[tq1] + W.get("rqk"))
                t2 = E.v(lambda e: e.scalar_tensor_tensor(fb4[:], rb4[:], -0.5, kb4[:],
                                                          ALU.add, ALU.subtract), deps=[tk])
                E.v(lambda e: e.tensor_single_scalar(cb4[:], fb4[:], -0.5, ALU.is_lt))
                E.v(lambda e: e.tensor_add(fb4[:], fb4[:], cb4[:]))
                E.v(lambda e: e.tensor_sub(rb4[:], rb4[:], fb4[:]))
                tq3 = E.v(lambda e, p2=p2, dst=dst: e.tensor_scalar(
                    dst[p2][:], rb4[:], -128.0, 127.0, ALU.max, ALU.min),
                    deps=W.get(f"{dname}{p2}"))
                W.set("rqk", [t2])
            else:
                E.v(lambda e: e.tensor_copy(kb4[:], rb4[:]))
                tq3 = E.v(lambda e, p2=p2, dst=dst: e.tensor_scalar(
                    dst[p2][:], kb4[:], -128.0, 127.0, ALU.max, ALU.min),
                    deps=W.get(f"{dname}{p2}"))
            W.set("rq", [tq3])
            qr[(ic, w)] = (tq3, tmuls)

        NXR_TC = {}
        xs_state = {"free": []}

        def emit_nx(tc):
            ld = E.op("sy", "xs0", lambda e, tc=tc: e.dma_start(
                out=xns[:], in_=xnT_sp[:, NDC * 512 * tc:NDC * 512 * (tc + 1)]),
                deps=xs_state["free"])
            for dd in range(NDC // 2):
                src = xns[:, 1024 * dd:1024 * (dd + 1)]
                dstp = nx[:, 4096 * tc + 1024 * dd:4096 * tc + 1024 * (dd + 1)]
                emit_round_q(E, W, src, invx_col[:, 0:1], dstp,
                             rb4[:], kb4[:], fb4[:], cb4[:], [ld, t_invx], "rq")
            xs_state["free"] = [E.tok("v")]
            NXR_TC[tc] = E.tok("v")

        hs_prev = [None, None]

        def emit_fmms(k, pe, mid=None):
            p = k % 2
            tng, tmulsg = qr[(k, 0)]
            tnu, tmulsu = qr[(k, 1)]
            prow_g = prow_u = None
            if pe is not None:
                prow_g = pescale_T(pe, 0)
                prow_u = pescale_T(pe, 1)
            if mid is not None:
                mid()
            mg = mu = None
            for tc in range(NT512):
                pb = tc % 2
                for dc in range(NDC):
                    depg = ([tng, NXR_TC[tc]] + W.get("PB1")) if dc == 0 else []
                    mg = E.t(lambda e, p=p, dc=dc, tc=tc: e.matmul(
                        PB1[:], ngt[p][:, 128 * dc:128 * (dc + 1)],
                        nx[:, 4096 * tc + 512 * dc:4096 * tc + 512 * (dc + 1)],
                        start=(dc == 0), stop=(dc == NDC - 1)), deps=depg)
                for dc in range(NDC):
                    depu = ([tnu, NXR_TC[tc]] + W.get("PB2")) if dc == 0 else []
                    mu = E.t(lambda e, p=p, dc=dc, tc=tc: e.matmul(
                        PB2[:], nut[p][:, 128 * dc:128 * (dc + 1)],
                        nx[:, 4096 * tc + 512 * dc:4096 * tc + 512 * (dc + 1)],
                        start=(dc == 0), stop=(dc == NDC - 1)), deps=depu)
                tg_ = E.s(lambda e, k=k: e.activation(hg[:], PB1[:], AF.Silu,
                                                      scale=sgx[:, k:k + 1]),
                          deps=[mg, tmulsg] + W.get("hg"))
                W.set("PB1", [tg_])
                tu_ = E.v(lambda e, k=k: e.tensor_scalar_mul(hu[:], PB2[:],
                                                             sgx[:, NIC + k:NIC + k + 1]),
                          deps=[mu, tmulsu] + W.get("hu"))
                W.set("PB2", [tu_])
                th_ = E.v(lambda e, pb=pb: e.tensor_mul(hb[pb][:], hg[:], hu[:]),
                          deps=[tg_, tu_] + ([hs_prev[pb]] if hs_prev[pb] else []))
                W.set("hg", [th_])
                W.set("hu", [th_])
                E.v(lambda e, pb=pb: e.reduce_max(swcol2[:, 0:1], hb[pb][:], axis=AX.X,
                                                  apply_absolute_value=True))
                E.v(lambda e: e.tensor_max(hmax_col[:], hmax_col[:], swcol2[:, 0:1]))
                hs_prev[pb] = E.op("sy", f"hs{pb}", lambda e, pb=pb, k=k, tc=tc: e.dma_start(
                    out=h_sp[:, NIC * 512 * tc + 512 * k:NIC * 512 * tc + 512 * (k + 1)],
                    in_=hb[pb][:]),
                    deps=[th_])
                if pe is not None and tc < 4:
                    pescale_B(pe, tc // 2, tc % 2, prow_g if tc < 2 else prow_u)
            W.set(f"ngt{p}", [mg])
            W.set(f"nut{p}", [mu])

        def emit_item(it):
            kind, a = it
            if kind == "nx":
                emit_nx(a)
            else:
                k = a

                def mid(k=k):
                    if k + 1 < NIC:
                        qround(k + 1, 0)
                        qround(k + 1, 1)
                emit_fmms(k, (k + 2) if k + 2 < NIC else None, mid)
                if k + 3 < NIC:
                    vscale(k + 3, 0)
                    vscale(k + 3, 1)

        def emit_setup(j):
            bf = j % 2
            ob_ = 1 - bf
            ldq = E.op("sy", "ph1", lambda e, j=j: e.dma_start(
                out=phqs[:], in_=phi_q[:, 256 * j:256 * (j + 1)]), deps=W.get("phqs"))
            tqt = E.v(lambda e, bf=bf: e.tensor_mul(qt[bf][:], phqs[:], abc[:]),
                      deps=[t_abc, ldq] + W.get(f"qtr{bf}"))
            W.set("phqs", [tqt])
            W.set(f"qtr{bf}", [])
            if j > 0:
                E.v(lambda e, j=j, bf=bf, ob_=ob_: e.tensor_scalar_mul(
                    ktil[bf][:, 0:512 * j], ktil[ob_][:, 0:512 * j], 2.1175823681357508e-07),
                    deps=W.get(f"ktr{bf}"))
            ldp = E.op("sy", "ph0", lambda e, j=j: e.dma_start(
                out=phgs[:], in_=phi_g[:, 512 * j:512 * (j + 1)]), deps=W.get("phgs"))
            tsg = E.v(lambda e, j=j: e.tensor_scalar_mul(
                segb[:], qb_sb[0:1, j:j + 1], -DDEC), deps=[CL] + W.get("segb"))
            tbr = E.s(lambda e, j=j: e.activation(brow[:], io512[:], AF.Exp,
                                                  scale=DDEC, bias=segb[0:1, 0:1]),
                      deps=[tsg, CL] + W.get("brow"))
            W.set("segb", [tbr])
            tbk = E.t(lambda e: e.matmul(PB1[0:64, 0:512], on_sb[0:1, 0:64], brow[:],
                                         start=True, stop=True),
                      deps=[tbr, CL] + W.get("PB1"))
            W.set("brow", [tbk])
            tkt = E.v(lambda e, j=j, bf=bf: e.tensor_mul(
                ktil[bf][:, 512 * j:512 * (j + 1)], phgs[:], PB1[0:64, 0:512]),
                deps=[tbk, ldp] + W.get(f"ktr{bf}"))
            W.set("PB1", [tkt])
            W.set("phgs", [tkt])
            W.set(f"ktr{bf}", [])
            W.set(f"rdy{bf}", [E.tok("v")])

        def emit_chunk(j):
            bf = j % 2
            nkt = 4 * (j + 1)
            t_rdy = W.get(f"rdy{bf}")

            def emit_S(kt, bf=bf, t_rdy=t_rdy):
                h = kt % 2
                tsm = E.t(lambda e, kt=kt, bf=bf, h=h: e.matmul(
                    PS[h][:, 0:256], ktil[bf][:, 128 * kt:128 * (kt + 1)], qt[bf][:],
                    start=True, stop=True),
                    deps=t_rdy + W.get(f"PS{h}"))
                W.add(f"ktr{bf}", tsm)
                W.add(f"qtr{bf}", tsm)
                return tsm

            def emit_vld(kt):
                if kt < NVC:
                    return CL
                pv = kt % 3
                return E.op("s", f"vg{pv}", lambda e, kt=kt, pv=pv: e.dma_start(
                    out=vgb[pv][:], in_=vg[128 * kt:128 * (kt + 1), :]),
                    deps=W.get(f"vgb{pv}"))

            s_toks = {0: emit_S(0)}
            v_toks = {0: emit_vld(0)}
            av_done = []
            for kt in range(nkt):
                pa = kt % 2
                pv = kt % 3
                h = kt % 2
                if kt + 1 < nkt:
                    v_toks[kt + 1] = emit_vld(kt + 1)
                    s_toks[kt + 1] = emit_S(kt + 1)
                tsm = s_toks[kt]
                vld = v_toks[kt]
                m = kt - (nkt - 4)
                if m >= 0:
                    tsm = E.v(lambda e, h=h, m=m: e.tensor_tensor(
                        PS[h][:, 0:256], PS[h][:, 0:256], mk_sb[:, 256 * m:256 * (m + 1)],
                        ALU.min), deps=[tsm, CL])
                tex = E.s(lambda e, pa=pa, h=h: e.activation(
                    atile[pa][:], PS[h][:, 0:256], AF.Exp, bias=negc_col[:, 0:1]),
                    deps=[tsm, t_negc] + W.get(f"at{pa}"))
                W.set(f"PS{h}", [tex])
                first = (kt == 0)
                last = (kt == nkt - 1)
                rhs_v = vgc if kt < NVC else vgb[pv]
                rbase = D * kt if kt < NVC else 0
                mm_toks = []
                av_mms = []
                for sl in range(2):
                    po = PO1 if sl == 0 else PO2
                    for half in range(2):
                        dep = [tex, vld]
                        if first and half == 0:
                            dep = dep + W.get(f"PO{sl}")
                        mm = E.t(lambda e, sl=sl, half=half, po=po, pa=pa,
                                 rhs_v=rhs_v, rbase=rbase,
                                 first=first, last=last: e.matmul(
                            po[:, 512 * half:512 * (half + 1)],
                            atile[pa][:, 128 * sl:128 * (sl + 1)],
                            rhs_v[:, rbase + 512 * half:rbase + 512 * (half + 1)],
                            start=first, stop=last), deps=dep)
                        mm_toks.append(mm)
                        av_mms.append(mm)
                    col = (256 if sl == 0 else 320) + kt // 2
                    dmm = E.t(lambda e, sl=sl, pa=pa, h=h, col=col: e.matmul(
                        PS[h][:, col:col + 1], atile[pa][:, 128 * sl:128 * (sl + 1)],
                        onk_sb[:], start=True, stop=True),
                        deps=(W.get(f"PSD{h}") if kt < 2 else []))
                    mm_toks.append(dmm)
                if kt >= NVC:
                    W.set(f"vgb{pv}", list(av_mms))
                W.set(f"at{pa}", mm_toks)
                if last:
                    av_done = mm_toks
            if j + 1 < NCH:
                emit_setup(j + 1)
            ne = nkt // 2
            for h2 in range(2):
                t0 = E.v(lambda e, h2=h2, ne=ne: e.reduce_sum(
                    rd2[:, h2:h2 + 1], PS[h2][:, 256:256 + ne], axis=AX.X), deps=av_done)
                t1 = E.v(lambda e, h2=h2, ne=ne: e.reduce_sum(
                    rd2[:, 2 + h2:3 + h2], PS[h2][:, 320:320 + ne], axis=AX.X))
                W.set(f"PSD{h2}", [t0, t1])
                W.add(f"PS{h2}", t0)
                W.add(f"PS{h2}", t1)
            E.v(lambda e: e.tensor_add(rd2[:, 0:1], rd2[:, 0:1], rd2[:, 1:2]),
                deps=W.get("rd"))
            E.v(lambda e: e.tensor_add(rd2[:, 2:3], rd2[:, 2:3], rd2[:, 3:4]))
            E.v(lambda e: e.reciprocal(rd[:, 0:1], rd2[:, 0:1]))
            trd = E.v(lambda e: e.reciprocal(rd[:, 1:2], rd2[:, 2:3]))
            oa_toks = []
            for sl in range(2):
                po = PO1 if sl == 0 else PO2
                pb = sl
                tout = E.v(lambda e, sl=sl, po=po, pb=pb: e.scalar_tensor_tensor(
                    oatb[pb][:], po[:], rd[:, sl:sl + 1], bo_sb[:], ALU.mult, ALU.add),
                    deps=[trd, CL] + ([oa_prev[pb]] if oa_prev[pb] else []))
                W.set(f"PO{sl}", [tout])
                oa_prev[pb] = E.op("sy", f"oa{pb}", lambda e, pb=pb, j=j, sl=sl: e.dma_start(
                    out=oat_o[256 * j + 128 * sl:256 * j + 128 * (sl + 1), :], in_=oatb[pb][:]),
                    deps=[tout])
                oa_toks.append(tout)
            W.set("rd", oa_toks)

        oa_prev = [None, None]

        emit_setup(0)
        for ic in range(2):
            vscale(ic, 0)
            vscale(ic, 1)
        pescale_full(0)
        qround(0, 0)
        qround(0, 1)
        pescale_full(1)
        vscale(2, 0)
        vscale(2, 1)

        items = [("nx", t) for t in range(NT512)] + [("f", kk) for kk in range(NIC)]
        ndone = 0
        for j in range(NCH):
            target = min(len(items), int(round(len(items) * j / 14.0)))
            while ndone < target:
                emit_item(items[ndone])
                ndone += 1
            emit_chunk(j)
        while ndone < len(items):
            emit_item(items[ndone])
            ndone += 1

        thf = E.t(lambda e: e.transpose(PB1[0:1, 0:128], hmax_col[:], id_sb[:]),
                  deps=[E.tok("v"), CL] + W.get("PB1"))
        E.v(lambda e: e.tensor_copy(hrow[:], PB1[0:1, 0:128]), deps=[thf])
        W.set("PB1", [E.tok("v")])
        thx = E.v(lambda e: e.reduce_max(hx_sb[:], hrow[:], axis=AX.X))
        E.op("sy", "go", lambda e: e.dma_start(out=hx_o, in_=hx_sb[:]), deps=[thx])

        blk = ctx.enter_context(nc.Block())
        waits = [E.tok("hs0"), E.tok("hs1"), E.tok("oa0"), E.tok("oa1"), E.tok("go")]
        finish(E, blk, extra_waits=waits)
    return nc



def build_l3():
    nc = bass.Bass("TRN2", target_bir_lowering=False, dynamic_dma_scratch_size=2048)
    din = lambda n, s, d=F32: nc.dram_tensor(n, s, d, kind="ExternalInput").ap()
    dout = lambda n, s, d=F32: nc.dram_tensor(n, s, d, kind="ExternalOutput").ap()

    h_sp = din("h_sp", [128, NIC * TOK])
    hx8 = din("hx8", [1, 8])
    x_own = din("x_own", [TOK, D])
    oat = din("oat", [TOK, D])
    wd = din("wd", [D, IF])
    clip_d = din("clip_d", [1, 1])
    ident = din("ident", [128, 128])
    identb = din("identb", [128, 128], BF16)
    ones_r = din("ones_r", [1, 128])

    y_o = dout("y_out", [TOK, D])

    from contextlib import ExitStack
    with ExitStack() as ctx:
        sb = lambda n, s, d=F32: ctx.enter_context(nc.sbuf_tensor(n, s, d))
        psa = lambda n, s: ctx.enter_context(nc.psum_tensor(n, s, F32))
        sem = lambda n: ctx.enter_context(nc.semaphore(n))

        hx_sb = sb("hx_sb", [1, 8])
        cd_sb = sb("cd_sb", [1, 1])
        id_sb = sb("id_sb", [128, 128])
        idb_sb = sb("idb_sb", [128, 128], BF16)
        on_sb = sb("on_sb", [1, 128])
        srow = sb("srow", [1, 8])
        invh_col = sb("invh_col", [128, 1])
        sdrow = sb("sdrow", [1, D])
        sd_cols = sb("sd_cols", [128, NDC])
        sd_rows = sb("sd_rows", [NDC, 128])
        sdown_b = sb("sdown_b", [128, D])
        wdb = [sb("wdb0", [128, IF])]
        wdb = [wdb[0], wdb[0]]
        swc = sb("swc", [128, 1])
        ndT2 = sb("ndT2", [128, NIC * D], BF16)
        hsball = sb("hsball", [128, NIC * 512])
        nh = [sb(f"nh{p}", [128, NIC * 512], BF16) for p in range(2)]
        rb = sb("rb", [128, IF])
        kb = sb("kb", [128, IF], I32)
        fb = sb("fb", [128, IF if ROUND_SAFE else 1])
        cb = sb("cb", [128, IF if ROUND_SAFE else 1])
        xb = [sb(f"xb{p}", [128, D]) for p in range(2)]
        ob = [sb(f"ob{p}", [128, D]) for p in range(2)]
        yb = [sb(f"yb{p}", [128, D]) for p in range(2)]

        PY = [psa("PY0", [128, 1024]), psa("PY1", [128, 1024])]
        PT = psa("PT", [128, 512])

        E = Env(nc, ("sy", "v", "s", "t", "g"),
                ENG_CTRS + [("cl", 16), ("wd0", 16), ("hs0", 16),
                            ("xb0", 16), ("xb1", 16), ("ob0", 16), ("ob1", 16),
                            ("yo0", 16), ("yo1", 16), ("rg", 16)],
                sem)
        W = War()

        def cload(dst, src):
            return E.op("sy", "cl", lambda e, d=dst, s_=src: e.dma_start(out=d, in_=s_))

        for dst, src in [(hx_sb[:], hx8), (cd_sb[:], clip_d), (id_sb[:], ident),
                         (idb_sb[:], identb), (on_sb[:], ones_r)]:
            cload(dst, src)
        CL = E.tok("cl")

        E.v(lambda e: e.reduce_max(srow[0:1, 0:1], hx_sb[:], axis=AX.X), deps=[CL])
        ts1 = E.s(lambda e: e.activation(srow[0:1, 1:2], cd_sb[:], AF.Sigmoid), deps=[CL])
        E.v(lambda e: e.tensor_mul(srow[0:1, 2:3], srow[0:1, 1:2], srow[0:1, 0:1]), deps=[ts1])
        E.v(lambda e: e.reciprocal(srow[0:1, 3:4], srow[0:1, 2:3]))
        tih = E.v(lambda e: e.tensor_scalar_mul(srow[0:1, 3:4], srow[0:1, 3:4], 127.0))
        tsh = E.v(lambda e: e.tensor_scalar_mul(srow[0:1, 2:3], srow[0:1, 2:3], 1.0 / 127.0))
        tm = E.t(lambda e: e.matmul(PT[:, 0:1], on_sb[:], srow[0:1, 3:4], start=True, stop=True),
                 deps=[tih, CL])
        tivh = E.v(lambda e: e.tensor_copy(invh_col[:], PT[:, 0:1]), deps=[tm])
        W.set("PT", [tivh])

        for hc in range(NDC):
            p = hc % 2
            ld = E.op("sy", "wd0", lambda e, hc=hc, p=p: e.dma_start(
                out=wdb[p][:], in_=wd[128 * hc:128 * (hc + 1), :]), deps=W.get("wdb0"))
            E.v(lambda e, p=p: e.reduce_max(swc[:], wdb[p][:], axis=AX.X,
                                            apply_absolute_value=True), deps=[ld])
            E.v(lambda e, hc=hc: e.tensor_scalar_mul(sd_cols[:, hc:hc + 1], swc[:], 1.0 / 127.0))
            E.v(lambda e: e.reciprocal(swc[:], swc[:]))
            tiv = E.v(lambda e: e.tensor_scalar_mul(swc[:], swc[:], 127.0))
            tnq = emit_round_q(E, W, wdb[p][:], swc[:], rb[:],
                               rb[:], kb[:], fb[:], cb[:], [tiv], "rq")
            W.add("wdb0", tnq)
            for ic in range(NIC):
                ttr = E.t(lambda e, ic=ic: e.transpose(PT[:, 128:256], rb[:, 128 * ic:128 * (ic + 1)],
                                                       id_sb[:]),
                          deps=[tnq, CL] + W.get("PT"))
                tev = E.s(lambda e, ic=ic, hc=hc: e.copy(
                    ndT2[:, D * ic + 128 * hc:D * ic + 128 * (hc + 1)], PT[:, 128:256]),
                    deps=[ttr] + W.get("ndT2"))
                W.set("PT", [tev])
            W.add("rq", E.tok("s"))
            W.add("rq", E.tok("t"))

        tpk = E.t(lambda e: e.transpose(PT[0:NDC, 128:256], sd_cols[:], id_sb[:]),
                  deps=[E.tok("v"), CL] + W.get("PT"))
        tcp = E.v(lambda e: e.tensor_copy(sd_rows[:], PT[0:NDC, 128:256]), deps=[tpk])
        W.set("PT", [tcp])
        trg = E.op("g", "rg", lambda e: e.dma_start(out=sdrow[:], in_=sd_rows[:]), deps=[tcp])
        tsc = E.v(lambda e: e.tensor_scalar_mul(sdrow[:], sdrow[:], srow[0:1, 2:3]),
                  deps=[trg, tsh])
        bt = []
        for half in range(2):
            tbm = E.t(lambda e, half=half: e.matmul(
                PT[:], on_sb[:], sdrow[0:1, 512 * half:512 * (half + 1)],
                start=True, stop=True), deps=[tsc, CL] + W.get("PT"))
            tbe = E.v(lambda e, half=half: e.tensor_copy(
                sdown_b[:, 512 * half:512 * (half + 1)], PT[:]), deps=[tbm])
            W.set("PT", [tbe])
            bt.append(tbe)

        HW = NIC * 512

        def emit_hload(tc, hh):
            return E.op("sy", "hs0", lambda e, tc=tc, hh=hh: e.dma_start(
                out=hsball[:, 5632 * hh:5632 * (hh + 1)],
                in_=h_sp[:, HW * tc + 5632 * hh:HW * tc + 5632 * (hh + 1)]),
                deps=W.get(f"hsball{hh}"))

        def emit_rounds(tc, pn, lds):
            toks = []
            for g4 in range(4):
                hh = g4 // 2
                sl4 = slice(IF * g4, IF * (g4 + 1))
                tq = emit_round_q(E, W, hsball[:, sl4], invh_col[:, 0:1],
                                  nh[pn][:, sl4],
                                  rb[:], kb[:], fb[:], cb[:],
                                  [lds[hh], tivh] + W.get(f"nhw{pn}"), "rq")
                toks.append(tq)
            W.set("hsball0", [toks[1]])
            W.set("hsball1", [toks[3]])
            return toks

        yo_prev = [None, None]
        lds = [emit_hload(0, 0), emit_hload(0, 1)]
        nh_toks = {0: emit_rounds(0, 0, lds)}
        for tc in range(NT512):
            pn = tc % 2
            if tc + 1 < NT512:
                lds = [emit_hload(tc + 1, 0), emit_hload(tc + 1, 1)]
                nh_toks[tc + 1] = emit_rounds(tc + 1, (tc + 1) % 2, lds)
            for tt in range(4):
                pp = tt % 2
                mm = None
                for ic in range(NIC):
                    for half in range(2):
                        dep = []
                        if ic == 0 and half == 0:
                            dep = nh_toks[tc] + W.get(f"PY{pp}")
                        mm = E.t(lambda e, ic=ic, tt=tt, half=half, pp=pp, pn=pn: e.matmul(
                            PY[pp][:, 512 * half:512 * (half + 1)],
                            nh[pn][:, 512 * ic + 128 * tt:512 * ic + 128 * (tt + 1)],
                            ndT2[:, D * ic + 512 * half:D * ic + 512 * (half + 1)],
                            start=(ic == 0), stop=(ic == NIC - 1)),
                            deps=dep)
                W.set(f"nh_mm{pp}", [mm])
                ti = tc * 4 + tt
                xld = E.op("sy", f"xb{pp}", lambda e, ti=ti, pp=pp: e.dma_start(
                    out=xb[pp][:], in_=x_own[128 * ti:128 * (ti + 1), :]), deps=W.get(f"xb{pp}"))
                old = E.op("sy", f"ob{pp}", lambda e, ti=ti, pp=pp: e.dma_start(
                    out=ob[pp][:], in_=oat[128 * ti:128 * (ti + 1), :]), deps=W.get(f"ob{pp}"))
                tyy = E.v(lambda e, pp=pp: e.tensor_mul(yb[pp][:], PY[pp][:], sdown_b[:]),
                          deps=[mm] + bt + ([yo_prev[pp]] if yo_prev[pp] else []))
                W.set(f"PY{pp}", [tyy])
                E.v(lambda e, pp=pp: e.tensor_add(yb[pp][:], yb[pp][:], xb[pp][:]),
                    deps=[xld])
                tfin = E.v(lambda e, pp=pp: e.tensor_add(yb[pp][:], yb[pp][:], ob[pp][:]),
                           deps=[old])
                W.set(f"xb{pp}", [tfin])
                W.set(f"ob{pp}", [tfin])
                yo_prev[pp] = E.op("sy", f"yo{pp}", lambda e, ti=ti, pp=pp: e.dma_start(
                    out=y_o[128 * ti:128 * (ti + 1), :], in_=yb[pp][:]), deps=[tfin])
            W.set(f"nhw{pn}", W.get(f"nh_mm0") + W.get(f"nh_mm1"))

        blk = ctx.enter_context(nc.Block())
        waits = [E.tok("yo0"), E.tok("yo1")]
        finish(E, blk, extra_waits=waits)
    return nc



def _f32(a):
    return np.ascontiguousarray(a, dtype=np.float32)


def own_token_index(role):
    idx = []
    for j in range(NCH):
        g = (2 * j + role) * QW
        idx.extend(range(g, g + QW))
    return np.asarray(idx)


def blocked_T(a):
    d, n = a.shape
    return _f32(np.concatenate([a[128 * c:128 * (c + 1), :] for c in range(d // 128)], axis=1))


def mask_tiles(role):
    out = np.empty((128, 4 * QW), dtype=np.float32)
    for m in range(4):
        delta = 128 * m - role * QW
        k = np.arange(128)[:, None]
        qq = np.arange(QW)[None, :]
        out[:, m * QW:(m + 1) * QW] = np.where(qq < (k + delta), BIGNEG, BIGPOS)
    return out


def stage_l1_inputs(inputs):
    x = np.asarray(inputs["L0_attn_x"])
    wv = np.asarray(inputs["L0_attn_W_v"])
    wout = np.asarray(inputs["L0_attn_W_out"])
    wsh = np.asarray(inputs["L0_attn_W_shell"])
    wch = np.asarray(inputs["L0_attn_W_charge"])
    om = np.asarray(inputs["L0_attn_omega"])
    rffb = np.asarray(inputs["L0_attn_rff_bias"])
    wcomb = np.asarray(inputs["L0_attn_W_combine"])
    bsh = np.asarray(inputs["L0_attn_b_shell"])
    bch = np.asarray(inputs["L0_attn_b_charge"])
    bv = np.asarray(inputs["L0_attn_b_v"])

    shared = {
        "wv_t": blocked_T(_f32(wv).T),
        "wout_t": blocked_T(_f32(wout).T),
        "wsh_t": blocked_T(_f32(wsh).T),
        "wch_t": blocked_T(_f32(wch).T),
        "bsh_row": _f32(bsh.reshape(1, 64)),
        "bch_col": _f32(np.tile(np.asarray(bch).reshape(1, 1), (128, 1))),
        "bv_cols": _f32(bv.reshape(NDC, 128).T),
        "omega_aug": _f32(np.concatenate([_f32(om).T, rffb.reshape(1, 32)], axis=0)),
        "wc0": _f32(_f32(wcomb).T[0:128, :]),
        "wc1": _f32(_f32(wcomb).T[128:256, :]),
        "wc2": _f32(_f32(wcomb).T[256:320, :]),
        "ident": np.eye(128, dtype=np.float32),
        "ones_r": np.ones((1, 128), dtype=np.float32),
        "ones_c": np.ones((64, 1), dtype=np.float32),
    }
    maps = []
    for c in range(NCORE):
        s, r = c >> 1, c & 1
        rows = _f32(x[s][own_token_index(r)])
        m = dict(shared)
        m["x_own"] = rows
        m["xT_own"] = blocked_T(_f32(rows.T))
        maps.append(m)
    return maps


def stage_l2_inputs(inputs, r1):
    import ml_dtypes
    bf = ml_dtypes.bfloat16
    wg = _f32(np.asarray(inputs["L0_ffn_w_gate"]))
    wu = _f32(np.asarray(inputs["L0_ffn_w_up"]))
    clip_x = _f32(np.asarray(inputs["L0_ffn_clip_x"]).reshape(1, 1))
    log_temp = _f32(np.asarray(inputs["L0_attn_log_temp"]).reshape(1, 1))
    bout = np.asarray(inputs["L0_attn_b_out"])

    gx8 = np.concatenate([r1[c]["mx"][0:1, 0:1] for c in range(NCORE)], axis=1)
    nn8 = np.concatenate([r1[c]["mx"][0:1, 1:2] for c in range(NCORE)], axis=1)

    def icmajor_T(w):
        wt = _f32(w).T
        out = np.empty((128, NIC * D), dtype=np.float32)
        for ic in range(NIC):
            for dc in range(NDC):
                out[:, D * ic + 128 * dc:D * ic + 128 * (dc + 1)] = \
                    wt[128 * dc:128 * (dc + 1), 128 * ic:128 * (ic + 1)]
        return out

    shared = {
        "gx8": _f32(gx8), "nn8": _f32(nn8), "wg": wg, "wu": wu,
        "wgt": icmajor_T(wg), "wut": icmajor_T(wu),
        "clip_x": clip_x, "log_temp": log_temp,
        "iota512": np.arange(512, dtype=np.float32).reshape(1, 512),
        "b_out_bc": _f32(np.tile(bout.reshape(1, D), (128, 1))),
        "ident": np.eye(128, dtype=np.float32),
        "identb": np.eye(128, dtype=bf),
        "ones_r": np.ones((1, 128), dtype=np.float32),
        "ones_kc": np.ones((128, 1), dtype=bf),
    }
    maps = []
    for c in range(NCORE):
        s, r = c >> 1, c & 1
        va = r1[2 * s]["v_own"]
        vb = r1[2 * s + 1]["v_own"]
        vg = np.empty((SEQ, D), dtype=va.dtype)
        ia = own_token_index(0)
        ib = own_token_index(1)
        vg[ia] = va
        vg[ib] = vb
        pa = r1[2 * s]["phiT"]
        pb = r1[2 * s + 1]["phiT"]
        pg = np.empty((64, SEQ), dtype=pa.dtype)
        pg[:, ia] = pa
        pg[:, ib] = pb
        m = dict(shared)
        m["vg"] = np.ascontiguousarray(vg)
        m["phi_g"] = np.ascontiguousarray(pg)
        m["phi_q"] = np.ascontiguousarray(r1[c]["phiT"])
        m["xnT_sp"] = r1[c]["xnT_sp"]
        m["qb16"] = np.asarray([(2 * j + r) * QW for j in range(NCH)],
                               dtype=np.float32).reshape(1, NCH)
        m["maskS"] = mask_tiles(r).astype(ml_dtypes.bfloat16)
        maps.append(m)
    return maps


def stage_l3_inputs(inputs, maps1, r2):
    wd = _f32(np.asarray(inputs["L0_ffn_w_down"]))
    clip_d = _f32(np.asarray(inputs["L0_ffn_clip_down"]).reshape(1, 1))
    hx8 = np.concatenate([r2[c]["hx"][0:1, 0:1] for c in range(NCORE)], axis=1)
    import ml_dtypes
    shared = {
        "wd": wd, "clip_d": clip_d, "hx8": _f32(hx8),
        "ident": np.eye(128, dtype=np.float32),
        "identb": np.eye(128, dtype=ml_dtypes.bfloat16),
        "ones_r": np.ones((1, 128), dtype=np.float32),
    }
    maps = []
    for c in range(NCORE):
        m = dict(shared)
        m["h_sp"] = r2[c]["h_sp"]
        m["x_own"] = maps1[c]["x_own"]
        m["oat"] = r2[c]["oat"]
        maps.append(m)
    return maps


def kernel(**inputs):
    core_ids = list(range(NCORE))
    nc1 = build_l1()
    maps1 = stage_l1_inputs(inputs)
    r1 = run_bass_kernel_spmd(nc1, maps1, core_ids=core_ids).results
    nc2 = build_l2()
    maps2 = stage_l2_inputs(inputs, r1)
    r2 = run_bass_kernel_spmd(nc2, maps2, core_ids=core_ids).results
    nc3 = build_l3()
    maps3 = stage_l3_inputs(inputs, maps1, r2)
    r3 = run_bass_kernel_spmd(nc3, maps3, core_ids=core_ids).results
    x = np.asarray(inputs["L0_attn_x"])
    out = np.empty((NB, SEQ, D), dtype=np.float32)
    for c in range(NCORE):
        s, r = c >> 1, c & 1
        out[s][own_token_index(r)] = r3[c]["y_out"]
    return out
